# version 01
# baseline (speedup 1.0000x reference)
import numpy as np
from contextlib import ExitStack
import concourse.bass as bass
import concourse.mybir as mybir
from concourse.bass_utils import run_bass_kernel_spmd

F32 = mybir.dt.float32
BF16 = mybir.dt.bfloat16
AF = mybir.ActivationFunctionType
ALU = mybir.AluOpType
AX = mybir.AxisListType

ENGINES = ("sync", "vector", "scalar", "tensor", "gpsimd")


class Sched:

    def __init__(self):
        self.streams = {e: [] for e in ENGINES}
        self.cnt = {}
        self.sem_names = []

    def sem(self, name):
        if name not in self.cnt:
            self.cnt[name] = 0
            self.sem_names.append(name)
        return name

    def op(self, engine, fn, inc=None, dma=False):
        incs = []
        if inc is not None:
            if isinstance(inc, str):
                inc = [inc]
            assert len(inc) <= 1, "one sync update per instruction"
            for s in inc:
                self.sem(s)
                self.cnt[s] += 16 if dma else 1
                incs.append((s, 16 if dma else 1))
        self.streams[engine].append(("op", fn, incs))
        if inc:
            return self.cnt[inc[0]]
        return None

    def wait(self, engine, sem, val):
        assert sem in self.cnt, f"wait on unknown sem {sem}"
        assert val <= self.cnt[sem], f"wait {sem}>={val} but max planned {self.cnt[sem]}"
        self.streams[engine].append(("wait", sem, val))

    def edge(self, engine, name=None):
        for it in reversed(self.streams[engine]):
            if it[0] == "op":
                if it[2]:
                    s0, _ = it[2][0]
                    self.streams[engine].append(("wait", s0, self.cnt[s0]))
                    return
                if name is None:
                    name = "_edge_" + engine
                s = self.sem(name)
                self.cnt[s] += 1
                it[2].append((s, 1))
                self.streams[engine].append(("wait", s, self.cnt[s]))
                return
        raise RuntimeError("edge with no preceding op")

    def emit(self, nc, block, sem_handles):
        def run_stream(eng_obj, items):
            for it in items:
                if it[0] == "op":
                    _, fn, incs = it
                    instr = fn(eng_obj)
                    for (s, v) in incs:
                        instr.then_inc(sem_handles[s], v)
                elif it[0] == "wait":
                    eng_obj.wait_ge(sem_handles[it[1]], it[2])

        decorators = {
            "sync": block.sync, "vector": block.vector, "scalar": block.scalar,
            "tensor": block.tensor, "gpsimd": block.gpsimd,
        }
        for name in ENGINES:
            items = self.streams[name]
            if not items:
                continue
            dec = decorators[name]

            def section(eng_obj, _items=items):
                run_stream(eng_obj, _items)
            dec(section)


def emit_ln_stats(S, ctx, tile_whole, tile_chunks, sq_whole, sq_chunks, tag, bcfree=None, ready=()):
    p_sum, p_sumsq, p_bcA, p_bcB = ctx["p_sum"], ctx["p_sumsq"], ctx["p_bcA"], ctx["p_bcB"]
    ones_col, ones_row = ctx["ones_col"], ctx["ones_row"]
    R = ctx["rows"]
    n = ctx.setdefault("ln_n", 0)
    KC = len(tile_chunks)

    for (sm_, vv_) in ready:
        S.wait("scalar", sm_, vv_)
        S.wait("tensor", sm_, vv_)
    if ("lsmv", tag) in ctx:
        S.wait("scalar", "sm_" + tag, ctx[("lsmv", tag)])
    ssq = S.op("scalar", lambda e, a=sq_whole, b=tile_whole: e.activation(a, b, AF.Square),
               inc="ssq_" + tag)
    if n > 0:
        S.wait("tensor", "svrow", n)
    for c in range(KC):
        S.op("tensor", lambda e, r=tile_chunks[c], c0=(c == 0), c1=(c == KC - 1):
             e.matmul(p_sum, ones_col, r, start=c0, stop=c1))
    S.wait("tensor", "ssq_" + tag, ssq)
    for c in range(KC):
        S.op("tensor", lambda e, r=sq_chunks[c], c0=(c == 0), c1=(c == KC - 1):
             e.matmul(p_sumsq, ones_col, r, start=c0, stop=c1),
             inc=("sm_" + tag) if c == KC - 1 else None)
    smv = S.cnt["sm_" + tag]
    ctx[("lsmv", tag)] = smv

    S.wait("vector", "sm_" + tag, smv)
    if ctx.get("last_smbc_cnt", 0) > 0:
        S.wait("vector", "sm_bc", ctx["last_smbc_cnt"])
    S.op("vector", lambda e: e.tensor_scalar_mul(R["mean"], p_sum, 1.0 / 1024.0))
    S.op("vector", lambda e: e.tensor_scalar_mul(R["msq"], p_sumsq, 1.0 / 1024.0),
         inc="svrow")
    S.edge("vector")
    S.op("vector", lambda e: e.tensor_mul(R["m2"], R["mean"], R["mean"]))
    S.edge("vector")
    S.op("vector", lambda e: e.scalar_tensor_tensor(R["var"], R["m2"], -1.0, R["msq"],
                                                    ALU.mult, ALU.add))
    S.edge("vector")
    sva = S.op("vector", lambda e: e.tensor_scalar_add(R["vare"], R["var"], 1e-5),
               inc="sv_" + tag)
    S.wait("scalar", "sv_" + tag, sva)
    ssd = S.op("scalar", lambda e: e.activation(R["std"], R["vare"], AF.Sqrt),
               inc="ss_" + tag)
    S.wait("vector", "ss_" + tag, ssd)
    S.op("vector", lambda e: e.reciprocal(R["rstd"], R["std"]))
    S.edge("vector")
    svb = S.op("vector", lambda e: e.scalar_tensor_tensor(R["brow"], R["mean"], -1.0,
                                                          R["rstd"], ALU.mult, ALU.mult),
               inc="sv2_" + tag)
    S.wait("tensor", "sv2_" + tag, svb)
    if bcfree is not None:
        S.wait("tensor", bcfree[0], bcfree[1])
    S.op("tensor", lambda e: e.matmul(p_bcA, ones_row, R["rstd"], start=True, stop=True))
    smbc = S.op("tensor", lambda e: e.matmul(p_bcB, ones_row, R["brow"], start=True, stop=True),
                inc="sm_bc")
    ctx["ln_n"] = n + 1
    ctx["last_smbc_cnt"] = smbc
    return smbc


def _mkctx(S, e, nc, T=512):
    ctx = {}
    ctx["ones_col"] = e(nc.sbuf_tensor("ones_col", [128, 1], F32))[:]
    ctx["ones_row"] = e(nc.sbuf_tensor("ones_row", [1, 128], F32))[:]
    rowbuf = e(nc.sbuf_tensor("rowbuf", [1, 8 * T], F32))
    names = ["mean", "msq", "m2", "var", "vare", "std", "rstd", "brow"]
    ctx["rows"] = {nm: rowbuf[:, i * T:(i + 1) * T] for i, nm in enumerate(names)}
    ctx["p_sum"] = e(nc.psum_tensor("p_sum", [1, T], F32))[:]
    ctx["p_sumsq"] = e(nc.psum_tensor("p_sumsq", [1, T], F32))[:]
    ctx["p_bcA"] = e(nc.psum_tensor("p_bcA", [128, T], F32))[:]
    ctx["p_bcB"] = e(nc.psum_tensor("p_bcB", [128, T], F32))[:]
    S.op("gpsimd", lambda en, a=ctx["ones_col"]: en.memset(a, 1.0), inc="sg")
    S.op("gpsimd", lambda en, a=ctx["ones_row"]: en.memset(a, 1.0), inc="sg")
    return ctx


def chunks8(tile, T=512):
    return [tile[:, c * T:(c + 1) * T] for c in range(8)]


def build_A1():
    nc = bass.Bass("TRN2", target_bir_lowering=False)
    xT = nc.dram_tensor("xT", [1024, 8192], F32, kind="ExternalInput").ap()
    bmat = nc.dram_tensor("bmat", [1024, 16], F32, kind="ExternalInput").ap()
    alhsT = nc.dram_tensor("alhsT", [16, 16], F32, kind="ExternalInput").ap()
    hs_out = nc.dram_tensor("hs_out", [16, 8192], F32, kind="ExternalOutput").ap()
    T, NT, SEQ = 512, 16, 8192

    with ExitStack() as st_:
        e = st_.enter_context
        S = Sched()
        xc = e(nc.sbuf_tensor("xc", [128, 4096], F32))
        sq = e(nc.sbuf_tensor("sq", [128, 4096], F32))
        xn = e(nc.sbuf_tensor("xn", [128, 4096], F32))
        xb_buf = e(nc.sbuf_tensor("xb_buf", [16, SEQ], F32))
        hs_buf = e(nc.sbuf_tensor("hs_buf", [16, SEQ], F32))
        bm_sb = e(nc.sbuf_tensor("bm_sb", [128, 128], F32))
        al_sb = e(nc.sbuf_tensor("al_sb", [16, 16], F32))
        ctx = _mkctx(S, e, nc, T)
        ctx["sq_chunks"] = None
        p_xb = e(nc.psum_tensor("p_xb", [16, T], F32))[:]
        p_h = e(nc.psum_tensor("p_h", [16, 1], F32))[:]

        S.op("sync", lambda en: en.dma_start(
            out=bm_sb[:].rearrange("p (c n) -> p c n", n=16),
            in_=bmat.rearrange("(c p) n -> p c n", p=128)), inc="dw", dma=True)
        S.op("sync", lambda en: en.dma_start(out=al_sb[:], in_=alhsT), inc="dw", dma=True)

        S.wait("tensor", "sg", 2)
        S.wait("tensor", "dw", 32)
        for t in range(NT):
            tsl = slice(t * T, (t + 1) * T)
            if t > 0:
                S.wait("sync", "sv_xc", t)
            dl = S.op("sync", lambda en, s=tsl: en.dma_start(
                out=xc[:].rearrange("p (c s) -> p c s", s=T),
                in_=xT[:, s].rearrange("(c p) s -> p c s", p=128)), inc="dl", dma=True)
            S.wait("scalar", "dl", dl)
            S.wait("tensor", "dl", dl)
            smbc = emit_ln_stats(S, ctx, xc[:], chunks8(xc[:]), sq[:], chunks8(sq[:]), "ln1",
                                 bcfree=("sv_xn", t) if t > 0 else None)
            S.wait("vector", "sm_bc", smbc)
            xnc, xcc = chunks8(xn[:]), chunks8(xc[:])
            for c in range(8):
                S.op("vector", lambda en, a=xnc[c], b=xcc[c]:
                     en.tensor_tensor(a, b, ctx["p_bcA"], ALU.mult),
                     inc="sv_xc" if c == 7 else None)
            S.edge("vector")
            for c in range(8):
                S.op("vector", lambda en, a=xnc[c]:
                     en.tensor_tensor(a, a, ctx["p_bcB"], ALU.add),
                     inc="sv_xn" if c == 7 else None)
            S.wait("tensor", "sv_xn", t + 1)
            if t > 0:
                S.wait("tensor", "ss_xb", t)
            for c in range(8):
                S.op("tensor", lambda en, r=xnc[c], w=bm_sb[:, c * 16:(c + 1) * 16],
                     c0=(c == 0), c1=(c == 7):
                     en.matmul(p_xb, w, r, start=c0, stop=c1),
                     inc="sm3" if c == 7 else None)
            S.wait("scalar", "sm3", t + 1)
            S.op("scalar", lambda en, o=xb_buf[:, tsl]: en.copy(o, p_xb), inc="ss_xb")

        S.op("scalar", lambda en: en.activation(hs_buf[:, 0:1], xb_buf[:, 0:1], AF.Tanh),
             inc="sh")
        for s in range(1, SEQ):
            S.wait("tensor", "sh", s)
            S.op("tensor", lambda en, h=hs_buf[:, s - 1:s]:
                 en.matmul(p_h, al_sb[:], h, start=True, stop=True), inc="smh")
            S.wait("scalar", "smh", s)
            S.op("scalar", lambda en, o=hs_buf[:, s:s + 1], b=xb_buf[:, s:s + 1]:
                 en.activation(o, p_h, AF.Tanh, bias=b, scale=1.0), inc="sh")

        S.wait("gpsimd", "sh", SEQ)
        S.op("gpsimd", lambda en: en.dma_start(out=hs_out, in_=hs_buf[:]), inc="st", dma=True)
        S.wait("gpsimd", "st", 16)

        hmap = {}
        for nm in S.sem_names:
            hmap[nm] = e(nc.semaphore(nm))
        block = e(nc.Block())
        S.emit(nc, block, hmap)
    return nc


def build_A2():
    nc = bass.Bass("TRN2", target_bir_lowering=False)
    hs_in = nc.dram_tensor("hs_in", [16, 4096], F32, kind="ExternalInput").ap()
    xTown = nc.dram_tensor("xTown", [1024, 4096], F32, kind="ExternalInput").ap()
    cmat = nc.dram_tensor("cmat", [16, 1024], F32, kind="ExternalInput").ap()
    wot = nc.dram_tensor("wot", [1024, 1024], F32, kind="ExternalInput").ap()
    bo = nc.dram_tensor("bo", [128, 8], F32, kind="ExternalInput").ap()
    x1 = nc.dram_tensor("x1", [1024, 4096], F32, kind="ExternalOutput").ap()
    T, NU = 512, 8

    with ExitStack() as st_:
        e = st_.enter_context
        S = Sched()
        xc = e(nc.sbuf_tensor("xc", [128, 4096], F32))
        sq = e(nc.sbuf_tensor("sq", [128, 4096], F32))
        ao = e(nc.sbuf_tensor("ao", [128, 4096], F32))
        y_bf = e(nc.sbuf_tensor("y_bf", [128, 4096], BF16))
        hs_sb = e(nc.sbuf_tensor("hs_sb", [16, 4096], F32))
        cm_sb = e(nc.sbuf_tensor("cm_sb", [16, 1024], F32))
        wo_st = e(nc.sbuf_tensor("wo_st", [128, 1024], F32))
        wo_bf = e(nc.sbuf_tensor("wo_bf", [128, 8192], BF16))
        bo_sb = e(nc.sbuf_tensor("bo_sb", [128, 8], F32))
        ctx = _mkctx(S, e, nc, T)
        p_y = e(nc.psum_tensor("p_y", [128, T], F32))[:]
        p_o = e(nc.psum_tensor("p_o", [128, T], F32))[:]

        S.op("sync", lambda en: en.dma_start(out=hs_sb[:], in_=hs_in), inc="dw", dma=True)
        S.op("sync", lambda en: en.dma_start(out=cm_sb[:], in_=cmat), inc="dw", dma=True)
        S.op("sync", lambda en: en.dma_start(out=bo_sb[:], in_=bo), inc="dw", dma=True)
        for k in range(8):
            if k > 0:
                S.wait("sync", "sv_wo", k)
            dk = S.op("sync", lambda en, w=wot[k * 128:(k + 1) * 128, :]:
                      en.dma_start(out=wo_st[:], in_=w), inc="dwo", dma=True)
            S.wait("vector", "dwo", dk)
            S.op("vector", lambda en, o=wo_bf[:, k * 1024:(k + 1) * 1024]:
                 en.tensor_copy(o, wo_st[:]), inc="sv_wo")

        S.wait("tensor", "sg", 2)
        S.wait("tensor", "dw", 48)
        for u in range(NU):
            usl = slice(u * T, (u + 1) * T)
            if u > 0:
                S.wait("sync", "sv_out", u)
            dl = S.op("sync", lambda en, s=usl: en.dma_start(
                out=xc[:].rearrange("p (c s) -> p c s", s=T),
                in_=xTown[:, s].rearrange("(c p) s -> p c s", p=128)), inc="dl", dma=True)
            for m in range(8):
                idx = u * 8 + m
                if idx > 0:
                    S.wait("tensor", "ss_y", idx)
                S.op("tensor", lambda en, w=cm_sb[:, m * 128:(m + 1) * 128], r=hs_sb[:, usl]:
                     en.matmul(p_y, w, r, start=True, stop=True), inc="sm_y")
                S.wait("scalar", "sm_y", idx + 1)
                if u > 0 and m == 0:
                    S.wait("scalar", "sm_o", 8 * u)
                S.op("scalar", lambda en, o=y_bf[:, m * T:(m + 1) * T]:
                     en.copy(o, p_y), inc="ss_y")
            S.wait("tensor", "ss_y", u * 8 + 8)
            if u == 0:
                S.wait("tensor", "sv_wo", 8)
            for m2 in range(8):
                idx2 = u * 8 + m2
                if idx2 > 0:
                    S.wait("tensor", "ss_o", idx2)
                for kc in range(8):
                    S.op("tensor", lambda en,
                         w=wo_bf[:, kc * 1024 + m2 * 128: kc * 1024 + m2 * 128 + 128],
                         r=y_bf[:, kc * T:(kc + 1) * T], c0=(kc == 0), c1=(kc == 7):
                         en.matmul(p_o, w, r, start=c0, stop=c1),
                         inc="sm_o" if kc == 7 else None)
                S.wait("scalar", "sm_o", idx2 + 1)
                if u > 0 and m2 == 0:
                    S.wait("scalar", "sv_ao", u)
                S.op("scalar", lambda en, o=ao[:, m2 * T:(m2 + 1) * T], b=bo_sb[:, m2:m2 + 1]:
                     en.activation(o, p_o, AF.Identity, bias=b, scale=1.0), inc="ss_o")
            S.edge("scalar")
            if u > 0:
                S.wait("scalar", "st", 16 * u)
            smbc = emit_ln_stats(S, ctx, ao[:], chunks8(ao[:]), sq[:], chunks8(sq[:]), "ln2",
                                 bcfree=("svbc2", u) if u > 0 else None,
                                 ready=[("ss_o", u * 8 + 8)])
            S.wait("vector", "sm_bc", smbc)
            S.wait("vector", "dl", dl)
            sqc, aoc, xcc = chunks8(sq[:]), chunks8(ao[:]), chunks8(xc[:])
            for c in range(8):
                S.op("vector", lambda en, a=sqc[c], b=aoc[c]:
                     en.tensor_tensor(a, b, ctx["p_bcA"], ALU.mult),
                     inc="sv_ao" if c == 7 else None)
            S.edge("vector")
            for c in range(8):
                S.op("vector", lambda en, a=sqc[c]:
                     en.tensor_tensor(a, a, ctx["p_bcB"], ALU.add),
                     inc="svbc2" if c == 7 else None)
            S.edge("vector")
            for c in range(8):
                S.op("vector", lambda en, a=sqc[c], b=xcc[c]:
                     en.tensor_tensor(a, a, b, ALU.add),
                     inc="sv_out" if c == 7 else None)
            S.wait("gpsimd", "sv_out", u + 1)
            S.op("gpsimd", lambda en, s=usl: en.dma_start(
                out=x1[:, s].rearrange("(c p) s -> p c s", p=128),
                in_=sq[:].rearrange("p (c s) -> p c s", s=T)), inc="st", dma=True)
        S.wait("gpsimd", "st", 16 * NU)

        hmap = {}
        for nm in S.sem_names:
            hmap[nm] = e(nc.semaphore(nm))
        block = e(nc.Block())
        S.emit(nc, block, hmap)
    return nc


def build_B():
    nc = bass.Bass("TRN2", target_bir_lowering=False)
    x1T = nc.dram_tensor("x1T", [1024, 4096], F32, kind="ExternalInput").ap()
    pcol = nc.dram_tensor("pcol", [128, 8], F32, kind="ExternalInput").ap()
    wrT = nc.dram_tensor("wrT", [1024, 1024], F32, kind="ExternalInput").ap()
    wkT = nc.dram_tensor("wkT", [1024, 1024], F32, kind="ExternalInput").ap()
    wvT = nc.dram_tensor("wvT", [1024, 1024], F32, kind="ExternalInput").ap()
    mur = nc.dram_tensor("mur", [128, 8], F32, kind="ExternalInput").ap()
    muk = nc.dram_tensor("muk", [128, 8], F32, kind="ExternalInput").ap()
    x2T = nc.dram_tensor("x2T", [1024, 4096], F32, kind="ExternalOutput").ap()
    T, NU = 512, 8

    with ExitStack() as st_:
        e = st_.enter_context
        S = Sched()
        xc = e(nc.sbuf_tensor("xc", [128, 4096], F32))
        sq = e(nc.sbuf_tensor("sq", [128, 4096], F32))
        dbuf = e(nc.sbuf_tensor("dbuf", [128, 4096], F32))
        vout = e(nc.sbuf_tensor("vout", [128, 4096], F32))
        r_sb = e(nc.sbuf_tensor("r_sb", [128, 4096], F32))
        xn_pad = e(nc.sbuf_tensor("xn_pad", [128, 8 * 513], F32))
        xr_bf = e(nc.sbuf_tensor("xr_bf", [128, 4096], BF16))
        xk_bf = e(nc.sbuf_tensor("xk_bf", [128, 4096], BF16))
        ka_bf = e(nc.sbuf_tensor("ka_bf", [128, 4096], BF16))
        w_st = e(nc.sbuf_tensor("w_st", [128, 1024], F32))
        wr_bf = e(nc.sbuf_tensor("wr_bf", [128, 8192], BF16))
        wk_bf = e(nc.sbuf_tensor("wk_bf", [128, 8192], BF16))
        wv_bf = e(nc.sbuf_tensor("wv_bf", [128, 8192], BF16))
        mur_sb = e(nc.sbuf_tensor("mur_sb", [128, 8], F32))
        muk_sb = e(nc.sbuf_tensor("muk_sb", [128, 8], F32))
        pc_sb = e(nc.sbuf_tensor("pc_sb", [128, 8], F32))
        pcsq = e(nc.sbuf_tensor("pcsq", [128, 8], F32))
        xnprev0 = e(nc.sbuf_tensor("xnprev0", [128, 8], F32))
        prow = e(nc.sbuf_tensor("prow", [1, 32], F32))
        bc2 = e(nc.sbuf_tensor("bc2", [128, 2], F32))
        ctx = _mkctx(S, e, nc, T)
        p_m = e(nc.psum_tensor("p_m", [128, T], F32))[:]
        p_r1 = e(nc.psum_tensor("p_r1", [1, 8], F32))[:]
        p_b1 = e(nc.psum_tensor("p_b1", [128, 1], F32))[:]
        p_b2 = e(nc.psum_tensor("p_b2", [128, 1], F32))[:]

        for nm, dst in (("mur", mur_sb), ("muk", muk_sb), ("pcol", pc_sb)):
            src = {"mur": mur, "muk": muk, "pcol": pcol}[nm]
            S.op("sync", lambda en, o=dst[:], i=src: en.dma_start(out=o, in_=i),
                 inc="dw", dma=True)
        for wi, (wdram, wbf) in enumerate(((wrT, wr_bf), (wkT, wk_bf), (wvT, wv_bf))):
            for k in range(8):
                idx = wi * 8 + k
                if idx > 0:
                    S.wait("sync", "sv_wo", idx)
                dk = S.op("sync", lambda en, w=wdram[k * 128:(k + 1) * 128, :]:
                          en.dma_start(out=w_st[:], in_=w), inc="dwo", dma=True)
                S.wait("vector", "dwo", dk)
                S.op("vector", lambda en, o=wbf[:, k * 1024:(k + 1) * 1024]:
                     en.tensor_copy(o, w_st[:]), inc="sv_wo")

        R = ctx["rows"]
        S.wait("tensor", "sg", 2)
        S.wait("tensor", "dw", 48)
        S.op("tensor", lambda en: en.matmul(p_r1, ctx["ones_col"], pc_sb[:], start=True, stop=True),
             inc="smp")
        S.wait("scalar", "dw", 48)
        sqp = S.op("scalar", lambda en: en.activation(pcsq[:], pc_sb[:], AF.Square), inc="ssp")
        S.wait("vector", "smp", 1)
        S.op("vector", lambda en: en.reduce_sum(prow[0:1, 0:1], p_r1, axis=AX.X), inc="svp")
        S.wait("tensor", "ssp", sqp)
        S.wait("tensor", "svp", 1)
        S.op("tensor", lambda en: en.matmul(p_r1, ctx["ones_col"], pcsq[:], start=True, stop=True),
             inc="smp")
        S.wait("vector", "smp", 2)
        S.op("vector", lambda en: en.reduce_sum(prow[0:1, 1:2], p_r1, axis=AX.X))
        S.edge("vector")
        S.op("vector", lambda en: en.tensor_scalar_mul(prow[0:1, 2:3], prow[0:1, 0:1], 1.0 / 1024.0))
        S.edge("vector")
        S.op("vector", lambda en: en.tensor_mul(prow[0:1, 3:4], prow[0:1, 2:3], prow[0:1, 2:3]))
        S.op("vector", lambda en: en.tensor_scalar_mul(prow[0:1, 4:5], prow[0:1, 1:2], 1.0 / 1024.0))
        S.edge("vector")
        S.op("vector", lambda en: en.tensor_sub(prow[0:1, 5:6], prow[0:1, 4:5], prow[0:1, 3:4]))
        S.edge("vector")
        svp2 = S.op("vector", lambda en: en.tensor_scalar_add(prow[0:1, 6:7], prow[0:1, 5:6], 1e-5),
                    inc="svp")
        S.wait("scalar", "svp", svp2)
        ssp2 = S.op("scalar", lambda en: en.activation(prow[0:1, 7:8], prow[0:1, 6:7], AF.Sqrt),
                    inc="ssp")
        S.wait("vector", "ssp", ssp2)
        S.op("vector", lambda en: en.reciprocal(prow[0:1, 8:9], prow[0:1, 7:8]))
        S.edge("vector")
        svp3 = S.op("vector", lambda en: en.scalar_tensor_tensor(
            prow[0:1, 9:10], prow[0:1, 2:3], -1.0, prow[0:1, 8:9], ALU.mult, ALU.mult),
            inc="svp")
        S.wait("tensor", "svp", svp3)
        S.op("tensor", lambda en: en.matmul(p_b1, ctx["ones_row"], prow[0:1, 8:9], start=True, stop=True))
        S.op("tensor", lambda en: en.matmul(p_b2, ctx["ones_row"], prow[0:1, 9:10], start=True, stop=True),
             inc="smp")
        S.wait("scalar", "smp", 3)
        S.op("scalar", lambda en: en.copy(bc2[:, 0:1], p_b1))
        S.op("scalar", lambda en: en.copy(bc2[:, 1:2], p_b2), inc="ssp")
        S.edge("scalar")
        S.wait("vector", "ssp", S.cnt["ssp"])
        svp4 = S.op("vector", lambda en: en.tensor_scalar(
            xnprev0[:], pc_sb[:], bc2[:, 0:1], bc2[:, 1:2], ALU.mult, ALU.add), inc="svp")

        xpc3 = xn_pad[:].rearrange("p (c s) -> p c s", s=513)
        xn_ch = [xn_pad[:, c * 513 + 1: c * 513 + 513] for c in range(8)]
        xsh_ch = [xn_pad[:, c * 513 + 0: c * 513 + 512] for c in range(8)]
        for u in range(NU):
            usl = slice(u * T, (u + 1) * T)
            if u > 0:
                S.wait("sync", "sv_out", u)
            dl = S.op("sync", lambda en, s=usl: en.dma_start(
                out=xc[:].rearrange("p (c s) -> p c s", s=T),
                in_=x1T[:, s].rearrange("(c p) s -> p c s", p=128)), inc="dl", dma=True)
            S.wait("scalar", "dl", dl)
            S.wait("tensor", "dl", dl)
            if u > 0:
                S.wait("scalar", "st", 16 * u)
            smbc = emit_ln_stats(S, ctx, xc[:], chunks8(xc[:]), sq[:], chunks8(sq[:]), "lnb",
                                 bcfree=("sv_xn", 2 * u) if u > 0 else None)
            if u == 0:
                S.wait("vector", "svp", svp4)
                S.op("vector", lambda en: en.tensor_copy(
                    xpc3[:, :, 0:1], xnprev0[:].rearrange("p (c o) -> p c o", o=1)))
            else:
                S.op("vector", lambda en: en.tensor_copy(xpc3[:, :, 0:1], xpc3[:, :, 512:513]))
            S.edge("vector")
            S.wait("vector", "sm_bc", smbc)
            xcc = chunks8(xc[:])
            for c in range(8):
                S.op("vector", lambda en, a=xn_ch[c], b=xcc[c]:
                     en.tensor_tensor(a, b, ctx["p_bcA"], ALU.mult))
            S.edge("vector")
            for c in range(8):
                S.op("vector", lambda en, a=xn_ch[c]:
                     en.tensor_tensor(a, a, ctx["p_bcB"], ALU.add),
                     inc="sv_xn" if c == 7 else None)
            S.edge("vector")
            dch = chunks8(dbuf[:])
            xrc, xkc, kac = chunks8(xr_bf[:]), chunks8(xk_bf[:]), chunks8(ka_bf[:])
            if u > 0:
                S.wait("vector", "st", 16 * u)
            for c in range(8):
                S.op("vector", lambda en, a=dch[c], n=xn_ch[c], s=xsh_ch[c]:
                     en.tensor_sub(a, n, s))
            S.edge("vector")
            for c in range(8):
                S.op("vector", lambda en, a=xrc[c], dd=dch[c], s=xsh_ch[c], m=mur_sb[:, c:c + 1]:
                     en.scalar_tensor_tensor(a, dd, m, s, ALU.mult, ALU.add))
            for c in range(8):
                S.op("vector", lambda en, a=xkc[c], dd=dch[c], s=xsh_ch[c], m=muk_sb[:, c:c + 1]:
                     en.scalar_tensor_tensor(a, dd, m, s, ALU.mult, ALU.add),
                     inc="sv_xk" if c == 7 else None)
            S.wait("tensor", "sv_xk", u + 1)
            if u == 0:
                S.wait("tensor", "sv_wo", 24)
            rch = chunks8(r_sb[:])
            if u > 0:
                S.wait("tensor", "sv_vo", 8 * u)
            for m in range(8):
                idx = u * 16 + m
                if idx > 0:
                    S.wait("tensor", "ss_e", idx)
                for kc in range(8):
                    S.op("tensor", lambda en,
                         w=wr_bf[:, kc * 1024 + m * 128: kc * 1024 + m * 128 + 128],
                         r=xrc[kc], c0=(kc == 0), c1=(kc == 7):
                         en.matmul(p_m, w, r, start=c0, stop=c1),
                         inc="sm_e" if kc == 7 else None)
                S.wait("scalar", "sm_e", idx + 1)
                if u > 0 and m == 0:
                    S.wait("scalar", "sv_vo", 8 * u)
                S.op("scalar", lambda en, o=rch[m]:
                     en.activation(o, p_m, AF.Sigmoid), inc="ss_e")
            for m in range(8):
                idx = u * 16 + 8 + m
                S.wait("tensor", "ss_e", idx)
                for kc in range(8):
                    S.op("tensor", lambda en,
                         w=wk_bf[:, kc * 1024 + m * 128: kc * 1024 + m * 128 + 128],
                         r=xkc[kc], c0=(kc == 0), c1=(kc == 7):
                         en.matmul(p_m, w, r, start=c0, stop=c1),
                         inc="sm_e" if kc == 7 else None)
                S.wait("scalar", "sm_e", idx + 1)
                S.op("scalar", lambda en, o=dch[m]: en.activation(o, p_m, AF.Relu))
                S.edge("scalar")
                S.op("scalar", lambda en, o=kac[m], i=dch[m]:
                     en.activation(o, i, AF.Square), inc="ss_e")
            vch = chunks8(vout[:])
            S.wait("tensor", "ss_e", u * 16 + 16)
            for m in range(8):
                idx = u * 8 + m
                if idx > 0:
                    S.wait("tensor", "sv_vo", idx)
                for kc in range(8):
                    S.op("tensor", lambda en,
                         w=wv_bf[:, kc * 1024 + m * 128: kc * 1024 + m * 128 + 128],
                         r=kac[kc], c0=(kc == 0), c1=(kc == 7):
                         en.matmul(p_m, w, r, start=c0, stop=c1),
                         inc="sm_v" if kc == 7 else None)
                S.wait("vector", "sm_v", idx + 1)
                S.op("vector", lambda en, o=vch[m], r=rch[m]:
                     en.tensor_tensor(o, p_m, r, ALU.mult), inc="sv_vo")
            S.wait("scalar", "sv_vo", u * 8 + 8)
            smbc2 = emit_ln_stats(S, ctx, vout[:], chunks8(vout[:]), sq[:], chunks8(sq[:]), "lnb",
                                  bcfree=("sv_xn", 2 * u + 1),
                                  ready=[("sv_vo", u * 8 + 8)])
            S.wait("vector", "sm_bc", smbc2)
            for c in range(8):
                S.op("vector", lambda en, a=dch[c], b=vch[c]:
                     en.tensor_tensor(a, b, ctx["p_bcA"], ALU.mult))
            S.edge("vector")
            for c in range(8):
                S.op("vector", lambda en, a=dch[c]:
                     en.tensor_tensor(a, a, ctx["p_bcB"], ALU.add),
                     inc="sv_xn" if c == 7 else None)
            S.edge("vector")
            for c in range(8):
                S.op("vector", lambda en, a=dch[c], b=xcc[c]:
                     en.tensor_tensor(a, a, b, ALU.add),
                     inc="sv_out" if c == 7 else None)
            S.wait("gpsimd", "sv_out", u + 1)
            S.op("gpsimd", lambda en, s=usl: en.dma_start(
                out=x2T[:, s].rearrange("(c p) s -> p c s", p=128),
                in_=dbuf[:].rearrange("p (c s) -> p c s", s=T)), inc="st", dma=True)
        S.wait("gpsimd", "st", 16 * NU)

        hmap = {}
        for nm in S.sem_names:
            hmap[nm] = e(nc.semaphore(nm))
        block = e(nc.Block())
        S.emit(nc, block, hmap)
    return nc


def build_C1():
    nc = bass.Bass("TRN2", target_bir_lowering=False)
    x2T = nc.dram_tensor("x2T", [1024, 4096], F32, kind="ExternalInput").ap()
    wqkv = nc.dram_tensor("wqkv", [1024, 3072], F32, kind="ExternalInput").ap()
    gam = nc.dram_tensor("gam", [128, 8], F32, kind="ExternalInput").ap()
    bet = nc.dram_tensor("bet", [128, 8], F32, kind="ExternalInput").ap()
    xnO = nc.dram_tensor("xnO", [1024, 4096], BF16, kind="ExternalOutput").ap()
    qO = nc.dram_tensor("qO", [1024, 4096], BF16, kind="ExternalOutput").ap()
    kvO = nc.dram_tensor("kvO", [128, 520], F32, kind="ExternalOutput").ap()
    T, NU = 512, 8

    with ExitStack() as st_:
        e = st_.enter_context
        S = Sched()
        xc = e(nc.sbuf_tensor("xc", [128, 4096], F32))
        sq = e(nc.sbuf_tensor("sq", [128, 4096], F32))
        xin = e(nc.sbuf_tensor("xin", [128, 4096], F32))
        xn_bf = e(nc.sbuf_tensor("xn_bf", [128, 4096], BF16))
        xnn_bf = e(nc.sbuf_tensor("xnn_bf", [128, 4096], BF16))
        q_bf = e(nc.sbuf_tensor("q_bf", [128, 4096], BF16))
        kvtm = e(nc.sbuf_tensor("kvtm", [128, 2048], BF16))
        tmpe = e(nc.sbuf_tensor("tmpe", [128, 1536], F32))
        w_st = e(nc.sbuf_tensor("w_st", [128, 3072], F32))
        wq_bf = e(nc.sbuf_tensor("wq_bf", [128, 24576], BF16))
        gam_sb = e(nc.sbuf_tensor("gam_sb", [128, 8], F32))
        bet_sb = e(nc.sbuf_tensor("bet_sb", [128, 8], F32))
        kvp_sb = e(nc.sbuf_tensor("kvp_sb", [128, 520], F32))
        onesbf = e(nc.sbuf_tensor("onesbf", [128, 1], BF16))
        ctx = _mkctx(S, e, nc, T)
        p_t = e(nc.psum_tensor("p_t", [128, T], F32))[:]
        p_kvA = e(nc.psum_tensor("p_kvA", [128, 260], F32))[:]
        p_kvB = e(nc.psum_tensor("p_kvB", [128, 260], F32))[:]

        S.op("gpsimd", lambda en: en.memset(onesbf[:], 1.0), inc="sg")
        S.op("sync", lambda en: en.dma_start(out=gam_sb[:], in_=gam), inc="dw", dma=True)
        S.op("sync", lambda en: en.dma_start(out=bet_sb[:], in_=bet), inc="dw", dma=True)
        for k in range(8):
            if k > 0:
                S.wait("sync", "sv_wo", k)
            dk = S.op("sync", lambda en, w=wqkv[k * 128:(k + 1) * 128, :]:
                      en.dma_start(out=w_st[:], in_=w), inc="dwo", dma=True)
            S.wait("vector", "dwo", dk)
            S.op("vector", lambda en, o=wq_bf[:, k * 3072:(k + 1) * 3072]:
                 en.tensor_copy(o, w_st[:]), inc="sv_wo")

        S.wait("tensor", "sg", 3)
        S.wait("tensor", "dw", 32)
        xinc = chunks8(xin[:])
        xcc = chunks8(xc[:])
        sqc = chunks8(sq[:])
        for u in range(NU):
            usl = slice(u * T, (u + 1) * T)
            if u > 0:
                S.wait("sync", "sv_xcf", u)
            dl = S.op("sync", lambda en, s=usl: en.dma_start(
                out=xc[:].rearrange("p (c s) -> p c s", s=T),
                in_=x2T[:, s].rearrange("(c p) s -> p c s", p=128)), inc="dl", dma=True)
            S.wait("scalar", "dl", dl)
            S.wait("tensor", "dl", dl)
            if u > 0:
                S.wait("vector", "dgq", 32 * u)
            smbc = emit_ln_stats(S, ctx, xc[:], chunks8(xc[:]), sq[:], sqc, "lnc",
                                 bcfree=("svb2", 2 * u) if u > 0 else None)
            S.wait("vector", "sm_bc", smbc)
            for c in range(8):
                S.op("vector", lambda en, a=xinc[c], b=xcc[c]:
                     en.tensor_tensor(a, b, ctx["p_bcA"], ALU.mult),
                     inc="sv_xcf" if c == 7 else None)
            S.edge("vector")
            for c in range(8):
                S.op("vector", lambda en, a=xinc[c]:
                     en.tensor_tensor(a, a, ctx["p_bcB"], ALU.add),
                     inc="svb2" if c == 7 else None)
            S.edge("vector")
            svxn = S.op("vector", lambda en: en.tensor_copy(xn_bf[:], xin[:]), inc="sv_xnb")
            S.wait("gpsimd", "sv_xnb", svxn)
            S.op("gpsimd", lambda en, s=usl: en.dma_start(
                out=xnO[:, s].rearrange("(c p) s -> p c s", p=128),
                in_=xn_bf[:].rearrange("p (c s) -> p c s", s=T)), inc="dgq", dma=True)
            S.edge("scalar")
            smbc2 = emit_ln_stats(S, ctx, xin[:], xinc, sq[:], sqc, "lnc",
                                  bcfree=("svb2", 2 * u + 1),
                                  ready=[("svb2", 2 * u + 1)])
            S.wait("vector", "sm_bc", smbc2)
            for c in range(8):
                S.op("vector", lambda en, a=sqc[c], b=xinc[c]:
                     en.tensor_tensor(a, b, ctx["p_bcA"], ALU.mult))
            S.edge("vector")
            for c in range(8):
                S.op("vector", lambda en, a=sqc[c]:
                     en.tensor_tensor(a, a, ctx["p_bcB"], ALU.add),
                     inc="svb2" if c == 7 else None)
            S.edge("vector")
            xnnc = chunks8(xnn_bf[:])
            for c in range(8):
                S.op("vector", lambda en, a=xnnc[c], b=sqc[c], g=gam_sb[:, c:c + 1],
                     bb=bet_sb[:, c:c + 1]:
                     en.tensor_scalar(a, b, g, bb, ALU.mult, ALU.add),
                     inc="sv_xnn" if c == 7 else None)
            S.wait("tensor", "sv_xnn", u + 1)
            if u == 0:
                S.wait("tensor", "sv_wo", 8)
            qc = chunks8(q_bf[:])
            t1 = tmpe[:, 0:512]
            t2 = tmpe[:, 512:1024]
            t3 = tmpe[:, 1024:1536]
            for m in range(8):
                idx = u * 20 + m
                if u + m > 0:
                    S.wait("tensor", "sv_pt", S.cnt["sv_pt"])
                for kc in range(8):
                    S.op("tensor", lambda en,
                         w=wq_bf[:, kc * 3072 + m * 128: kc * 3072 + m * 128 + 128],
                         r=xnnc[kc], c0=(kc == 0), c1=(kc == 7):
                         en.matmul(p_t, w, r, start=c0, stop=c1),
                         inc="sm_t" if kc == 7 else None)
                smt = S.cnt["sm_t"]
                S.wait("vector", "sm_t", smt)
                S.op("vector", lambda en: en.tensor_scalar_min(t1, p_t, 0.0))
                S.op("vector", lambda en: en.tensor_scalar_max(t3, p_t, 0.0), inc="sv_e1")
                S.wait("scalar", "sv_e1", S.cnt["sv_e1"])
                S.op("scalar", lambda en: en.activation(t2, t1, AF.Exp), inc="ss_e1")
                S.wait("vector", "ss_e1", S.cnt["ss_e1"])
                S.op("vector", lambda en, o=qc[m]: en.tensor_add(o, t2, t3), inc="sv_pt")
            svq = S.cnt["sv_pt"]
            S.wait("gpsimd", "sv_pt", svq)
            S.op("gpsimd", lambda en, s=usl: en.dma_start(
                out=qO[:, s].rearrange("(c p) s -> p c s", p=128),
                in_=q_bf[:].rearrange("p (c s) -> p c s", s=T)), inc="dgq", dma=True)
            for tb in range(4):
                tok0 = tb * 128
                gb = u * 4 + tb
                for nb in range(4):
                    S.wait("tensor", "sv_pt", S.cnt["sv_pt"])
                    for kc in range(8):
                        S.op("tensor", lambda en,
                             w=xnnc[kc][:, tok0:tok0 + 128],
                             r=wq_bf[:, kc * 3072 + 1024 + nb * 512: kc * 3072 + 1024 + (nb + 1) * 512],
                             c0=(kc == 0), c1=(kc == 7):
                             en.matmul(p_t, w, r, start=c0, stop=c1),
                             inc="sm_t" if kc == 7 else None)
                    smt = S.cnt["sm_t"]
                    if nb < 2:
                        S.wait("vector", "sm_t", smt)
                        if gb > 0 and nb == 0:
                            S.wait("vector", "sm_kv", gb)
                        S.op("vector", lambda en: en.tensor_scalar_min(t1, p_t, 0.0))
                        S.op("vector", lambda en: en.tensor_scalar_max(t3, p_t, 0.0), inc="sv_e1")
                        S.wait("scalar", "sv_e1", S.cnt["sv_e1"])
                        S.op("scalar", lambda en: en.activation(t2, t1, AF.Exp), inc="ss_e1")
                        S.wait("vector", "ss_e1", S.cnt["ss_e1"])
                        S.op("vector", lambda en, o=kvtm[:, nb * 512:(nb + 1) * 512]:
                             en.tensor_add(o, t2, t3), inc="sv_pt")
                    else:
                        S.wait("scalar", "sm_t", smt)
                        if gb > 0 and nb == 2:
                            S.wait("scalar", "sm_kv", gb)
                        S.op("scalar", lambda en, o=kvtm[:, nb * 512:(nb + 1) * 512]:
                             en.copy(o, p_t), inc="sv_pt")
                S.wait("tensor", "sv_pt", S.cnt["sv_pt"])
                for hd in range(16):
                    pk = p_kvA if hd < 8 else p_kvB
                    base = (hd % 2) * 64
                    j2 = (hd // 2) % 4
                    S.op("tensor", lambda en, l=kvtm[:, hd * 64:(hd + 1) * 64],
                         r=kvtm[:, 1024 + hd * 64: 1024 + (hd + 1) * 64],
                         o=pk, b=base, j=j2, c0=(gb == 0), c1=(gb == 31):
                         en.matmul(o[b:b + 64, j * 65: j * 65 + 64], l, r, start=c0, stop=c1))
                    S.op("tensor", lambda en, l=kvtm[:, hd * 64:(hd + 1) * 64],
                         o=pk, b=base, j=j2, c0=(gb == 0), c1=(gb == 31):
                         en.matmul(o[b:b + 64, j * 65 + 64: j * 65 + 65], l, onesbf[:],
                                   start=c0, stop=c1),
                         inc="sm_kv" if hd == 15 else None)
        S.wait("scalar", "sm_kv", S.cnt["sm_kv"])
        S.op("scalar", lambda en: en.copy(kvp_sb[:, 0:260], p_kvA))
        S.op("scalar", lambda en: en.copy(kvp_sb[:, 260:520], p_kvB), inc="ss_f")
        S.wait("gpsimd", "ss_f", 1)
        S.op("gpsimd", lambda en: en.dma_start(out=kvO, in_=kvp_sb[:]), inc="dgq", dma=True)
        S.wait("gpsimd", "dgq", S.cnt["dgq"])

        hmap = {}
        for nm in S.sem_names:
            hmap[nm] = e(nc.semaphore(nm))
        block = e(nc.Block())
        S.emit(nc, block, hmap)
    return nc


def build_C2():
    nc = bass.Bass("TRN2", target_bir_lowering=False)
    x2T = nc.dram_tensor("x2T", [1024, 4096], F32, kind="ExternalInput").ap()
    xnI = nc.dram_tensor("xnI", [1024, 4096], BF16, kind="ExternalInput").ap()
    qI = nc.dram_tensor("qI", [1024, 4096], BF16, kind="ExternalInput").ap()
    kva = nc.dram_tensor("kva", [128, 520], F32, kind="ExternalInput").ap()
    kvb = nc.dram_tensor("kvb", [128, 520], F32, kind="ExternalInput").ap()
    wout = nc.dram_tensor("wout", [1024, 1024], F32, kind="ExternalInput").ap()
    x3T = nc.dram_tensor("x3T", [1024, 4096], F32, kind="ExternalOutput").ap()
    attO = nc.dram_tensor("attO", [1024, 4096], BF16, kind="ExternalOutput").ap()
    innO = nc.dram_tensor("innO", [1024, 4096], F32, kind="ExternalOutput").ap()
    T, NU = 512, 8

    with ExitStack() as st_:
        e = st_.enter_context
        S = Sched()
        xc = e(nc.sbuf_tensor("xc", [128, 4096], F32))
        sq = e(nc.sbuf_tensor("sq", [128, 4096], F32))
        ao = e(nc.sbuf_tensor("ao", [128, 4096], F32))
        qt = e(nc.sbuf_tensor("qt", [128, 4096], BF16))
        xnt = e(nc.sbuf_tensor("xnt", [128, 4096], BF16))
        att_bf = e(nc.sbuf_tensor("att_bf", [128, 4096], BF16))
        kv2 = e(nc.sbuf_tensor("kv2", [128, 1040], F32))
        kvf = e(nc.sbuf_tensor("kvf", [128, 520], F32))
        kv_bf = e(nc.sbuf_tensor("kv_bf", [128, 520], BF16))
        w_st = e(nc.sbuf_tensor("w_st", [128, 1024], F32))
        wo_bf = e(nc.sbuf_tensor("wo_bf", [128, 8192], BF16))
        dnbuf = e(nc.sbuf_tensor("dnbuf", [1, 2048], F32))
        ctx = _mkctx(S, e, nc, T)
        p_at = e(nc.psum_tensor("p_at", [128, T], F32))[:]
        p_dn = e(nc.psum_tensor("p_dn", [1, T], F32))[:]
        p_bc2 = e(nc.psum_tensor("p_bc2", [128, T], F32))[:]
        p_o = e(nc.psum_tensor("p_o", [128, T], F32))[:]

        S.op("sync", lambda en: en.dma_start(out=kv2[:, 0:520], in_=kva), inc="dw", dma=True)
        S.op("sync", lambda en: en.dma_start(out=kv2[:, 520:1040], in_=kvb), inc="dw", dma=True)
        S.wait("vector", "dw", 32)
        S.op("vector", lambda en: en.tensor_add(kvf[:], kv2[:, 0:520], kv2[:, 520:1040]))
        S.edge("vector")
        S.op("vector", lambda en: en.tensor_copy(kv_bf[:], kvf[:]), inc="sv_kv")
        for k in range(8):
            if k > 0:
                S.wait("sync", "sv_wo", k)
            dk = S.op("sync", lambda en, w=wout[k * 128:(k + 1) * 128, :]:
                      en.dma_start(out=w_st[:], in_=w), inc="dwo", dma=True)
            S.wait("vector", "dwo", dk)
            S.op("vector", lambda en, o=wo_bf[:, k * 1024:(k + 1) * 1024]:
                 en.tensor_copy(o, w_st[:]), inc="sv_wo")

        S.wait("tensor", "sg", 2)
        S.wait("tensor", "sv_kv", 1)
        aoc = chunks8(ao[:])
        xcc = chunks8(xc[:])
        sqc = chunks8(sq[:])
        attc = chunks8(att_bf[:])
        xntc = chunks8(xnt[:])
        for u in range(NU):
            usl = slice(u * T, (u + 1) * T)
            if u > 0:
                S.wait("sync", "sv_out", u)
                S.wait("sync", "ss_at", 8 * u)
                S.wait("sync", "sv_wo2", 8 * u)
            dl = S.op("sync", lambda en, s=usl: en.dma_start(
                out=xc[:].rearrange("p (c s) -> p c s", s=T),
                in_=x2T[:, s].rearrange("(c p) s -> p c s", p=128)), inc="dl", dma=True)
            S.op("sync", lambda en, s=usl: en.dma_start(
                out=qt[:].rearrange("p (c s) -> p c s", s=T),
                in_=qI[:, s].rearrange("(c p) s -> p c s", p=128)), inc="dl", dma=True)
            dl = S.op("sync", lambda en, s=usl: en.dma_start(
                out=xnt[:].rearrange("p (c s) -> p c s", s=T),
                in_=xnI[:, s].rearrange("(c p) s -> p c s", p=128)), inc="dl", dma=True)
            S.wait("tensor", "dl", dl)
            for j in range(8):
                jdx = u * 8 + j
                if jdx > 0:
                    S.wait("tensor", "ss_at", jdx)
                if u > 0 and j == 0:
                    S.wait("scalar", "dgx", 32 * u)
                for par in range(2):
                    hd = 2 * j + par
                    base = (hd % 2) * 64
                    cb = (hd // 2) * 65 if hd < 8 else 260 + (hd // 2 - 4) * 65
                    S.op("tensor", lambda en, l=kv_bf[base:base + 64, cb:cb + 64],
                         r=qt[base:base + 64, j * T:(j + 1) * T], b=base:
                         en.matmul(p_at[b:b + 64, :], l, r, start=True, stop=True),
                         inc="sm_at" if par == 1 else None)
                    if S.cnt.get("ss_dn", 0) > 0:
                        S.wait("tensor", "ss_dn", S.cnt["ss_dn"])
                    S.op("tensor", lambda en, l=kv_bf[base:base + 64, cb + 64:cb + 65],
                         r=qt[base:base + 64, j * T:(j + 1) * T]:
                         en.matmul(p_dn, l, r, start=True, stop=True), inc="sm_dn")
                    S.wait("scalar", "sm_dn", S.cnt["sm_dn"])
                    if par == 0 and jdx > 0:
                        S.wait("scalar", "sm_bc2", jdx)
                    S.op("scalar", lambda en, o=dnbuf[0:1, par * T:(par + 1) * T]:
                         en.copy(o, p_dn), inc="ss_dn")
                S.wait("vector", "ss_dn", S.cnt["ss_dn"])
                S.op("vector", lambda en: en.tensor_scalar_add(
                    dnbuf[0:1, 1024:2048], dnbuf[0:1, 0:1024], 1e-6))
                S.edge("vector")
                svr = S.op("vector", lambda en: en.reciprocal(
                    dnbuf[0:1, 0:1024], dnbuf[0:1, 1024:2048]), inc="sv_dn")
                S.wait("tensor", "sv_dn", svr)
                if jdx > 0:
                    S.wait("tensor", "sv_ab", jdx)
                for par in range(2):
                    S.op("tensor", lambda en, r=dnbuf[0:1, par * T:(par + 1) * T], b=par * 64:
                         en.matmul(p_bc2[b:b + 64, :], ctx["ones_row"][0:1, 0:64], r,
                                   start=True, stop=True),
                         inc="sm_bc2" if par == 1 else None)
                S.wait("scalar", "sm_at", S.cnt["sm_at"])
                S.op("scalar", lambda en, o=aoc[j]: en.copy(o, p_at), inc="ss_at")
                S.wait("vector", "ss_at", S.cnt["ss_at"])
                S.wait("vector", "sm_bc2", S.cnt["sm_bc2"])
                if u > 0 and j == 0:
                    S.wait("vector", "dgx", 32 * u)
                S.op("vector", lambda en, o=attc[j], a=aoc[j]:
                     en.tensor_tensor(o, a, p_bc2, ALU.mult), inc="sv_ab")
            S.wait("gpsimd", "sv_ab", 8 * (u + 1))
            S.op("gpsimd", lambda en, s=usl: en.dma_start(
                out=attO[:, s].rearrange("(c p) s -> p c s", p=128),
                in_=att_bf[:].rearrange("p (c s) -> p c s", s=T)), inc="dgx", dma=True)
            S.wait("tensor", "sv_ab", 8 * (u + 1))
            if u == 0:
                S.wait("tensor", "sv_wo", 8)
            for m in range(8):
                idx = u * 8 + m
                if idx > 0:
                    S.wait("tensor", "sv_wo2", idx)
                for kc in range(8):
                    S.op("tensor", lambda en,
                         w=wo_bf[:, kc * 1024 + m * 128: kc * 1024 + m * 128 + 128],
                         r=attc[kc], c0=(kc == 0), c1=(kc == 7):
                         en.matmul(p_o, w, r, start=c0, stop=c1),
                         inc="sm_o" if kc == 7 else None)
                S.wait("vector", "sm_o", S.cnt["sm_o"])
                S.op("vector", lambda en, o=aoc[m], xn_=xntc[m]:
                     en.tensor_tensor(o, p_o, xn_, ALU.add), inc="sv_wo2")
            S.wait("gpsimd", "sv_wo2", 8 * (u + 1))
            S.op("gpsimd", lambda en, s=usl: en.dma_start(
                out=innO[:, s].rearrange("(c p) s -> p c s", p=128),
                in_=ao[:].rearrange("p (c s) -> p c s", s=T)), inc="dgx", dma=True)
            S.edge("vector")
            if u > 0:
                S.wait("scalar", "st", 16 * u)
            smbc = emit_ln_stats(S, ctx, ao[:], aoc, sq[:], sqc, "lnc2",
                                 bcfree=("svb2", u) if u > 0 else None,
                                 ready=[("sv_wo2", 8 * (u + 1))])
            S.wait("vector", "sm_bc", smbc)
            S.wait("vector", "dl", dl)
            for c in range(8):
                S.op("vector", lambda en, a=sqc[c], b=aoc[c]:
                     en.tensor_tensor(a, b, ctx["p_bcA"], ALU.mult))
            S.edge("vector")
            for c in range(8):
                S.op("vector", lambda en, a=sqc[c]:
                     en.tensor_tensor(a, a, ctx["p_bcB"], ALU.add),
                     inc="svb2" if c == 7 else None)
            S.edge("vector")
            for c in range(8):
                S.op("vector", lambda en, a=sqc[c], b=xcc[c]:
                     en.tensor_tensor(a, a, b, ALU.add),
                     inc="sv_out" if c == 7 else None)
            S.wait("gpsimd", "sv_out", u + 1)
            S.op("gpsimd", lambda en, s=usl: en.dma_start(
                out=x3T[:, s].rearrange("(c p) s -> p c s", p=128),
                in_=sq[:].rearrange("p (c s) -> p c s", s=T)), inc="st", dma=True)
        S.wait("gpsimd", "st", 16 * NU)

        hmap = {}
        for nm in S.sem_names:
            hmap[nm] = e(nc.semaphore(nm))
        block = e(nc.Block())
        S.emit(nc, block, hmap)
    return nc


def build_D():
    nc = bass.Bass("TRN2", target_bir_lowering=False)
    x3T = nc.dram_tensor("x3T", [1024, 4096], F32, kind="ExternalInput").ap()
    gwT = nc.dram_tensor("gwT", [1024, 8], F32, kind="ExternalInput").ap()
    ebias = nc.dram_tensor("ebias", [8, 1], F32, kind="ExternalInput").ap()
    wupT = nc.dram_tensor("wupT", [8, 1024, 1408], F32, kind="ExternalInput").ap()
    wdnT = nc.dram_tensor("wdnT", [8, 1408, 1024], F32, kind="ExternalInput").ap()
    swuT = nc.dram_tensor("swuT", [1024, 1408], F32, kind="ExternalInput").ap()
    swdT = nc.dram_tensor("swdT", [1408, 1024], F32, kind="ExternalInput").ap()
    i8d = nc.dram_tensor("i8d", [8, 8], F32, kind="ExternalInput").ap()
    i128d = nc.dram_tensor("i128d", [128, 128], F32, kind="ExternalInput").ap()
    x4T = nc.dram_tensor("x4T", [1024, 4096], F32, kind="ExternalOutput").ap()
    T, NV, TV = 512, 4, 1024

    with ExitStack() as st_:
        e = st_.enter_context
        S = Sched()
        xc = e(nc.sbuf_tensor("xc", [128, 4096], F32))
        sq = e(nc.sbuf_tensor("sq", [128, 4096], F32))
        xnf = e(nc.sbuf_tensor("xnf", [128, 4096], F32))
        xn_bf = e(nc.sbuf_tensor("xn_bf", [128, 8192], BF16))
        z_bf = e(nc.sbuf_tensor("z_bf", [128, 4096], BF16))
        h_bf = e(nc.sbuf_tensor("h_bf", [128, 5632], BF16))
        macc = e(nc.sbuf_tensor("macc", [128, 8192], F32))
        w_st = e(nc.sbuf_tensor("w_st", [128, 1408], F32))
        wup_bf = e(nc.sbuf_tensor("wup_bf", [128, 11264], BF16))
        wdn_bf = e(nc.sbuf_tensor("wdn_bf", [128, 11264], BF16))
        gw_sb = e(nc.sbuf_tensor("gw_sb", [128, 64], F32))
        eb_sb = e(nc.sbuf_tensor("eb_sb", [8, 1], F32))
        i8_sb = e(nc.sbuf_tensor("i8_sb", [8, 8], F32))
        i128_sb = e(nc.sbuf_tensor("i128_sb", [128, 128], F32))
        scr_fm = e(nc.sbuf_tensor("scr_fm", [8, 512], F32))
        bia_fm = e(nc.sbuf_tensor("bia_fm", [8, 512], F32))
        tm_sb = e(nc.sbuf_tensor("tm_sb", [128, 64], F32))
        msk_sb = e(nc.sbuf_tensor("msk_sb", [128, 80], F32))
        gsc_row = e(nc.sbuf_tensor("gsc_row", [1, 8192], BF16))
        onesbf = e(nc.sbuf_tensor("onesbf", [1, 128], BF16))
        ctx = _mkctx(S, e, nc, T)
        p_g = e(nc.psum_tensor("p_g", [8, T], F32))[:]
        p_r = e(nc.psum_tensor("p_r", [128, 64], F32))[:]
        p_u = e(nc.psum_tensor("p_u", [128, T], F32))[:]
        p_d = e(nc.psum_tensor("p_d", [128, T], F32))[:]

        S.op("gpsimd", lambda en: en.memset(onesbf[:], 1.0), inc="sg")
        S.op("sync", lambda en: en.dma_start(
            out=gw_sb[:].rearrange("p (c n) -> p c n", n=8),
            in_=gwT.rearrange("(c p) n -> p c n", p=128)), inc="dw", dma=True)
        S.op("sync", lambda en: en.dma_start(out=eb_sb[:], in_=ebias), inc="dw", dma=True)
        S.op("sync", lambda en: en.dma_start(out=i8_sb[:], in_=i8d), inc="dw", dma=True)
        S.op("sync", lambda en: en.dma_start(out=i128_sb[:], in_=i128d), inc="dw", dma=True)

        def load_expert(g):
            if S.cnt.get("sm_u", 0) > 0:
                S.wait("vector", "sm_u", S.cnt["sm_u"])
            if S.cnt.get("sm_d", 0) > 0:
                S.wait("vector", "sm_d", S.cnt["sm_d"])
            for k in range(8):
                if S.cnt.get("sv_wc", 0) > 0:
                    S.wait("sync", "sv_wc", S.cnt["sv_wc"])
                src = swuT[k * 128:(k + 1) * 128, :] if g < 0 else wupT[g, k * 128:(k + 1) * 128, :]
                dk = S.op("sync", lambda en, w=src: en.dma_start(out=w_st[:], in_=w),
                          inc="dwo", dma=True)
                S.wait("vector", "dwo", dk)
                S.op("vector", lambda en, o=wup_bf[:, k * 1408:(k + 1) * 1408]:
                     en.tensor_copy(o, w_st[:]), inc="sv_wc")
            for k in range(11):
                S.wait("sync", "sv_wc", S.cnt["sv_wc"])
                src = swdT[k * 128:(k + 1) * 128, :] if g < 0 else wdnT[g, k * 128:(k + 1) * 128, :]
                dk = S.op("sync", lambda en, w=src: en.dma_start(out=w_st[:, 0:1024], in_=w),
                          inc="dwo", dma=True)
                S.wait("vector", "dwo", dk)
                S.op("vector", lambda en, o=wdn_bf[:, k * 1024:(k + 1) * 1024]:
                     en.tensor_copy(o, w_st[:, 0:1024]), inc="sv_wc")

        S.wait("tensor", "sg", 3)
        S.wait("tensor", "dw", 64)
        xnfc = chunks8(xnf[:])
        xcc = chunks8(xc[:])
        sqc = chunks8(sq[:])
        for v in range(NV):
            for w in range(2):
                g0 = v * TV + w * T
                usl = slice(g0, g0 + T)
                if v + w > 0:
                    S.wait("sync", "sv_xnb", 2 * v + w)
                if S.cnt.get("sv_out", 0) > 0 and w == 0:
                    S.wait("sync", "sv_out", S.cnt["sv_out"])
                dl = S.op("sync", lambda en, s=usl: en.dma_start(
                    out=xc[:].rearrange("p (c s) -> p c s", s=T),
                    in_=x3T[:, s].rearrange("(c p) s -> p c s", p=128)), inc="dl", dma=True)
                S.wait("scalar", "dl", dl)
                S.wait("tensor", "dl", dl)
                if v + w == 0:
                    bcf = None
                elif w == 0:
                    bcf = ("sv_z", S.cnt["sv_z"])
                else:
                    bcf = ("sv_xn", S.cnt["sv_xn"])
                smbc = emit_ln_stats(S, ctx, xc[:], xcc, sq[:], sqc, "lnd", bcfree=bcf)
                S.wait("vector", "sm_bc", smbc)
                if S.cnt.get("st", 0) > 0 and w == 0:
                    S.wait("vector", "st", S.cnt["st"])
                for c in range(8):
                    S.op("vector", lambda en, a=xnfc[c], b=xcc[c]:
                         en.tensor_tensor(a, b, ctx["p_bcA"], ALU.mult))
                S.edge("vector")
                for c in range(8):
                    S.op("vector", lambda en, a=xnfc[c]:
                         en.tensor_tensor(a, a, ctx["p_bcB"], ALU.add),
                         inc="sv_xn" if c == 7 else None)
                S.edge("vector")
                xnb3 = xn_bf[:].rearrange("p (c s) -> p c s", s=TV)
                svb = S.op("vector", lambda en, o=xnb3[:, :, w * T:(w + 1) * T],
                           i=xnf[:].rearrange("p (c s) -> p c s", s=T):
                           en.tensor_copy(o, i), inc="sv_xnb")
                S.wait("tensor", "sv_xn", S.cnt["sv_xn"])
                if S.cnt.get("ss_g", 0) > 0:
                    S.wait("tensor", "ss_g", S.cnt["ss_g"])
                for c in range(8):
                    S.op("tensor", lambda en, wg=gw_sb[:, c * 8:(c + 1) * 8], r=xnfc[c],
                         c0=(c == 0), c1=(c == 7):
                         en.matmul(p_g, wg, r, start=c0, stop=c1),
                         inc="sm_g" if c == 7 else None)
                S.wait("scalar", "sm_g", S.cnt["sm_g"])
                S.op("scalar", lambda en: en.activation(scr_fm[:], p_g, AF.Sigmoid), inc="ss_g")
                S.wait("vector", "ss_g", S.cnt["ss_g"])
                S.op("vector", lambda en: en.tensor_scalar(
                    bia_fm[:], scr_fm[:], eb_sb[:], None, ALU.add), inc="sv_g")
                S.wait("tensor", "sv_g", S.cnt["sv_g"])
                if S.cnt.get("sv_tm", 0) > 0:
                    S.wait("tensor", "sv_tm", S.cnt["sv_tm"])
                for bk in range(4):
                    S.op("tensor", lambda en, l=bia_fm[:, bk * 128:(bk + 1) * 128], b=bk:
                         en.matmul(p_r[:, b * 8:(b + 1) * 8], l, i8_sb[:], start=True, stop=True))
                for bk in range(4):
                    S.op("tensor", lambda en, l=scr_fm[:, bk * 128:(bk + 1) * 128], b=bk:
                         en.matmul(p_r[:, 32 + b * 8: 32 + (b + 1) * 8], l, i8_sb[:],
                                   start=True, stop=True),
                         inc="sm_tm" if bk == 3 else None)
                S.wait("vector", "sm_tm", S.cnt["sm_tm"])
                S.op("vector", lambda en: en.tensor_copy(tm_sb[:], p_r), inc="sv_tm")
                S.edge("vector")
                for bk in range(4):
                    S.op("vector", lambda en, b=bk: en.reduce_max(
                        msk_sb[:, b:b + 1], tm_sb[:, b * 8:(b + 1) * 8], axis=AX.X))
                S.edge("vector")
                for bk in range(4):
                    S.op("vector", lambda en, b=bk: en.tensor_scalar(
                        msk_sb[:, 8 + b * 8: 8 + (b + 1) * 8], tm_sb[:, b * 8:(b + 1) * 8],
                        msk_sb[:, b:b + 1], None, ALU.is_equal))
                S.edge("vector")
                for bk in range(4):
                    S.op("vector", lambda en, b=bk: en.scalar_tensor_tensor(
                        msk_sb[:, 40 + b * 8: 40 + (b + 1) * 8],
                        msk_sb[:, 8 + b * 8: 8 + (b + 1) * 8], -1e30,
                        tm_sb[:, b * 8:(b + 1) * 8], ALU.mult, ALU.add))
                S.edge("vector")
                for bk in range(4):
                    S.op("vector", lambda en, b=bk: en.reduce_max(
                        msk_sb[:, 4 + b:5 + b], msk_sb[:, 40 + b * 8: 40 + (b + 1) * 8], axis=AX.X))
                S.edge("vector")
                for bk in range(4):
                    S.op("vector", lambda en, b=bk: en.tensor_scalar(
                        msk_sb[:, 8 + b * 8: 8 + (b + 1) * 8], tm_sb[:, b * 8:(b + 1) * 8],
                        msk_sb[:, 4 + b:5 + b], None, ALU.is_ge))
                S.edge("vector")
                for bk in range(4):
                    S.op("vector", lambda en, b=bk: en.tensor_mul(
                        msk_sb[:, 8 + b * 8: 8 + (b + 1) * 8],
                        msk_sb[:, 8 + b * 8: 8 + (b + 1) * 8],
                        tm_sb[:, 32 + b * 8: 32 + (b + 1) * 8]),
                        inc="sv_mk" if bk == 3 else None)
                S.wait("tensor", "sv_mk", S.cnt["sv_mk"])
                for g in range(8):
                    S.wait("tensor", "ss_g", S.cnt["ss_g"])
                    for bk in range(4):
                        S.op("tensor", lambda en,
                             l=msk_sb[:, 8 + bk * 8 + g: 8 + bk * 8 + g + 1], b=bk:
                             en.matmul(p_g[0:1, b * 128:(b + 1) * 128], l, i128_sb[:],
                                       start=True, stop=True),
                             inc="sm_g2" if bk == 3 else None)
                    S.wait("scalar", "sm_g2", S.cnt["sm_g2"])
                    S.op("scalar", lambda en, o=gsc_row[0:1, g * 1024 + w * T: g * 1024 + (w + 1) * T]:
                         en.copy(o, p_g[0:1, 0:T]), inc="ss_g")
            gsc_ready = S.cnt["ss_g"]
            for g in range(-1, 8):
                load_expert(g)
                wready = S.cnt["sv_wc"]
                S.wait("tensor", "sv_wc", wready)
                for half in range(2):
                    if g >= 0:
                        if S.cnt.get("sv_z", 0) > 0:
                            S.wait("tensor", "sv_z", S.cnt["sv_z"])
                        S.wait("tensor", "sv_xnb", 2 * v + 2)
                        S.wait("tensor", "ss_g", gsc_ready)
                        S.op("tensor", lambda en,
                             r=gsc_row[0:1, g * 1024 + half * T: g * 1024 + (half + 1) * T]:
                             en.matmul(ctx["p_bcA"], onesbf[:], r, start=True, stop=True),
                             inc="sm_z")
                        S.wait("vector", "sm_z", S.cnt["sm_z"])
                        for c in range(8):
                            S.op("vector", lambda en, o=z_bf[:, c * T:(c + 1) * T],
                                 i=xn_bf[:, c * TV + half * T: c * TV + (half + 1) * T]:
                                 en.tensor_tensor(o, i, ctx["p_bcA"], ALU.mult),
                                 inc="sv_z" if c == 7 else None)
                        S.wait("tensor", "sv_z", S.cnt["sv_z"])
                    for j in range(11):
                        if S.cnt.get("ss_h", 0) > 0:
                            S.wait("tensor", "ss_h", S.cnt["ss_h"])
                        if S.cnt.get("sm_d", 0) > 0 and j == 0:
                            S.wait("scalar", "sm_d", S.cnt["sm_d"])
                        for kc in range(8):
                            if g >= 0:
                                rz = z_bf[:, kc * T:(kc + 1) * T]
                            else:
                                rz = xn_bf[:, kc * TV + half * T: kc * TV + (half + 1) * T]
                            S.op("tensor", lambda en,
                                 wslice=wup_bf[:, kc * 1408 + j * 128: kc * 1408 + j * 128 + 128],
                                 r=rz, c0=(kc == 0), c1=(kc == 7):
                                 en.matmul(p_u, wslice, r, start=c0, stop=c1),
                                 inc="sm_u" if kc == 7 else None)
                        S.wait("scalar", "sm_u", S.cnt["sm_u"])
                        S.op("scalar", lambda en, o=h_bf[:, j * T:(j + 1) * T]:
                             en.activation(o, p_u, AF.Relu))
                        S.edge("scalar")
                        S.op("scalar", lambda en, o=h_bf[:, j * T:(j + 1) * T]:
                             en.activation(o, o, AF.Square), inc="ss_h")
                    S.wait("tensor", "ss_h", S.cnt["ss_h"])
                    for m in range(8):
                        if S.cnt.get("sv_ac", 0) > 0:
                            S.wait("tensor", "sv_ac", S.cnt["sv_ac"])
                        for jc in range(11):
                            S.op("tensor", lambda en,
                                 wslice=wdn_bf[:, jc * 1024 + m * 128: jc * 1024 + m * 128 + 128],
                                 r=h_bf[:, jc * T:(jc + 1) * T],
                                 c0=(jc == 0), c1=(jc == 10):
                                 en.matmul(p_d, wslice, r, start=c0, stop=c1),
                                 inc="sm_d" if jc == 10 else None)
                        S.wait("vector", "sm_d", S.cnt["sm_d"])
                        mslice = macc[:, m * TV + half * T: m * TV + (half + 1) * T]
                        if g < 0:
                            S.op("vector", lambda en, o=mslice: en.tensor_copy(o, p_d),
                                 inc="sv_ac")
                        else:
                            S.op("vector", lambda en, o=mslice:
                                 en.tensor_tensor(o, o, p_d, ALU.add), inc="sv_ac")
            for w in range(2):
                g0 = v * TV + w * T
                usl = slice(g0, g0 + T)
                if S.cnt.get("st", 0) > 0:
                    S.wait("sync", "st", S.cnt["st"])
                S.wait("sync", "sv_xnb", 2 * v + 2)
                if S.cnt.get("sv_out", 0) > 0:
                    S.wait("sync", "sv_out", S.cnt["sv_out"])
                dl = S.op("sync", lambda en, s=usl: en.dma_start(
                    out=xc[:].rearrange("p (c s) -> p c s", s=T),
                    in_=x3T[:, s].rearrange("(c p) s -> p c s", p=128)), inc="dl", dma=True)
                mwhole = macc[:].rearrange("p (c s) -> p c s", s=TV)[:, :, w * T:(w + 1) * T]
                mch = [macc[:, c * TV + w * T: c * TV + w * T + T] for c in range(8)]
                if w == 0:
                    bcf2 = ("sv_z", S.cnt["sv_z"])
                else:
                    bcf2 = ("svo2", S.cnt["svo2"])
                smbc = emit_ln_stats(S, ctx, mwhole, mch, sq[:], sqc, "lnd2",
                                     bcfree=bcf2, ready=[("sv_ac", S.cnt["sv_ac"])])
                S.wait("vector", "sm_bc", smbc)
                S.wait("vector", "dl", dl)
                if w == 1:
                    S.wait("vector", "st", S.cnt["st"])
                for c in range(8):
                    S.op("vector", lambda en, a=xnfc[c], b=mch[c]:
                         en.tensor_tensor(a, b, ctx["p_bcA"], ALU.mult))
                S.edge("vector")
                for c in range(8):
                    S.op("vector", lambda en, a=xnfc[c]:
                         en.tensor_tensor(a, a, ctx["p_bcB"], ALU.add),
                         inc="svo2" if c == 7 else None)
                S.edge("vector")
                for c in range(8):
                    S.op("vector", lambda en, a=xnfc[c], b=xcc[c]:
                         en.tensor_tensor(a, a, b, ALU.add),
                         inc="sv_out" if c == 7 else None)
                S.wait("gpsimd", "sv_out", S.cnt["sv_out"])
                S.op("gpsimd", lambda en, s=usl: en.dma_start(
                    out=x4T[:, s].rearrange("(c p) s -> p c s", p=128),
                    in_=xnf[:].rearrange("p (c s) -> p c s", s=T)), inc="st", dma=True)
        S.wait("gpsimd", "st", S.cnt["st"])

        hmap = {}
        for nm in S.sem_names:
            hmap[nm] = e(nc.semaphore(nm))
        block = e(nc.Block())
        S.emit(nc, block, hmap)
    return nc


def kernel(**inp):
    x = np.ascontiguousarray(inp["L0_attn_x"], dtype=np.float32)
    cores = list(range(8))

    def cat(v):
        return np.ascontiguousarray(v)

    xT = [cat(x[b].T) for b in range(4)]

    nc = build_A1()
    maps = [{"xT": xT[c // 2],
             "bmat": cat(inp["L0_attn_B"]),
             "alhsT": cat(inp["L0_attn_A"].T)} for c in cores]
    r1 = run_bass_kernel_spmd(nc, maps, core_ids=cores).results

    nc = build_A2()
    maps = []
    for c in cores:
        b, h = c // 2, c % 2
        sl = slice(h * 4096, (h + 1) * 4096)
        maps.append({"hs_in": cat(r1[c]["hs_out"][:, sl]),
                     "xTown": cat(xT[b][:, sl]),
                     "cmat": cat(inp["L0_attn_C"]),
                     "wot": cat(inp["L0_attn_W_out"].T),
                     "bo": cat(inp["L0_attn_b_out"].reshape(8, 128).T)})
    r2 = run_bass_kernel_spmd(nc, maps, core_ids=cores).results

    nc = build_B()
    maps = []
    for c in cores:
        b, h = c // 2, c % 2
        if h == 0:
            pcol = np.zeros((128, 8), np.float32)
        else:
            pcol = cat(r2[2 * b]["x1"][:, 4095].reshape(8, 128).T)
        maps.append({"x1T": cat(r2[c]["x1"]),
                     "pcol": pcol,
                     "wrT": cat(inp["L0_ffn_W_r"].T),
                     "wkT": cat(inp["L0_ffn_W_k"].T),
                     "wvT": cat(inp["L0_ffn_W_v"].T),
                     "mur": cat(inp["L0_ffn_mu_r"].reshape(8, 128).T),
                     "muk": cat(inp["L0_ffn_mu_k"].reshape(8, 128).T)})
    r3 = run_bass_kernel_spmd(nc, maps, core_ids=cores).results

    nc = build_C1()
    maps = [{"x2T": cat(r3[c]["x2T"]),
             "wqkv": cat(inp["L1_attn_Wqkv"]),
             "gam": cat(inp["L1_attn_gamma"].reshape(8, 128).T),
             "bet": cat(inp["L1_attn_beta"].reshape(8, 128).T)} for c in cores]
    r4 = run_bass_kernel_spmd(nc, maps, core_ids=cores).results

    nc = build_C2()
    maps = []
    for c in cores:
        b, h = c // 2, c % 2
        maps.append({"x2T": cat(r3[c]["x2T"]),
                     "xnI": cat(r4[c]["xnO"]),
                     "qI": cat(r4[c]["qO"]),
                     "kva": cat(r4[c]["kvO"]),
                     "kvb": cat(r4[2 * b + (1 - h)]["kvO"]),
                     "wout": cat(inp["L1_attn_Wout"])})
    r5 = run_bass_kernel_spmd(nc, maps, core_ids=cores).results

    nc = build_D()
    common = {"gwT": cat(inp["L1_ffn_gate_w"].T),
              "ebias": cat(inp["L1_ffn_expert_bias"].reshape(8, 1)),
              "wupT": cat(inp["L1_ffn_w_up"].transpose(0, 2, 1)),
              "wdnT": cat(inp["L1_ffn_w_down"].transpose(0, 2, 1)),
              "swuT": cat(inp["L1_ffn_shared_w_up"].T),
              "swdT": cat(inp["L1_ffn_shared_w_down"].T),
              "i8d": np.eye(8, dtype=np.float32),
              "i128d": np.eye(128, dtype=np.float32)}
    maps = []
    for c in cores:
        m = dict(common)
        m["x3T"] = cat(r5[c]["x3T"])
        maps.append(m)
    r6 = run_bass_kernel_spmd(nc, maps, core_ids=cores).results

    out = np.empty((4, 8192, 1024), np.float32)
    for c in cores:
        b, h = c // 2, c % 2
        out[b, h * 4096:(h + 1) * 4096, :] = np.asarray(r6[c]["x4T"]).T
    return out


# version 02
# speedup vs baseline: 1.0022x; 1.0022x over previous
import numpy as np
from contextlib import ExitStack
import concourse.bass as bass
import concourse.mybir as mybir
from concourse.bass_utils import run_bass_kernel_spmd

F32 = mybir.dt.float32
BF16 = mybir.dt.bfloat16
AF = mybir.ActivationFunctionType
ALU = mybir.AluOpType
AX = mybir.AxisListType

ENGINES = ("sync", "vector", "scalar", "tensor", "gpsimd")


class Sched:

    def __init__(self):
        self.streams = {e: [] for e in ENGINES}
        self.cnt = {}
        self.sem_names = []

    def sem(self, name):
        if name not in self.cnt:
            self.cnt[name] = 0
            self.sem_names.append(name)
        return name

    def op(self, engine, fn, inc=None, dma=False):
        incs = []
        if inc is not None:
            if isinstance(inc, str):
                inc = [inc]
            assert len(inc) <= 1, "one sync update per instruction"
            for s in inc:
                self.sem(s)
                self.cnt[s] += 16 if dma else 1
                incs.append((s, 16 if dma else 1))
        self.streams[engine].append(("op", fn, incs))
        if inc:
            return self.cnt[inc[0]]
        return None

    def wait(self, engine, sem, val):
        assert sem in self.cnt, f"wait on unknown sem {sem}"
        assert val <= self.cnt[sem], f"wait {sem}>={val} but max planned {self.cnt[sem]}"
        self.streams[engine].append(("wait", sem, val))

    def edge(self, engine, name=None):
        for it in reversed(self.streams[engine]):
            if it[0] == "op":
                if it[2]:
                    s0, _ = it[2][0]
                    self.streams[engine].append(("wait", s0, self.cnt[s0]))
                    return
                if name is None:
                    name = "_edge_" + engine
                s = self.sem(name)
                self.cnt[s] += 1
                it[2].append((s, 1))
                self.streams[engine].append(("wait", s, self.cnt[s]))
                return
        raise RuntimeError("edge with no preceding op")

    def emit(self, nc, block, sem_handles):
        def run_stream(eng_obj, items):
            for it in items:
                if it[0] == "op":
                    _, fn, incs = it
                    instr = fn(eng_obj)
                    for (s, v) in incs:
                        instr.then_inc(sem_handles[s], v)
                elif it[0] == "wait":
                    eng_obj.wait_ge(sem_handles[it[1]], it[2])

        decorators = {
            "sync": block.sync, "vector": block.vector, "scalar": block.scalar,
            "tensor": block.tensor, "gpsimd": block.gpsimd,
        }
        for name in ENGINES:
            items = self.streams[name]
            if not items:
                continue
            dec = decorators[name]

            def section(eng_obj, _items=items):
                run_stream(eng_obj, _items)
            dec(section)


def emit_ln_stats(S, ctx, tile_whole, tile_chunks, sq_whole, sq_chunks, tag, bcfree=None, ready=()):
    p_sum, p_sumsq, p_bcA, p_bcB = ctx["p_sum"], ctx["p_sumsq"], ctx["p_bcA"], ctx["p_bcB"]
    ones_col, ones_row = ctx["ones_col"], ctx["ones_row"]
    R = ctx["rows"]
    n = ctx.setdefault("ln_n", 0)
    KC = len(tile_chunks)

    for (sm_, vv_) in ready:
        S.wait("scalar", sm_, vv_)
        S.wait("tensor", sm_, vv_)
    if ("lsmv", tag) in ctx:
        S.wait("scalar", "sm_" + tag, ctx[("lsmv", tag)])
    ssq = S.op("scalar", lambda e, a=sq_whole, b=tile_whole: e.activation(a, b, AF.Square),
               inc="ssq_" + tag)
    if n > 0:
        S.wait("tensor", "svrow", n)
    for c in range(KC):
        S.op("tensor", lambda e, r=tile_chunks[c], c0=(c == 0), c1=(c == KC - 1):
             e.matmul(p_sum, ones_col, r, start=c0, stop=c1))
    S.wait("tensor", "ssq_" + tag, ssq)
    for c in range(KC):
        S.op("tensor", lambda e, r=sq_chunks[c], c0=(c == 0), c1=(c == KC - 1):
             e.matmul(p_sumsq, ones_col, r, start=c0, stop=c1),
             inc=("sm_" + tag) if c == KC - 1 else None)
    smv = S.cnt["sm_" + tag]
    ctx[("lsmv", tag)] = smv

    S.wait("vector", "sm_" + tag, smv)
    if ctx.get("last_smbc_cnt", 0) > 0:
        S.wait("vector", "sm_bc", ctx["last_smbc_cnt"])
    S.op("vector", lambda e: e.tensor_scalar_mul(R["mean"], p_sum, 1.0 / 1024.0))
    S.op("vector", lambda e: e.tensor_scalar_mul(R["msq"], p_sumsq, 1.0 / 1024.0),
         inc="svrow")
    S.edge("vector")
    S.op("vector", lambda e: e.tensor_mul(R["m2"], R["mean"], R["mean"]))
    S.edge("vector")
    S.op("vector", lambda e: e.scalar_tensor_tensor(R["var"], R["m2"], -1.0, R["msq"],
                                                    ALU.mult, ALU.add))
    S.edge("vector")
    sva = S.op("vector", lambda e: e.tensor_scalar_add(R["vare"], R["var"], 1e-5),
               inc="sv_" + tag)
    S.wait("scalar", "sv_" + tag, sva)
    ssd = S.op("scalar", lambda e: e.activation(R["std"], R["vare"], AF.Sqrt),
               inc="ss_" + tag)
    S.wait("vector", "ss_" + tag, ssd)
    S.op("vector", lambda e: e.reciprocal(R["rstd"], R["std"]))
    S.edge("vector")
    svb = S.op("vector", lambda e: e.scalar_tensor_tensor(R["brow"], R["mean"], -1.0,
                                                          R["rstd"], ALU.mult, ALU.mult),
               inc="sv2_" + tag)
    S.wait("tensor", "sv2_" + tag, svb)
    if bcfree is not None:
        S.wait("tensor", bcfree[0], bcfree[1])
    S.op("tensor", lambda e: e.matmul(p_bcA, ones_row, R["rstd"], start=True, stop=True))
    smbc = S.op("tensor", lambda e: e.matmul(p_bcB, ones_row, R["brow"], start=True, stop=True),
                inc="sm_bc")
    ctx["ln_n"] = n + 1
    ctx["last_smbc_cnt"] = smbc
    return smbc


def _mkctx(S, e, nc, T=512):
    ctx = {}
    ctx["ones_col"] = e(nc.sbuf_tensor("ones_col", [128, 1], F32))[:]
    ctx["ones_row"] = e(nc.sbuf_tensor("ones_row", [1, 128], F32))[:]
    rowbuf = e(nc.sbuf_tensor("rowbuf", [1, 8 * T], F32))
    names = ["mean", "msq", "m2", "var", "vare", "std", "rstd", "brow"]
    ctx["rows"] = {nm: rowbuf[:, i * T:(i + 1) * T] for i, nm in enumerate(names)}
    ctx["p_sum"] = e(nc.psum_tensor("p_sum", [1, T], F32))[:]
    ctx["p_sumsq"] = e(nc.psum_tensor("p_sumsq", [1, T], F32))[:]
    ctx["p_bcA"] = e(nc.psum_tensor("p_bcA", [128, T], F32))[:]
    ctx["p_bcB"] = e(nc.psum_tensor("p_bcB", [128, T], F32))[:]
    S.op("gpsimd", lambda en, a=ctx["ones_col"]: en.memset(a, 1.0), inc="sg")
    S.op("gpsimd", lambda en, a=ctx["ones_row"]: en.memset(a, 1.0), inc="sg")
    return ctx


def chunks8(tile, T=512):
    return [tile[:, c * T:(c + 1) * T] for c in range(8)]


def build_A1():
    nc = bass.Bass("TRN2", target_bir_lowering=False)
    xT = nc.dram_tensor("xT", [1024, 8192], F32, kind="ExternalInput").ap()
    bmat = nc.dram_tensor("bmat", [1024, 16], F32, kind="ExternalInput").ap()
    alhsT = nc.dram_tensor("alhsT", [16, 16], F32, kind="ExternalInput").ap()
    hs_out = nc.dram_tensor("hs_out", [16, 8192], F32, kind="ExternalOutput").ap()
    T, NT, SEQ = 512, 16, 8192

    with ExitStack() as st_:
        e = st_.enter_context
        S = Sched()
        xc = e(nc.sbuf_tensor("xc", [128, 4096], F32))
        sq = e(nc.sbuf_tensor("sq", [128, 4096], F32))
        xn = e(nc.sbuf_tensor("xn", [128, 4096], F32))
        xb_buf = e(nc.sbuf_tensor("xb_buf", [16, SEQ], F32))
        hs_buf = e(nc.sbuf_tensor("hs_buf", [16, SEQ], F32))
        bm_sb = e(nc.sbuf_tensor("bm_sb", [128, 128], F32))
        al_sb = e(nc.sbuf_tensor("al_sb", [16, 16], F32))
        ctx = _mkctx(S, e, nc, T)
        ctx["sq_chunks"] = None
        p_xb = e(nc.psum_tensor("p_xb", [16, T], F32))[:]
        p_h = e(nc.psum_tensor("p_h", [16, 1], F32))[:]

        S.op("sync", lambda en: en.dma_start(
            out=bm_sb[:].rearrange("p (c n) -> p c n", n=16),
            in_=bmat.rearrange("(c p) n -> p c n", p=128)), inc="dw", dma=True)
        S.op("sync", lambda en: en.dma_start(out=al_sb[:], in_=alhsT), inc="dw", dma=True)

        S.wait("tensor", "sg", 2)
        S.wait("tensor", "dw", 32)
        for t in range(NT):
            tsl = slice(t * T, (t + 1) * T)
            if t > 0:
                S.wait("sync", "sv_xc", t)
            dl = S.op("sync", lambda en, s=tsl: en.dma_start(
                out=xc[:].rearrange("p (c s) -> p c s", s=T),
                in_=xT[:, s].rearrange("(c p) s -> p c s", p=128)), inc="dl", dma=True)
            S.wait("scalar", "dl", dl)
            S.wait("tensor", "dl", dl)
            smbc = emit_ln_stats(S, ctx, xc[:], chunks8(xc[:]), sq[:], chunks8(sq[:]), "ln1",
                                 bcfree=("sv_xn", t) if t > 0 else None)
            S.wait("vector", "sm_bc", smbc)
            xnc, xcc = chunks8(xn[:]), chunks8(xc[:])
            for c in range(8):
                S.op("vector", lambda en, a=xnc[c], b=xcc[c]:
                     en.tensor_tensor(a, b, ctx["p_bcA"], ALU.mult),
                     inc="sv_xc" if c == 7 else None)
            S.edge("vector")
            for c in range(8):
                S.op("vector", lambda en, a=xnc[c]:
                     en.tensor_tensor(a, a, ctx["p_bcB"], ALU.add),
                     inc="sv_xn" if c == 7 else None)
            S.wait("tensor", "sv_xn", t + 1)
            if t > 0:
                S.wait("tensor", "ss_xb", t)
            for c in range(8):
                S.op("tensor", lambda en, r=xnc[c], w=bm_sb[:, c * 16:(c + 1) * 16],
                     c0=(c == 0), c1=(c == 7):
                     en.matmul(p_xb, w, r, start=c0, stop=c1),
                     inc="sm3" if c == 7 else None)
            S.wait("scalar", "sm3", t + 1)
            S.op("scalar", lambda en, o=xb_buf[:, tsl]: en.copy(o, p_xb), inc="ss_xb")

        S.op("scalar", lambda en: en.activation(hs_buf[:, 0:1], xb_buf[:, 0:1], AF.Tanh),
             inc="sh")
        for s in range(1, SEQ):
            S.wait("tensor", "sh", s)
            S.op("tensor", lambda en, h=hs_buf[:, s - 1:s]:
                 en.matmul(p_h, al_sb[:], h, start=True, stop=True), inc="smh")
            S.wait("scalar", "smh", s)
            S.op("scalar", lambda en, o=hs_buf[:, s:s + 1], b=xb_buf[:, s:s + 1]:
                 en.activation(o, p_h, AF.Tanh, bias=b, scale=1.0), inc="sh")

        S.wait("gpsimd", "sh", SEQ)
        S.op("gpsimd", lambda en: en.dma_start(out=hs_out, in_=hs_buf[:]), inc="st", dma=True)
        S.wait("gpsimd", "st", 16)

        hmap = {}
        for nm in S.sem_names:
            hmap[nm] = e(nc.semaphore(nm))
        block = e(nc.Block())
        S.emit(nc, block, hmap)
    return nc


def build_A2():
    nc = bass.Bass("TRN2", target_bir_lowering=False)
    hs_in = nc.dram_tensor("hs_in", [16, 4096], F32, kind="ExternalInput").ap()
    xTown = nc.dram_tensor("xTown", [1024, 4096], F32, kind="ExternalInput").ap()
    cmat = nc.dram_tensor("cmat", [16, 1024], F32, kind="ExternalInput").ap()
    wot = nc.dram_tensor("wot", [1024, 1024], F32, kind="ExternalInput").ap()
    bo = nc.dram_tensor("bo", [128, 8], F32, kind="ExternalInput").ap()
    x1 = nc.dram_tensor("x1", [1024, 4096], F32, kind="ExternalOutput").ap()
    T, NU = 512, 8

    with ExitStack() as st_:
        e = st_.enter_context
        S = Sched()
        xc = e(nc.sbuf_tensor("xc", [128, 4096], F32))
        sq = e(nc.sbuf_tensor("sq", [128, 4096], F32))
        ao = e(nc.sbuf_tensor("ao", [128, 4096], F32))
        y_bf = e(nc.sbuf_tensor("y_bf", [128, 4096], BF16))
        hs_sb = e(nc.sbuf_tensor("hs_sb", [16, 4096], F32))
        cm_sb = e(nc.sbuf_tensor("cm_sb", [16, 1024], F32))
        wo_st = e(nc.sbuf_tensor("wo_st", [128, 1024], F32))
        wo_bf = e(nc.sbuf_tensor("wo_bf", [128, 8192], BF16))
        bo_sb = e(nc.sbuf_tensor("bo_sb", [128, 8], F32))
        ctx = _mkctx(S, e, nc, T)
        p_y = e(nc.psum_tensor("p_y", [128, T], F32))[:]
        p_o = e(nc.psum_tensor("p_o", [128, T], F32))[:]

        S.op("sync", lambda en: en.dma_start(out=hs_sb[:], in_=hs_in), inc="dw", dma=True)
        S.op("sync", lambda en: en.dma_start(out=cm_sb[:], in_=cmat), inc="dw", dma=True)
        S.op("sync", lambda en: en.dma_start(out=bo_sb[:], in_=bo), inc="dw", dma=True)
        for k in range(8):
            if k > 0:
                S.wait("sync", "sv_wo", k)
            dk = S.op("sync", lambda en, w=wot[k * 128:(k + 1) * 128, :]:
                      en.dma_start(out=wo_st[:], in_=w), inc="dwo", dma=True)
            S.wait("vector", "dwo", dk)
            S.op("vector", lambda en, o=wo_bf[:, k * 1024:(k + 1) * 1024]:
                 en.tensor_copy(o, wo_st[:]), inc="sv_wo")

        S.wait("tensor", "sg", 2)
        S.wait("tensor", "dw", 48)
        for u in range(NU):
            usl = slice(u * T, (u + 1) * T)
            if u > 0:
                S.wait("sync", "sv_out", u)
            dl = S.op("sync", lambda en, s=usl: en.dma_start(
                out=xc[:].rearrange("p (c s) -> p c s", s=T),
                in_=xTown[:, s].rearrange("(c p) s -> p c s", p=128)), inc="dl", dma=True)
            for m in range(8):
                idx = u * 8 + m
                if idx > 0:
                    S.wait("tensor", "ss_y", idx)
                S.op("tensor", lambda en, w=cm_sb[:, m * 128:(m + 1) * 128], r=hs_sb[:, usl]:
                     en.matmul(p_y, w, r, start=True, stop=True), inc="sm_y")
                S.wait("scalar", "sm_y", idx + 1)
                if u > 0 and m == 0:
                    S.wait("scalar", "sm_o", 8 * u)
                S.op("scalar", lambda en, o=y_bf[:, m * T:(m + 1) * T]:
                     en.copy(o, p_y), inc="ss_y")
            S.wait("tensor", "ss_y", u * 8 + 8)
            if u == 0:
                S.wait("tensor", "sv_wo", 8)
            for m2 in range(8):
                idx2 = u * 8 + m2
                if idx2 > 0:
                    S.wait("tensor", "ss_o", idx2)
                for kc in range(8):
                    S.op("tensor", lambda en,
                         w=wo_bf[:, kc * 1024 + m2 * 128: kc * 1024 + m2 * 128 + 128],
                         r=y_bf[:, kc * T:(kc + 1) * T], c0=(kc == 0), c1=(kc == 7):
                         en.matmul(p_o, w, r, start=c0, stop=c1),
                         inc="sm_o" if kc == 7 else None)
                S.wait("scalar", "sm_o", idx2 + 1)
                if u > 0 and m2 == 0:
                    S.wait("scalar", "sv_ao", u)
                S.op("scalar", lambda en, o=ao[:, m2 * T:(m2 + 1) * T], b=bo_sb[:, m2:m2 + 1]:
                     en.activation(o, p_o, AF.Identity, bias=b, scale=1.0), inc="ss_o")
            S.edge("scalar")
            if u > 0:
                S.wait("scalar", "st", 16 * u)
            smbc = emit_ln_stats(S, ctx, ao[:], chunks8(ao[:]), sq[:], chunks8(sq[:]), "ln2",
                                 bcfree=("svbc2", u) if u > 0 else None,
                                 ready=[("ss_o", u * 8 + 8)])
            S.wait("vector", "sm_bc", smbc)
            S.wait("vector", "dl", dl)
            sqc, aoc, xcc = chunks8(sq[:]), chunks8(ao[:]), chunks8(xc[:])
            for c in range(8):
                S.op("vector", lambda en, a=sqc[c], b=aoc[c]:
                     en.tensor_tensor(a, b, ctx["p_bcA"], ALU.mult),
                     inc="sv_ao" if c == 7 else None)
            S.edge("vector")
            for c in range(8):
                S.op("vector", lambda en, a=sqc[c]:
                     en.tensor_tensor(a, a, ctx["p_bcB"], ALU.add),
                     inc="svbc2" if c == 7 else None)
            S.edge("vector")
            for c in range(8):
                S.op("vector", lambda en, a=sqc[c], b=xcc[c]:
                     en.tensor_tensor(a, a, b, ALU.add),
                     inc="sv_out" if c == 7 else None)
            S.wait("gpsimd", "sv_out", u + 1)
            S.op("gpsimd", lambda en, s=usl: en.dma_start(
                out=x1[:, s].rearrange("(c p) s -> p c s", p=128),
                in_=sq[:].rearrange("p (c s) -> p c s", s=T)), inc="st", dma=True)
        S.wait("gpsimd", "st", 16 * NU)

        hmap = {}
        for nm in S.sem_names:
            hmap[nm] = e(nc.semaphore(nm))
        block = e(nc.Block())
        S.emit(nc, block, hmap)
    return nc


def build_B():
    nc = bass.Bass("TRN2", target_bir_lowering=False)
    x1T = nc.dram_tensor("x1T", [1024, 4096], F32, kind="ExternalInput").ap()
    pcol = nc.dram_tensor("pcol", [128, 8], F32, kind="ExternalInput").ap()
    wrT = nc.dram_tensor("wrT", [1024, 1024], F32, kind="ExternalInput").ap()
    wkT = nc.dram_tensor("wkT", [1024, 1024], F32, kind="ExternalInput").ap()
    wvT = nc.dram_tensor("wvT", [1024, 1024], F32, kind="ExternalInput").ap()
    mur = nc.dram_tensor("mur", [128, 8], F32, kind="ExternalInput").ap()
    muk = nc.dram_tensor("muk", [128, 8], F32, kind="ExternalInput").ap()
    x2T = nc.dram_tensor("x2T", [1024, 4096], F32, kind="ExternalOutput").ap()
    T, NU = 512, 8

    with ExitStack() as st_:
        e = st_.enter_context
        S = Sched()
        xc = e(nc.sbuf_tensor("xc", [128, 4096], F32))
        sq = e(nc.sbuf_tensor("sq", [128, 4096], F32))
        dbuf = e(nc.sbuf_tensor("dbuf", [128, 4096], F32))
        vout = e(nc.sbuf_tensor("vout", [128, 4096], F32))
        r_sb = e(nc.sbuf_tensor("r_sb", [128, 4096], F32))
        xn_pad = e(nc.sbuf_tensor("xn_pad", [128, 8 * 513], F32))
        xr_bf = e(nc.sbuf_tensor("xr_bf", [128, 4096], BF16))
        xk_bf = e(nc.sbuf_tensor("xk_bf", [128, 4096], BF16))
        ka_bf = e(nc.sbuf_tensor("ka_bf", [128, 4096], BF16))
        w_st = e(nc.sbuf_tensor("w_st", [128, 1024], F32))
        wr_bf = e(nc.sbuf_tensor("wr_bf", [128, 8192], BF16))
        wk_bf = e(nc.sbuf_tensor("wk_bf", [128, 8192], BF16))
        wv_bf = e(nc.sbuf_tensor("wv_bf", [128, 8192], BF16))
        mur_sb = e(nc.sbuf_tensor("mur_sb", [128, 8], F32))
        muk_sb = e(nc.sbuf_tensor("muk_sb", [128, 8], F32))
        pc_sb = e(nc.sbuf_tensor("pc_sb", [128, 8], F32))
        pcsq = e(nc.sbuf_tensor("pcsq", [128, 8], F32))
        xnprev0 = e(nc.sbuf_tensor("xnprev0", [128, 8], F32))
        prow = e(nc.sbuf_tensor("prow", [1, 32], F32))
        bc2 = e(nc.sbuf_tensor("bc2", [128, 2], F32))
        ctx = _mkctx(S, e, nc, T)
        p_m = e(nc.psum_tensor("p_m", [128, T], F32))[:]
        p_r1 = e(nc.psum_tensor("p_r1", [1, 8], F32))[:]
        p_b1 = e(nc.psum_tensor("p_b1", [128, 1], F32))[:]
        p_b2 = e(nc.psum_tensor("p_b2", [128, 1], F32))[:]

        for nm, dst in (("mur", mur_sb), ("muk", muk_sb), ("pcol", pc_sb)):
            src = {"mur": mur, "muk": muk, "pcol": pcol}[nm]
            S.op("sync", lambda en, o=dst[:], i=src: en.dma_start(out=o, in_=i),
                 inc="dw", dma=True)
        for wi, (wdram, wbf) in enumerate(((wrT, wr_bf), (wkT, wk_bf), (wvT, wv_bf))):
            for k in range(8):
                idx = wi * 8 + k
                if idx > 0:
                    S.wait("sync", "sv_wo", idx)
                dk = S.op("sync", lambda en, w=wdram[k * 128:(k + 1) * 128, :]:
                          en.dma_start(out=w_st[:], in_=w), inc="dwo", dma=True)
                S.wait("vector", "dwo", dk)
                S.op("vector", lambda en, o=wbf[:, k * 1024:(k + 1) * 1024]:
                     en.tensor_copy(o, w_st[:]), inc="sv_wo")

        R = ctx["rows"]
        S.wait("tensor", "sg", 2)
        S.wait("tensor", "dw", 48)
        S.op("tensor", lambda en: en.matmul(p_r1, ctx["ones_col"], pc_sb[:], start=True, stop=True),
             inc="smp")
        S.wait("scalar", "dw", 48)
        sqp = S.op("scalar", lambda en: en.activation(pcsq[:], pc_sb[:], AF.Square), inc="ssp")
        S.wait("vector", "smp", 1)
        S.op("vector", lambda en: en.reduce_sum(prow[0:1, 0:1], p_r1, axis=AX.X), inc="svp")
        S.wait("tensor", "ssp", sqp)
        S.wait("tensor", "svp", 1)
        S.op("tensor", lambda en: en.matmul(p_r1, ctx["ones_col"], pcsq[:], start=True, stop=True),
             inc="smp")
        S.wait("vector", "smp", 2)
        S.op("vector", lambda en: en.reduce_sum(prow[0:1, 1:2], p_r1, axis=AX.X))
        S.edge("vector")
        S.op("vector", lambda en: en.tensor_scalar_mul(prow[0:1, 2:3], prow[0:1, 0:1], 1.0 / 1024.0))
        S.edge("vector")
        S.op("vector", lambda en: en.tensor_mul(prow[0:1, 3:4], prow[0:1, 2:3], prow[0:1, 2:3]))
        S.op("vector", lambda en: en.tensor_scalar_mul(prow[0:1, 4:5], prow[0:1, 1:2], 1.0 / 1024.0))
        S.edge("vector")
        S.op("vector", lambda en: en.tensor_sub(prow[0:1, 5:6], prow[0:1, 4:5], prow[0:1, 3:4]))
        S.edge("vector")
        svp2 = S.op("vector", lambda en: en.tensor_scalar_add(prow[0:1, 6:7], prow[0:1, 5:6], 1e-5),
                    inc="svp")
        S.wait("scalar", "svp", svp2)
        ssp2 = S.op("scalar", lambda en: en.activation(prow[0:1, 7:8], prow[0:1, 6:7], AF.Sqrt),
                    inc="ssp")
        S.wait("vector", "ssp", ssp2)
        S.op("vector", lambda en: en.reciprocal(prow[0:1, 8:9], prow[0:1, 7:8]))
        S.edge("vector")
        svp3 = S.op("vector", lambda en: en.scalar_tensor_tensor(
            prow[0:1, 9:10], prow[0:1, 2:3], -1.0, prow[0:1, 8:9], ALU.mult, ALU.mult),
            inc="svp")
        S.wait("tensor", "svp", svp3)
        S.op("tensor", lambda en: en.matmul(p_b1, ctx["ones_row"], prow[0:1, 8:9], start=True, stop=True))
        S.op("tensor", lambda en: en.matmul(p_b2, ctx["ones_row"], prow[0:1, 9:10], start=True, stop=True),
             inc="smp")
        S.wait("scalar", "smp", 3)
        S.op("scalar", lambda en: en.copy(bc2[:, 0:1], p_b1))
        S.op("scalar", lambda en: en.copy(bc2[:, 1:2], p_b2), inc="ssp")
        S.edge("scalar")
        S.wait("vector", "ssp", S.cnt["ssp"])
        svp4 = S.op("vector", lambda en: en.tensor_scalar(
            xnprev0[:], pc_sb[:], bc2[:, 0:1], bc2[:, 1:2], ALU.mult, ALU.add), inc="svp")

        xpc3 = xn_pad[:].rearrange("p (c s) -> p c s", s=513)
        xn_ch = [xn_pad[:, c * 513 + 1: c * 513 + 513] for c in range(8)]
        xsh_ch = [xn_pad[:, c * 513 + 0: c * 513 + 512] for c in range(8)]
        for u in range(NU):
            usl = slice(u * T, (u + 1) * T)
            if u > 0:
                S.wait("sync", "sv_out", u)
            dl = S.op("sync", lambda en, s=usl: en.dma_start(
                out=xc[:].rearrange("p (c s) -> p c s", s=T),
                in_=x1T[:, s].rearrange("(c p) s -> p c s", p=128)), inc="dl", dma=True)
            S.wait("scalar", "dl", dl)
            S.wait("tensor", "dl", dl)
            if u > 0:
                S.wait("scalar", "st", 16 * u)
            smbc = emit_ln_stats(S, ctx, xc[:], chunks8(xc[:]), sq[:], chunks8(sq[:]), "lnb",
                                 bcfree=("sv_xn", 2 * u) if u > 0 else None)
            if u == 0:
                S.wait("vector", "svp", svp4)
                S.op("vector", lambda en: en.tensor_copy(
                    xpc3[:, :, 0:1], xnprev0[:].rearrange("p (c o) -> p c o", o=1)))
            else:
                S.op("vector", lambda en: en.tensor_copy(xpc3[:, :, 0:1], xpc3[:, :, 512:513]))
            S.edge("vector")
            S.wait("vector", "sm_bc", smbc)
            xcc = chunks8(xc[:])
            for c in range(8):
                S.op("vector", lambda en, a=xn_ch[c], b=xcc[c]:
                     en.tensor_tensor(a, b, ctx["p_bcA"], ALU.mult))
            S.edge("vector")
            for c in range(8):
                S.op("vector", lambda en, a=xn_ch[c]:
                     en.tensor_tensor(a, a, ctx["p_bcB"], ALU.add),
                     inc="sv_xn" if c == 7 else None)
            S.edge("vector")
            dch = chunks8(dbuf[:])
            xrc, xkc, kac = chunks8(xr_bf[:]), chunks8(xk_bf[:]), chunks8(ka_bf[:])
            if u > 0:
                S.wait("vector", "st", 16 * u)
            for c in range(8):
                S.op("vector", lambda en, a=dch[c], n=xn_ch[c], s=xsh_ch[c]:
                     en.tensor_sub(a, n, s))
            S.edge("vector")
            for c in range(8):
                S.op("vector", lambda en, a=xrc[c], dd=dch[c], s=xsh_ch[c], m=mur_sb[:, c:c + 1]:
                     en.scalar_tensor_tensor(a, dd, m, s, ALU.mult, ALU.add))
            for c in range(8):
                S.op("vector", lambda en, a=xkc[c], dd=dch[c], s=xsh_ch[c], m=muk_sb[:, c:c + 1]:
                     en.scalar_tensor_tensor(a, dd, m, s, ALU.mult, ALU.add),
                     inc="sv_xk" if c == 7 else None)
            S.wait("tensor", "sv_xk", u + 1)
            if u == 0:
                S.wait("tensor", "sv_wo", 24)
            rch = chunks8(r_sb[:])
            if u > 0:
                S.wait("tensor", "sv_vo", 8 * u)
            for m in range(8):
                idx = u * 16 + m
                if idx > 0:
                    S.wait("tensor", "ss_e", idx)
                for kc in range(8):
                    S.op("tensor", lambda en,
                         w=wr_bf[:, kc * 1024 + m * 128: kc * 1024 + m * 128 + 128],
                         r=xrc[kc], c0=(kc == 0), c1=(kc == 7):
                         en.matmul(p_m, w, r, start=c0, stop=c1),
                         inc="sm_e" if kc == 7 else None)
                S.wait("scalar", "sm_e", idx + 1)
                if u > 0 and m == 0:
                    S.wait("scalar", "sv_vo", 8 * u)
                S.op("scalar", lambda en, o=rch[m]:
                     en.activation(o, p_m, AF.Sigmoid), inc="ss_e")
            for m in range(8):
                idx = u * 16 + 8 + m
                S.wait("tensor", "ss_e", idx)
                for kc in range(8):
                    S.op("tensor", lambda en,
                         w=wk_bf[:, kc * 1024 + m * 128: kc * 1024 + m * 128 + 128],
                         r=xkc[kc], c0=(kc == 0), c1=(kc == 7):
                         en.matmul(p_m, w, r, start=c0, stop=c1),
                         inc="sm_e" if kc == 7 else None)
                S.wait("scalar", "sm_e", idx + 1)
                S.op("scalar", lambda en, o=dch[m]: en.activation(o, p_m, AF.Relu))
                S.edge("scalar")
                S.op("scalar", lambda en, o=kac[m], i=dch[m]:
                     en.activation(o, i, AF.Square), inc="ss_e")
            vch = chunks8(vout[:])
            S.wait("tensor", "ss_e", u * 16 + 16)
            for m in range(8):
                idx = u * 8 + m
                if idx > 0:
                    S.wait("tensor", "sv_vo", idx)
                for kc in range(8):
                    S.op("tensor", lambda en,
                         w=wv_bf[:, kc * 1024 + m * 128: kc * 1024 + m * 128 + 128],
                         r=kac[kc], c0=(kc == 0), c1=(kc == 7):
                         en.matmul(p_m, w, r, start=c0, stop=c1),
                         inc="sm_v" if kc == 7 else None)
                S.wait("vector", "sm_v", idx + 1)
                S.op("vector", lambda en, o=vch[m], r=rch[m]:
                     en.tensor_tensor(o, p_m, r, ALU.mult), inc="sv_vo")
            S.wait("scalar", "sv_vo", u * 8 + 8)
            smbc2 = emit_ln_stats(S, ctx, vout[:], chunks8(vout[:]), sq[:], chunks8(sq[:]), "lnb",
                                  bcfree=("sv_xn", 2 * u + 1),
                                  ready=[("sv_vo", u * 8 + 8)])
            S.wait("vector", "sm_bc", smbc2)
            for c in range(8):
                S.op("vector", lambda en, a=dch[c], b=vch[c]:
                     en.tensor_tensor(a, b, ctx["p_bcA"], ALU.mult))
            S.edge("vector")
            for c in range(8):
                S.op("vector", lambda en, a=dch[c]:
                     en.tensor_tensor(a, a, ctx["p_bcB"], ALU.add),
                     inc="sv_xn" if c == 7 else None)
            S.edge("vector")
            for c in range(8):
                S.op("vector", lambda en, a=dch[c], b=xcc[c]:
                     en.tensor_tensor(a, a, b, ALU.add),
                     inc="sv_out" if c == 7 else None)
            S.wait("gpsimd", "sv_out", u + 1)
            S.op("gpsimd", lambda en, s=usl: en.dma_start(
                out=x2T[:, s].rearrange("(c p) s -> p c s", p=128),
                in_=dbuf[:].rearrange("p (c s) -> p c s", s=T)), inc="st", dma=True)
        S.wait("gpsimd", "st", 16 * NU)

        hmap = {}
        for nm in S.sem_names:
            hmap[nm] = e(nc.semaphore(nm))
        block = e(nc.Block())
        S.emit(nc, block, hmap)
    return nc


def build_C1():
    nc = bass.Bass("TRN2", target_bir_lowering=False)
    x2T = nc.dram_tensor("x2T", [1024, 4096], F32, kind="ExternalInput").ap()
    wqkv = nc.dram_tensor("wqkv", [1024, 3072], F32, kind="ExternalInput").ap()
    gam = nc.dram_tensor("gam", [128, 8], F32, kind="ExternalInput").ap()
    bet = nc.dram_tensor("bet", [128, 8], F32, kind="ExternalInput").ap()
    xnO = nc.dram_tensor("xnO", [1024, 4096], BF16, kind="ExternalOutput").ap()
    qO = nc.dram_tensor("qO", [1024, 4096], BF16, kind="ExternalOutput").ap()
    kvO = nc.dram_tensor("kvO", [128, 520], F32, kind="ExternalOutput").ap()
    T, NU = 512, 8

    with ExitStack() as st_:
        e = st_.enter_context
        S = Sched()
        xc = e(nc.sbuf_tensor("xc", [128, 4096], F32))
        sq = e(nc.sbuf_tensor("sq", [128, 4096], F32))
        xin = e(nc.sbuf_tensor("xin", [128, 4096], F32))
        xn_bf = e(nc.sbuf_tensor("xn_bf", [128, 4096], BF16))
        xnn_bf = e(nc.sbuf_tensor("xnn_bf", [128, 4096], BF16))
        q_bf = e(nc.sbuf_tensor("q_bf", [128, 4096], BF16))
        kvtm = e(nc.sbuf_tensor("kvtm", [128, 2048], BF16))
        tmpe = e(nc.sbuf_tensor("tmpe", [128, 1536], F32))
        w_st = e(nc.sbuf_tensor("w_st", [128, 3072], F32))
        wq_bf = e(nc.sbuf_tensor("wq_bf", [128, 24576], BF16))
        gam_sb = e(nc.sbuf_tensor("gam_sb", [128, 8], F32))
        bet_sb = e(nc.sbuf_tensor("bet_sb", [128, 8], F32))
        kvp_sb = e(nc.sbuf_tensor("kvp_sb", [128, 520], F32))
        onesbf = e(nc.sbuf_tensor("onesbf", [128, 1], BF16))
        ctx = _mkctx(S, e, nc, T)
        p_t = e(nc.psum_tensor("p_t", [128, T], F32))[:]
        p_kvA = e(nc.psum_tensor("p_kvA", [128, 260], F32))[:]
        p_kvB = e(nc.psum_tensor("p_kvB", [128, 260], F32))[:]

        S.op("gpsimd", lambda en: en.memset(onesbf[:], 1.0), inc="sg")
        S.op("sync", lambda en: en.dma_start(out=gam_sb[:], in_=gam), inc="dw", dma=True)
        S.op("sync", lambda en: en.dma_start(out=bet_sb[:], in_=bet), inc="dw", dma=True)
        for k in range(8):
            if k > 0:
                S.wait("sync", "sv_wo", k)
            dk = S.op("sync", lambda en, w=wqkv[k * 128:(k + 1) * 128, :]:
                      en.dma_start(out=w_st[:], in_=w), inc="dwo", dma=True)
            S.wait("vector", "dwo", dk)
            S.op("vector", lambda en, o=wq_bf[:, k * 3072:(k + 1) * 3072]:
                 en.tensor_copy(o, w_st[:]), inc="sv_wo")

        S.wait("tensor", "sg", 3)
        S.wait("tensor", "dw", 32)
        xinc = chunks8(xin[:])
        xcc = chunks8(xc[:])
        sqc = chunks8(sq[:])
        for u in range(NU):
            usl = slice(u * T, (u + 1) * T)
            if u > 0:
                S.wait("sync", "sv_xcf", u)
            dl = S.op("sync", lambda en, s=usl: en.dma_start(
                out=xc[:].rearrange("p (c s) -> p c s", s=T),
                in_=x2T[:, s].rearrange("(c p) s -> p c s", p=128)), inc="dl", dma=True)
            S.wait("scalar", "dl", dl)
            S.wait("tensor", "dl", dl)
            if u > 0:
                S.wait("vector", "dgq", 32 * u)
            smbc = emit_ln_stats(S, ctx, xc[:], chunks8(xc[:]), sq[:], sqc, "lnc",
                                 bcfree=("svb2", 2 * u) if u > 0 else None)
            S.wait("vector", "sm_bc", smbc)
            for c in range(8):
                S.op("vector", lambda en, a=xinc[c], b=xcc[c]:
                     en.tensor_tensor(a, b, ctx["p_bcA"], ALU.mult),
                     inc="sv_xcf" if c == 7 else None)
            S.edge("vector")
            for c in range(8):
                S.op("vector", lambda en, a=xinc[c]:
                     en.tensor_tensor(a, a, ctx["p_bcB"], ALU.add),
                     inc="svb2" if c == 7 else None)
            S.edge("vector")
            svxn = S.op("vector", lambda en: en.tensor_copy(xn_bf[:], xin[:]), inc="sv_xnb")
            S.wait("gpsimd", "sv_xnb", svxn)
            S.op("gpsimd", lambda en, s=usl: en.dma_start(
                out=xnO[:, s].rearrange("(c p) s -> p c s", p=128),
                in_=xn_bf[:].rearrange("p (c s) -> p c s", s=T)), inc="dgq", dma=True)
            S.edge("scalar")
            smbc2 = emit_ln_stats(S, ctx, xin[:], xinc, sq[:], sqc, "lnc",
                                  bcfree=("svb2", 2 * u + 1),
                                  ready=[("svb2", 2 * u + 1)])
            S.wait("vector", "sm_bc", smbc2)
            for c in range(8):
                S.op("vector", lambda en, a=sqc[c], b=xinc[c]:
                     en.tensor_tensor(a, b, ctx["p_bcA"], ALU.mult))
            S.edge("vector")
            for c in range(8):
                S.op("vector", lambda en, a=sqc[c]:
                     en.tensor_tensor(a, a, ctx["p_bcB"], ALU.add),
                     inc="svb2" if c == 7 else None)
            S.edge("vector")
            xnnc = chunks8(xnn_bf[:])
            for c in range(8):
                S.op("vector", lambda en, a=xnnc[c], b=sqc[c], g=gam_sb[:, c:c + 1],
                     bb=bet_sb[:, c:c + 1]:
                     en.tensor_scalar(a, b, g, bb, ALU.mult, ALU.add),
                     inc="sv_xnn" if c == 7 else None)
            S.wait("tensor", "sv_xnn", u + 1)
            if u == 0:
                S.wait("tensor", "sv_wo", 8)
            qc = chunks8(q_bf[:])
            t1 = tmpe[:, 0:512]
            t2 = tmpe[:, 512:1024]
            t3 = tmpe[:, 1024:1536]
            for m in range(8):
                idx = u * 20 + m
                if u + m > 0:
                    S.wait("tensor", "sv_pt", S.cnt["sv_pt"])
                for kc in range(8):
                    S.op("tensor", lambda en,
                         w=wq_bf[:, kc * 3072 + m * 128: kc * 3072 + m * 128 + 128],
                         r=xnnc[kc], c0=(kc == 0), c1=(kc == 7):
                         en.matmul(p_t, w, r, start=c0, stop=c1),
                         inc="sm_t" if kc == 7 else None)
                smt = S.cnt["sm_t"]
                S.wait("vector", "sm_t", smt)
                S.op("vector", lambda en: en.tensor_scalar_min(t1, p_t, 0.0))
                S.op("vector", lambda en: en.tensor_scalar_max(t3, p_t, 0.0), inc="sv_e1")
                S.wait("scalar", "sv_e1", S.cnt["sv_e1"])
                S.op("scalar", lambda en: en.activation(t2, t1, AF.Exp), inc="ss_e1")
                S.wait("vector", "ss_e1", S.cnt["ss_e1"])
                S.op("vector", lambda en, o=qc[m]: en.tensor_add(o, t2, t3), inc="sv_pt")
            svq = S.cnt["sv_pt"]
            S.wait("gpsimd", "sv_pt", svq)
            S.op("gpsimd", lambda en, s=usl: en.dma_start(
                out=qO[:, s].rearrange("(c p) s -> p c s", p=128),
                in_=q_bf[:].rearrange("p (c s) -> p c s", s=T)), inc="dgq", dma=True)
            for tb in range(4):
                tok0 = tb * 128
                gb = u * 4 + tb
                for nb in range(4):
                    S.wait("tensor", "sv_pt", S.cnt["sv_pt"])
                    for kc in range(8):
                        S.op("tensor", lambda en,
                             w=xnnc[kc][:, tok0:tok0 + 128],
                             r=wq_bf[:, kc * 3072 + 1024 + nb * 512: kc * 3072 + 1024 + (nb + 1) * 512],
                             c0=(kc == 0), c1=(kc == 7):
                             en.matmul(p_t, w, r, start=c0, stop=c1),
                             inc="sm_t" if kc == 7 else None)
                    smt = S.cnt["sm_t"]
                    if nb < 2:
                        S.wait("vector", "sm_t", smt)
                        if gb > 0 and nb == 0:
                            S.wait("vector", "sm_kv", gb)
                        S.op("vector", lambda en: en.tensor_scalar_min(t1, p_t, 0.0))
                        S.op("vector", lambda en: en.tensor_scalar_max(t3, p_t, 0.0), inc="sv_e1")
                        S.wait("scalar", "sv_e1", S.cnt["sv_e1"])
                        S.op("scalar", lambda en: en.activation(t2, t1, AF.Exp), inc="ss_e1")
                        S.wait("vector", "ss_e1", S.cnt["ss_e1"])
                        S.op("vector", lambda en, o=kvtm[:, nb * 512:(nb + 1) * 512]:
                             en.tensor_add(o, t2, t3), inc="sv_pt")
                    else:
                        S.wait("scalar", "sm_t", smt)
                        if gb > 0 and nb == 2:
                            S.wait("scalar", "sm_kv", gb)
                        S.op("scalar", lambda en, o=kvtm[:, nb * 512:(nb + 1) * 512]:
                             en.copy(o, p_t), inc="sv_pt")
                S.wait("tensor", "sv_pt", S.cnt["sv_pt"])
                for hd in range(16):
                    pk = p_kvA if hd < 8 else p_kvB
                    base = (hd % 2) * 64
                    j2 = (hd // 2) % 4
                    S.op("tensor", lambda en, l=kvtm[:, hd * 64:(hd + 1) * 64],
                         r=kvtm[:, 1024 + hd * 64: 1024 + (hd + 1) * 64],
                         o=pk, b=base, j=j2, c0=(gb == 0), c1=(gb == 31):
                         en.matmul(o[b:b + 64, j * 65: j * 65 + 64], l, r, start=c0, stop=c1))
                    S.op("tensor", lambda en, l=kvtm[:, hd * 64:(hd + 1) * 64],
                         o=pk, b=base, j=j2, c0=(gb == 0), c1=(gb == 31):
                         en.matmul(o[b:b + 64, j * 65 + 64: j * 65 + 65], l, onesbf[:],
                                   start=c0, stop=c1),
                         inc="sm_kv" if hd == 15 else None)
        S.wait("scalar", "sm_kv", S.cnt["sm_kv"])
        S.op("scalar", lambda en: en.copy(kvp_sb[:, 0:260], p_kvA))
        S.op("scalar", lambda en: en.copy(kvp_sb[:, 260:520], p_kvB), inc="ss_f")
        S.wait("gpsimd", "ss_f", 1)
        S.op("gpsimd", lambda en: en.dma_start(out=kvO, in_=kvp_sb[:]), inc="dgq", dma=True)
        S.wait("gpsimd", "dgq", S.cnt["dgq"])

        hmap = {}
        for nm in S.sem_names:
            hmap[nm] = e(nc.semaphore(nm))
        block = e(nc.Block())
        S.emit(nc, block, hmap)
    return nc


def build_C2():
    nc = bass.Bass("TRN2", target_bir_lowering=False)
    x2T = nc.dram_tensor("x2T", [1024, 4096], F32, kind="ExternalInput").ap()
    xnI = nc.dram_tensor("xnI", [1024, 4096], BF16, kind="ExternalInput").ap()
    qI = nc.dram_tensor("qI", [1024, 4096], BF16, kind="ExternalInput").ap()
    kva = nc.dram_tensor("kva", [128, 520], F32, kind="ExternalInput").ap()
    kvb = nc.dram_tensor("kvb", [128, 520], F32, kind="ExternalInput").ap()
    wout = nc.dram_tensor("wout", [1024, 1024], F32, kind="ExternalInput").ap()
    x3T = nc.dram_tensor("x3T", [1024, 4096], F32, kind="ExternalOutput").ap()
    attO = nc.dram_tensor("attO", [1024, 4096], BF16, kind="ExternalOutput").ap()
    innO = nc.dram_tensor("innO", [1024, 4096], F32, kind="ExternalOutput").ap()
    T, NU = 512, 8

    with ExitStack() as st_:
        e = st_.enter_context
        S = Sched()
        xc = e(nc.sbuf_tensor("xc", [128, 4096], F32))
        sq = e(nc.sbuf_tensor("sq", [128, 4096], F32))
        ao = e(nc.sbuf_tensor("ao", [128, 4096], F32))
        qt = e(nc.sbuf_tensor("qt", [128, 4096], BF16))
        xnt = e(nc.sbuf_tensor("xnt", [128, 4096], BF16))
        att_bf = e(nc.sbuf_tensor("att_bf", [128, 4096], BF16))
        kv2 = e(nc.sbuf_tensor("kv2", [128, 1040], F32))
        kvf = e(nc.sbuf_tensor("kvf", [128, 520], F32))
        kv_bf = e(nc.sbuf_tensor("kv_bf", [128, 520], BF16))
        w_st = e(nc.sbuf_tensor("w_st", [128, 1024], F32))
        wo_bf = e(nc.sbuf_tensor("wo_bf", [128, 8192], BF16))
        dnbuf = e(nc.sbuf_tensor("dnbuf", [1, 2048], F32))
        ctx = _mkctx(S, e, nc, T)
        p_at = e(nc.psum_tensor("p_at", [128, T], F32))[:]
        p_dn = e(nc.psum_tensor("p_dn", [1, T], F32))[:]
        p_bc2 = e(nc.psum_tensor("p_bc2", [128, T], F32))[:]
        p_o = e(nc.psum_tensor("p_o", [128, T], F32))[:]

        S.op("sync", lambda en: en.dma_start(out=kv2[:, 0:520], in_=kva), inc="dw", dma=True)
        S.op("sync", lambda en: en.dma_start(out=kv2[:, 520:1040], in_=kvb), inc="dw", dma=True)
        S.wait("vector", "dw", 32)
        S.op("vector", lambda en: en.tensor_add(kvf[:], kv2[:, 0:520], kv2[:, 520:1040]))
        S.edge("vector")
        S.op("vector", lambda en: en.tensor_copy(kv_bf[:], kvf[:]), inc="sv_kv")
        for k in range(8):
            if k > 0:
                S.wait("sync", "sv_wo", k)
            dk = S.op("sync", lambda en, w=wout[k * 128:(k + 1) * 128, :]:
                      en.dma_start(out=w_st[:], in_=w), inc="dwo", dma=True)
            S.wait("vector", "dwo", dk)
            S.op("vector", lambda en, o=wo_bf[:, k * 1024:(k + 1) * 1024]:
                 en.tensor_copy(o, w_st[:]), inc="sv_wo")

        S.wait("tensor", "sg", 2)
        S.wait("tensor", "sv_kv", 1)
        aoc = chunks8(ao[:])
        xcc = chunks8(xc[:])
        sqc = chunks8(sq[:])
        attc = chunks8(att_bf[:])
        xntc = chunks8(xnt[:])
        for u in range(NU):
            usl = slice(u * T, (u + 1) * T)
            if u > 0:
                S.wait("sync", "sv_out", u)
                S.wait("sync", "ss_at", 8 * u)
                S.wait("sync", "sv_wo2", 8 * u)
            dl = S.op("sync", lambda en, s=usl: en.dma_start(
                out=xc[:].rearrange("p (c s) -> p c s", s=T),
                in_=x2T[:, s].rearrange("(c p) s -> p c s", p=128)), inc="dl", dma=True)
            S.op("sync", lambda en, s=usl: en.dma_start(
                out=qt[:].rearrange("p (c s) -> p c s", s=T),
                in_=qI[:, s].rearrange("(c p) s -> p c s", p=128)), inc="dl", dma=True)
            dl = S.op("sync", lambda en, s=usl: en.dma_start(
                out=xnt[:].rearrange("p (c s) -> p c s", s=T),
                in_=xnI[:, s].rearrange("(c p) s -> p c s", p=128)), inc="dl", dma=True)
            S.wait("tensor", "dl", dl)
            for j in range(8):
                jdx = u * 8 + j
                if jdx > 0:
                    S.wait("tensor", "ss_at", jdx)
                if u > 0 and j == 0:
                    S.wait("scalar", "dgx", 32 * u)
                for par in range(2):
                    hd = 2 * j + par
                    base = (hd % 2) * 64
                    cb = (hd // 2) * 65 if hd < 8 else 260 + (hd // 2 - 4) * 65
                    S.op("tensor", lambda en, l=kv_bf[base:base + 64, cb:cb + 64],
                         r=qt[base:base + 64, j * T:(j + 1) * T], b=base:
                         en.matmul(p_at[b:b + 64, :], l, r, start=True, stop=True),
                         inc="sm_at" if par == 1 else None)
                    if S.cnt.get("ss_dn", 0) > 0:
                        S.wait("tensor", "ss_dn", S.cnt["ss_dn"])
                    S.op("tensor", lambda en, l=kv_bf[base:base + 64, cb + 64:cb + 65],
                         r=qt[base:base + 64, j * T:(j + 1) * T]:
                         en.matmul(p_dn, l, r, start=True, stop=True), inc="sm_dn")
                    S.wait("scalar", "sm_dn", S.cnt["sm_dn"])
                    if par == 0 and jdx > 0:
                        S.wait("scalar", "sm_bc2", jdx)
                    S.op("scalar", lambda en, o=dnbuf[0:1, par * T:(par + 1) * T]:
                         en.copy(o, p_dn), inc="ss_dn")
                S.wait("vector", "ss_dn", S.cnt["ss_dn"])
                S.op("vector", lambda en: en.tensor_scalar_add(
                    dnbuf[0:1, 1024:2048], dnbuf[0:1, 0:1024], 1e-6))
                S.edge("vector")
                svr = S.op("vector", lambda en: en.reciprocal(
                    dnbuf[0:1, 0:1024], dnbuf[0:1, 1024:2048]), inc="sv_dn")
                S.wait("tensor", "sv_dn", svr)
                if jdx > 0:
                    S.wait("tensor", "sv_ab", jdx)
                for par in range(2):
                    S.op("tensor", lambda en, r=dnbuf[0:1, par * T:(par + 1) * T], b=par * 64:
                         en.matmul(p_bc2[b:b + 64, :], ctx["ones_row"][0:1, 0:64], r,
                                   start=True, stop=True),
                         inc="sm_bc2" if par == 1 else None)
                S.wait("scalar", "sm_at", S.cnt["sm_at"])
                S.op("scalar", lambda en, o=aoc[j]: en.copy(o, p_at), inc="ss_at")
                S.wait("vector", "ss_at", S.cnt["ss_at"])
                S.wait("vector", "sm_bc2", S.cnt["sm_bc2"])
                if u > 0 and j == 0:
                    S.wait("vector", "dgx", 32 * u)
                S.op("vector", lambda en, o=attc[j], a=aoc[j]:
                     en.tensor_tensor(o, a, p_bc2, ALU.mult), inc="sv_ab")
            S.wait("gpsimd", "sv_ab", 8 * (u + 1))
            S.op("gpsimd", lambda en, s=usl: en.dma_start(
                out=attO[:, s].rearrange("(c p) s -> p c s", p=128),
                in_=att_bf[:].rearrange("p (c s) -> p c s", s=T)), inc="dgx", dma=True)
            S.wait("tensor", "sv_ab", 8 * (u + 1))
            if u == 0:
                S.wait("tensor", "sv_wo", 8)
            for m in range(8):
                idx = u * 8 + m
                if idx > 0:
                    S.wait("tensor", "sv_wo2", idx)
                for kc in range(8):
                    S.op("tensor", lambda en,
                         w=wo_bf[:, kc * 1024 + m * 128: kc * 1024 + m * 128 + 128],
                         r=attc[kc], c0=(kc == 0), c1=(kc == 7):
                         en.matmul(p_o, w, r, start=c0, stop=c1),
                         inc="sm_o" if kc == 7 else None)
                S.wait("vector", "sm_o", S.cnt["sm_o"])
                S.op("vector", lambda en, o=aoc[m], xn_=xntc[m]:
                     en.tensor_tensor(o, p_o, xn_, ALU.add), inc="sv_wo2")
            S.wait("gpsimd", "sv_wo2", 8 * (u + 1))
            S.op("gpsimd", lambda en, s=usl: en.dma_start(
                out=innO[:, s].rearrange("(c p) s -> p c s", p=128),
                in_=ao[:].rearrange("p (c s) -> p c s", s=T)), inc="dgx", dma=True)
            S.edge("vector")
            if u > 0:
                S.wait("scalar", "st", 16 * u)
            smbc = emit_ln_stats(S, ctx, ao[:], aoc, sq[:], sqc, "lnc2",
                                 bcfree=("svb2", u) if u > 0 else None,
                                 ready=[("sv_wo2", 8 * (u + 1))])
            S.wait("vector", "sm_bc", smbc)
            S.wait("vector", "dl", dl)
            for c in range(8):
                S.op("vector", lambda en, a=sqc[c], b=aoc[c]:
                     en.tensor_tensor(a, b, ctx["p_bcA"], ALU.mult))
            S.edge("vector")
            for c in range(8):
                S.op("vector", lambda en, a=sqc[c]:
                     en.tensor_tensor(a, a, ctx["p_bcB"], ALU.add),
                     inc="svb2" if c == 7 else None)
            S.edge("vector")
            for c in range(8):
                S.op("vector", lambda en, a=sqc[c], b=xcc[c]:
                     en.tensor_tensor(a, a, b, ALU.add),
                     inc="sv_out" if c == 7 else None)
            S.wait("gpsimd", "sv_out", u + 1)
            S.op("gpsimd", lambda en, s=usl: en.dma_start(
                out=x3T[:, s].rearrange("(c p) s -> p c s", p=128),
                in_=sq[:].rearrange("p (c s) -> p c s", s=T)), inc="st", dma=True)
        S.wait("gpsimd", "st", 16 * NU)

        hmap = {}
        for nm in S.sem_names:
            hmap[nm] = e(nc.semaphore(nm))
        block = e(nc.Block())
        S.emit(nc, block, hmap)
    return nc


def build_D():
    nc = bass.Bass("TRN2", target_bir_lowering=False)
    x3T = nc.dram_tensor("x3T", [1024, 4096], F32, kind="ExternalInput").ap()
    gwT = nc.dram_tensor("gwT", [1024, 8], F32, kind="ExternalInput").ap()
    ebias = nc.dram_tensor("ebias", [8, 1], F32, kind="ExternalInput").ap()
    wupT = nc.dram_tensor("wupT", [8, 1024, 1408], F32, kind="ExternalInput").ap()
    wdnT = nc.dram_tensor("wdnT", [8, 1408, 1024], F32, kind="ExternalInput").ap()
    swuT = nc.dram_tensor("swuT", [1024, 1408], F32, kind="ExternalInput").ap()
    swdT = nc.dram_tensor("swdT", [1408, 1024], F32, kind="ExternalInput").ap()
    i8d = nc.dram_tensor("i8d", [8, 8], F32, kind="ExternalInput").ap()
    i128d = nc.dram_tensor("i128d", [128, 128], F32, kind="ExternalInput").ap()
    x4T = nc.dram_tensor("x4T", [1024, 4096], F32, kind="ExternalOutput").ap()
    T, NV, TV = 512, 4, 1024

    with ExitStack() as st_:
        e = st_.enter_context
        S = Sched()
        xc = e(nc.sbuf_tensor("xc", [128, 4096], F32))
        sq = e(nc.sbuf_tensor("sq", [128, 4096], F32))
        xnf = e(nc.sbuf_tensor("xnf", [128, 4096], F32))
        xn_bf = e(nc.sbuf_tensor("xn_bf", [128, 8192], BF16))
        z_bf = e(nc.sbuf_tensor("z_bf", [128, 4096], BF16))
        h_bf = e(nc.sbuf_tensor("h_bf", [128, 5632], BF16))
        macc = e(nc.sbuf_tensor("macc", [128, 8192], F32))
        w_st = e(nc.sbuf_tensor("w_st", [128, 1408], F32))
        wup_bf = e(nc.sbuf_tensor("wup_bf", [128, 11264], BF16))
        wdn_bf = e(nc.sbuf_tensor("wdn_bf", [128, 11264], BF16))
        gw_sb = e(nc.sbuf_tensor("gw_sb", [128, 64], F32))
        eb_sb = e(nc.sbuf_tensor("eb_sb", [8, 1], F32))
        i8_sb = e(nc.sbuf_tensor("i8_sb", [8, 8], F32))
        i128_sb = e(nc.sbuf_tensor("i128_sb", [128, 128], F32))
        scr_fm = e(nc.sbuf_tensor("scr_fm", [8, 512], F32))
        bia_fm = e(nc.sbuf_tensor("bia_fm", [8, 512], F32))
        tm_sb = e(nc.sbuf_tensor("tm_sb", [128, 64], F32))
        msk_sb = e(nc.sbuf_tensor("msk_sb", [128, 80], F32))
        gsc_row = e(nc.sbuf_tensor("gsc_row", [1, 8192], BF16))
        onesbf = e(nc.sbuf_tensor("onesbf", [1, 128], BF16))
        ctx = _mkctx(S, e, nc, T)
        p_g = e(nc.psum_tensor("p_g", [8, T], F32))[:]
        p_r = e(nc.psum_tensor("p_r", [128, 64], F32))[:]
        p_u = e(nc.psum_tensor("p_u", [128, T], F32))[:]
        p_d = e(nc.psum_tensor("p_d", [128, T], F32))[:]

        S.op("gpsimd", lambda en: en.memset(onesbf[:], 1.0), inc="sg")
        S.op("sync", lambda en: en.dma_start(
            out=gw_sb[:].rearrange("p (c n) -> p c n", n=8),
            in_=gwT.rearrange("(c p) n -> p c n", p=128)), inc="dw", dma=True)
        S.op("sync", lambda en: en.dma_start(out=eb_sb[:], in_=ebias), inc="dw", dma=True)
        S.op("sync", lambda en: en.dma_start(out=i8_sb[:], in_=i8d), inc="dw", dma=True)
        S.op("sync", lambda en: en.dma_start(out=i128_sb[:], in_=i128d), inc="dw", dma=True)

        def load_expert(g):
            if S.cnt.get("sm_u", 0) > 0:
                S.wait("vector", "sm_u", S.cnt["sm_u"])
            for k in range(8):
                if S.cnt.get("sv_wc", 0) > 0:
                    S.wait("sync", "sv_wc", S.cnt["sv_wc"])
                src = swuT[k * 128:(k + 1) * 128, :] if g < 0 else wupT[g, k * 128:(k + 1) * 128, :]
                dk = S.op("sync", lambda en, w=src: en.dma_start(out=w_st[:], in_=w),
                          inc="dwo", dma=True)
                S.wait("vector", "dwo", dk)
                S.op("vector", lambda en, o=wup_bf[:, k * 1408:(k + 1) * 1408]:
                     en.tensor_copy(o, w_st[:]), inc="sv_wc")
            if S.cnt.get("sm_d", 0) > 0:
                S.wait("vector", "sm_d", S.cnt["sm_d"])
            for k in range(11):
                S.wait("sync", "sv_wc", S.cnt["sv_wc"])
                src = swdT[k * 128:(k + 1) * 128, :] if g < 0 else wdnT[g, k * 128:(k + 1) * 128, :]
                dk = S.op("sync", lambda en, w=src: en.dma_start(out=w_st[:, 0:1024], in_=w),
                          inc="dwo", dma=True)
                S.wait("vector", "dwo", dk)
                S.op("vector", lambda en, o=wdn_bf[:, k * 1024:(k + 1) * 1024]:
                     en.tensor_copy(o, w_st[:, 0:1024]), inc="sv_wc")

        S.wait("tensor", "sg", 3)
        S.wait("tensor", "dw", 64)
        xnfc = chunks8(xnf[:])
        xcc = chunks8(xc[:])
        sqc = chunks8(sq[:])
        for v in range(NV):
            for w in range(2):
                g0 = v * TV + w * T
                usl = slice(g0, g0 + T)
                if v + w > 0:
                    S.wait("sync", "sv_xnb", 2 * v + w)
                if S.cnt.get("sv_out", 0) > 0 and w == 0:
                    S.wait("sync", "sv_out", S.cnt["sv_out"])
                dl = S.op("sync", lambda en, s=usl: en.dma_start(
                    out=xc[:].rearrange("p (c s) -> p c s", s=T),
                    in_=x3T[:, s].rearrange("(c p) s -> p c s", p=128)), inc="dl", dma=True)
                S.wait("scalar", "dl", dl)
                S.wait("tensor", "dl", dl)
                if v + w == 0:
                    bcf = None
                elif w == 0:
                    bcf = ("sv_z", S.cnt["sv_z"])
                else:
                    bcf = ("sv_xn", S.cnt["sv_xn"])
                smbc = emit_ln_stats(S, ctx, xc[:], xcc, sq[:], sqc, "lnd", bcfree=bcf)
                S.wait("vector", "sm_bc", smbc)
                if S.cnt.get("st", 0) > 0 and w == 0:
                    S.wait("vector", "st", S.cnt["st"])
                for c in range(8):
                    S.op("vector", lambda en, a=xnfc[c], b=xcc[c]:
                         en.tensor_tensor(a, b, ctx["p_bcA"], ALU.mult))
                S.edge("vector")
                for c in range(8):
                    S.op("vector", lambda en, a=xnfc[c]:
                         en.tensor_tensor(a, a, ctx["p_bcB"], ALU.add),
                         inc="sv_xn" if c == 7 else None)
                S.edge("vector")
                xnb3 = xn_bf[:].rearrange("p (c s) -> p c s", s=TV)
                svb = S.op("vector", lambda en, o=xnb3[:, :, w * T:(w + 1) * T],
                           i=xnf[:].rearrange("p (c s) -> p c s", s=T):
                           en.tensor_copy(o, i), inc="sv_xnb")
                S.wait("tensor", "sv_xn", S.cnt["sv_xn"])
                if S.cnt.get("ss_g", 0) > 0:
                    S.wait("tensor", "ss_g", S.cnt["ss_g"])
                for c in range(8):
                    S.op("tensor", lambda en, wg=gw_sb[:, c * 8:(c + 1) * 8], r=xnfc[c],
                         c0=(c == 0), c1=(c == 7):
                         en.matmul(p_g, wg, r, start=c0, stop=c1),
                         inc="sm_g" if c == 7 else None)
                S.wait("scalar", "sm_g", S.cnt["sm_g"])
                S.op("scalar", lambda en: en.activation(scr_fm[:], p_g, AF.Sigmoid), inc="ss_g")
                S.wait("vector", "ss_g", S.cnt["ss_g"])
                S.op("vector", lambda en: en.tensor_scalar(
                    bia_fm[:], scr_fm[:], eb_sb[:], None, ALU.add), inc="sv_g")
                S.wait("tensor", "sv_g", S.cnt["sv_g"])
                if S.cnt.get("sv_tm", 0) > 0:
                    S.wait("tensor", "sv_tm", S.cnt["sv_tm"])
                for bk in range(4):
                    S.op("tensor", lambda en, l=bia_fm[:, bk * 128:(bk + 1) * 128], b=bk:
                         en.matmul(p_r[:, b * 8:(b + 1) * 8], l, i8_sb[:], start=True, stop=True))
                for bk in range(4):
                    S.op("tensor", lambda en, l=scr_fm[:, bk * 128:(bk + 1) * 128], b=bk:
                         en.matmul(p_r[:, 32 + b * 8: 32 + (b + 1) * 8], l, i8_sb[:],
                                   start=True, stop=True),
                         inc="sm_tm" if bk == 3 else None)
                S.wait("vector", "sm_tm", S.cnt["sm_tm"])
                S.op("vector", lambda en: en.tensor_copy(tm_sb[:], p_r), inc="sv_tm")
                S.edge("vector")
                for bk in range(4):
                    S.op("vector", lambda en, b=bk: en.reduce_max(
                        msk_sb[:, b:b + 1], tm_sb[:, b * 8:(b + 1) * 8], axis=AX.X))
                S.edge("vector")
                for bk in range(4):
                    S.op("vector", lambda en, b=bk: en.tensor_scalar(
                        msk_sb[:, 8 + b * 8: 8 + (b + 1) * 8], tm_sb[:, b * 8:(b + 1) * 8],
                        msk_sb[:, b:b + 1], None, ALU.is_equal))
                S.edge("vector")
                for bk in range(4):
                    S.op("vector", lambda en, b=bk: en.scalar_tensor_tensor(
                        msk_sb[:, 40 + b * 8: 40 + (b + 1) * 8],
                        msk_sb[:, 8 + b * 8: 8 + (b + 1) * 8], -1e30,
                        tm_sb[:, b * 8:(b + 1) * 8], ALU.mult, ALU.add))
                S.edge("vector")
                for bk in range(4):
                    S.op("vector", lambda en, b=bk: en.reduce_max(
                        msk_sb[:, 4 + b:5 + b], msk_sb[:, 40 + b * 8: 40 + (b + 1) * 8], axis=AX.X))
                S.edge("vector")
                for bk in range(4):
                    S.op("vector", lambda en, b=bk: en.tensor_scalar(
                        msk_sb[:, 8 + b * 8: 8 + (b + 1) * 8], tm_sb[:, b * 8:(b + 1) * 8],
                        msk_sb[:, 4 + b:5 + b], None, ALU.is_ge))
                S.edge("vector")
                for bk in range(4):
                    S.op("vector", lambda en, b=bk: en.tensor_mul(
                        msk_sb[:, 8 + b * 8: 8 + (b + 1) * 8],
                        msk_sb[:, 8 + b * 8: 8 + (b + 1) * 8],
                        tm_sb[:, 32 + b * 8: 32 + (b + 1) * 8]),
                        inc="sv_mk" if bk == 3 else None)
                S.wait("tensor", "sv_mk", S.cnt["sv_mk"])
                for g in range(8):
                    S.wait("tensor", "ss_g", S.cnt["ss_g"])
                    for bk in range(4):
                        S.op("tensor", lambda en,
                             l=msk_sb[:, 8 + bk * 8 + g: 8 + bk * 8 + g + 1], b=bk:
                             en.matmul(p_g[0:1, b * 128:(b + 1) * 128], l, i128_sb[:],
                                       start=True, stop=True),
                             inc="sm_g2" if bk == 3 else None)
                    S.wait("scalar", "sm_g2", S.cnt["sm_g2"])
                    S.op("scalar", lambda en, o=gsc_row[0:1, g * 1024 + w * T: g * 1024 + (w + 1) * T]:
                         en.copy(o, p_g[0:1, 0:T]), inc="ss_g")
            gsc_ready = S.cnt["ss_g"]
            for g in range(-1, 8):
                load_expert(g)
                wready = S.cnt["sv_wc"]
                S.wait("tensor", "sv_wc", wready)
                for half in range(2):
                    if g >= 0:
                        if S.cnt.get("sv_z", 0) > 0:
                            S.wait("tensor", "sv_z", S.cnt["sv_z"])
                        S.wait("tensor", "sv_xnb", 2 * v + 2)
                        S.wait("tensor", "ss_g", gsc_ready)
                        S.op("tensor", lambda en,
                             r=gsc_row[0:1, g * 1024 + half * T: g * 1024 + (half + 1) * T]:
                             en.matmul(ctx["p_bcA"], onesbf[:], r, start=True, stop=True),
                             inc="sm_z")
                        S.wait("vector", "sm_z", S.cnt["sm_z"])
                        for c in range(8):
                            S.op("vector", lambda en, o=z_bf[:, c * T:(c + 1) * T],
                                 i=xn_bf[:, c * TV + half * T: c * TV + (half + 1) * T]:
                                 en.tensor_tensor(o, i, ctx["p_bcA"], ALU.mult),
                                 inc="sv_z" if c == 7 else None)
                        S.wait("tensor", "sv_z", S.cnt["sv_z"])
                    for j in range(11):
                        if S.cnt.get("ss_h", 0) > 0:
                            S.wait("tensor", "ss_h", S.cnt["ss_h"])
                        if S.cnt.get("sm_d", 0) > 0 and j == 0:
                            S.wait("scalar", "sm_d", S.cnt["sm_d"])
                        for kc in range(8):
                            if g >= 0:
                                rz = z_bf[:, kc * T:(kc + 1) * T]
                            else:
                                rz = xn_bf[:, kc * TV + half * T: kc * TV + (half + 1) * T]
                            S.op("tensor", lambda en,
                                 wslice=wup_bf[:, kc * 1408 + j * 128: kc * 1408 + j * 128 + 128],
                                 r=rz, c0=(kc == 0), c1=(kc == 7):
                                 en.matmul(p_u, wslice, r, start=c0, stop=c1),
                                 inc="sm_u" if kc == 7 else None)
                        S.wait("scalar", "sm_u", S.cnt["sm_u"])
                        S.op("scalar", lambda en, o=h_bf[:, j * T:(j + 1) * T]:
                             en.activation(o, p_u, AF.Relu))
                        S.edge("scalar")
                        S.op("scalar", lambda en, o=h_bf[:, j * T:(j + 1) * T]:
                             en.activation(o, o, AF.Square), inc="ss_h")
                    S.wait("tensor", "ss_h", S.cnt["ss_h"])
                    for m in range(8):
                        if S.cnt.get("sv_ac", 0) > 0:
                            S.wait("tensor", "sv_ac", S.cnt["sv_ac"])
                        for jc in range(11):
                            S.op("tensor", lambda en,
                                 wslice=wdn_bf[:, jc * 1024 + m * 128: jc * 1024 + m * 128 + 128],
                                 r=h_bf[:, jc * T:(jc + 1) * T],
                                 c0=(jc == 0), c1=(jc == 10):
                                 en.matmul(p_d, wslice, r, start=c0, stop=c1),
                                 inc="sm_d" if jc == 10 else None)
                        S.wait("vector", "sm_d", S.cnt["sm_d"])
                        mslice = macc[:, m * TV + half * T: m * TV + (half + 1) * T]
                        if g < 0:
                            S.op("vector", lambda en, o=mslice: en.tensor_copy(o, p_d),
                                 inc="sv_ac")
                        else:
                            S.op("vector", lambda en, o=mslice:
                                 en.tensor_tensor(o, o, p_d, ALU.add), inc="sv_ac")
            for w in range(2):
                g0 = v * TV + w * T
                usl = slice(g0, g0 + T)
                if S.cnt.get("st", 0) > 0:
                    S.wait("sync", "st", S.cnt["st"])
                S.wait("sync", "sv_xnb", 2 * v + 2)
                if S.cnt.get("sv_out", 0) > 0:
                    S.wait("sync", "sv_out", S.cnt["sv_out"])
                dl = S.op("sync", lambda en, s=usl: en.dma_start(
                    out=xc[:].rearrange("p (c s) -> p c s", s=T),
                    in_=x3T[:, s].rearrange("(c p) s -> p c s", p=128)), inc="dl", dma=True)
                mwhole = macc[:].rearrange("p (c s) -> p c s", s=TV)[:, :, w * T:(w + 1) * T]
                mch = [macc[:, c * TV + w * T: c * TV + w * T + T] for c in range(8)]
                if w == 0:
                    bcf2 = ("sv_z", S.cnt["sv_z"])
                else:
                    bcf2 = ("svo2", S.cnt["svo2"])
                smbc = emit_ln_stats(S, ctx, mwhole, mch, sq[:], sqc, "lnd2",
                                     bcfree=bcf2, ready=[("sv_ac", S.cnt["sv_ac"])])
                S.wait("vector", "sm_bc", smbc)
                S.wait("vector", "dl", dl)
                if w == 1:
                    S.wait("vector", "st", S.cnt["st"])
                for c in range(8):
                    S.op("vector", lambda en, a=xnfc[c], b=mch[c]:
                         en.tensor_tensor(a, b, ctx["p_bcA"], ALU.mult))
                S.edge("vector")
                for c in range(8):
                    S.op("vector", lambda en, a=xnfc[c]:
                         en.tensor_tensor(a, a, ctx["p_bcB"], ALU.add),
                         inc="svo2" if c == 7 else None)
                S.edge("vector")
                for c in range(8):
                    S.op("vector", lambda en, a=xnfc[c], b=xcc[c]:
                         en.tensor_tensor(a, a, b, ALU.add),
                         inc="sv_out" if c == 7 else None)
                S.wait("gpsimd", "sv_out", S.cnt["sv_out"])
                S.op("gpsimd", lambda en, s=usl: en.dma_start(
                    out=x4T[:, s].rearrange("(c p) s -> p c s", p=128),
                    in_=xnf[:].rearrange("p (c s) -> p c s", s=T)), inc="st", dma=True)
        S.wait("gpsimd", "st", S.cnt["st"])

        hmap = {}
        for nm in S.sem_names:
            hmap[nm] = e(nc.semaphore(nm))
        block = e(nc.Block())
        S.emit(nc, block, hmap)
    return nc


def kernel(**inp):
    x = np.ascontiguousarray(inp["L0_attn_x"], dtype=np.float32)
    cores = list(range(8))

    def cat(v):
        return np.ascontiguousarray(v)

    xT = [cat(x[b].T) for b in range(4)]

    nc = build_A1()
    maps = [{"xT": xT[c // 2],
             "bmat": cat(inp["L0_attn_B"]),
             "alhsT": cat(inp["L0_attn_A"].T)} for c in cores]
    r1 = run_bass_kernel_spmd(nc, maps, core_ids=cores).results

    nc = build_A2()
    maps = []
    for c in cores:
        b, h = c // 2, c % 2
        sl = slice(h * 4096, (h + 1) * 4096)
        maps.append({"hs_in": cat(r1[c]["hs_out"][:, sl]),
                     "xTown": cat(xT[b][:, sl]),
                     "cmat": cat(inp["L0_attn_C"]),
                     "wot": cat(inp["L0_attn_W_out"].T),
                     "bo": cat(inp["L0_attn_b_out"].reshape(8, 128).T)})
    r2 = run_bass_kernel_spmd(nc, maps, core_ids=cores).results

    nc = build_B()
    maps = []
    for c in cores:
        b, h = c // 2, c % 2
        if h == 0:
            pcol = np.zeros((128, 8), np.float32)
        else:
            pcol = cat(r2[2 * b]["x1"][:, 4095].reshape(8, 128).T)
        maps.append({"x1T": cat(r2[c]["x1"]),
                     "pcol": pcol,
                     "wrT": cat(inp["L0_ffn_W_r"].T),
                     "wkT": cat(inp["L0_ffn_W_k"].T),
                     "wvT": cat(inp["L0_ffn_W_v"].T),
                     "mur": cat(inp["L0_ffn_mu_r"].reshape(8, 128).T),
                     "muk": cat(inp["L0_ffn_mu_k"].reshape(8, 128).T)})
    r3 = run_bass_kernel_spmd(nc, maps, core_ids=cores).results

    nc = build_C1()
    maps = [{"x2T": cat(r3[c]["x2T"]),
             "wqkv": cat(inp["L1_attn_Wqkv"]),
             "gam": cat(inp["L1_attn_gamma"].reshape(8, 128).T),
             "bet": cat(inp["L1_attn_beta"].reshape(8, 128).T)} for c in cores]
    r4 = run_bass_kernel_spmd(nc, maps, core_ids=cores).results

    nc = build_C2()
    maps = []
    for c in cores:
        b, h = c // 2, c % 2
        maps.append({"x2T": cat(r3[c]["x2T"]),
                     "xnI": cat(r4[c]["xnO"]),
                     "qI": cat(r4[c]["qO"]),
                     "kva": cat(r4[c]["kvO"]),
                     "kvb": cat(r4[2 * b + (1 - h)]["kvO"]),
                     "wout": cat(inp["L1_attn_Wout"])})
    r5 = run_bass_kernel_spmd(nc, maps, core_ids=cores).results

    nc = build_D()
    common = {"gwT": cat(inp["L1_ffn_gate_w"].T),
              "ebias": cat(inp["L1_ffn_expert_bias"].reshape(8, 1)),
              "wupT": cat(inp["L1_ffn_w_up"].transpose(0, 2, 1)),
              "wdnT": cat(inp["L1_ffn_w_down"].transpose(0, 2, 1)),
              "swuT": cat(inp["L1_ffn_shared_w_up"].T),
              "swdT": cat(inp["L1_ffn_shared_w_down"].T),
              "i8d": np.eye(8, dtype=np.float32),
              "i128d": np.eye(128, dtype=np.float32)}
    maps = []
    for c in cores:
        m = dict(common)
        m["x3T"] = cat(r5[c]["x3T"])
        maps.append(m)
    r6 = run_bass_kernel_spmd(nc, maps, core_ids=cores).results

    out = np.empty((4, 8192, 1024), np.float32)
    for c in cores:
        b, h = c // 2, c % 2
        out[b, h * 4096:(h + 1) * 4096, :] = np.asarray(r6[c]["x4T"]).T
    return out


# version 03 (final)
# speedup vs baseline: 1.1548x; 1.1522x over previous
import numpy as np
from contextlib import ExitStack
import concourse.bass as bass
import concourse.mybir as mybir
from concourse.bass_utils import run_bass_kernel_spmd

F32 = mybir.dt.float32
BF16 = mybir.dt.bfloat16
AF = mybir.ActivationFunctionType
ALU = mybir.AluOpType
AX = mybir.AxisListType

ENGINES = ("sync", "vector", "scalar", "tensor", "gpsimd")


class Sched:

    def __init__(self):
        self.streams = {e: [] for e in ENGINES}
        self.cnt = {}
        self.sem_names = []

    def sem(self, name):
        if name not in self.cnt:
            self.cnt[name] = 0
            self.sem_names.append(name)
        return name

    def op(self, engine, fn, inc=None, dma=False):
        incs = []
        if inc is not None:
            if isinstance(inc, str):
                inc = [inc]
            assert len(inc) <= 1, "one sync update per instruction"
            for s in inc:
                self.sem(s)
                self.cnt[s] += 16 if dma else 1
                incs.append((s, 16 if dma else 1))
        self.streams[engine].append(("op", fn, incs))
        if inc:
            return self.cnt[inc[0]]
        return None

    def wait(self, engine, sem, val):
        assert sem in self.cnt, f"wait on unknown sem {sem}"
        assert val <= self.cnt[sem], f"wait {sem}>={val} but max planned {self.cnt[sem]}"
        self.streams[engine].append(("wait", sem, val))

    def edge(self, engine, name=None):
        for it in reversed(self.streams[engine]):
            if it[0] == "op":
                if it[2]:
                    s0, _ = it[2][0]
                    self.streams[engine].append(("wait", s0, self.cnt[s0]))
                    return
                if name is None:
                    name = "_edge_" + engine
                s = self.sem(name)
                self.cnt[s] += 1
                it[2].append((s, 1))
                self.streams[engine].append(("wait", s, self.cnt[s]))
                return
        raise RuntimeError("edge with no preceding op")

    def emit(self, nc, block, sem_handles):
        def run_stream(eng_obj, items):
            for it in items:
                if it[0] == "op":
                    _, fn, incs = it
                    instr = fn(eng_obj)
                    for (s, v) in incs:
                        instr.then_inc(sem_handles[s], v)
                elif it[0] == "wait":
                    eng_obj.wait_ge(sem_handles[it[1]], it[2])

        decorators = {
            "sync": block.sync, "vector": block.vector, "scalar": block.scalar,
            "tensor": block.tensor, "gpsimd": block.gpsimd,
        }
        for name in ENGINES:
            items = self.streams[name]
            if not items:
                continue
            dec = decorators[name]

            def section(eng_obj, _items=items):
                run_stream(eng_obj, _items)
            dec(section)


def emit_ln_stats(S, ctx, tile_whole, tile_chunks, sq_whole, sq_chunks, tag, bcfree=None, ready=()):
    p_sum, p_sumsq, p_bcA, p_bcB = ctx["p_sum"], ctx["p_sumsq"], ctx["p_bcA"], ctx["p_bcB"]
    ones_col, ones_row = ctx["ones_col"], ctx["ones_row"]
    R = ctx["rows"]
    n = ctx.setdefault("ln_n", 0)
    KC = len(tile_chunks)

    for (sm_, vv_) in ready:
        S.wait("scalar", sm_, vv_)
        S.wait("tensor", sm_, vv_)
    if ("lsmv", tag) in ctx:
        S.wait("scalar", "sm_" + tag, ctx[("lsmv", tag)])
    ssq = S.op("scalar", lambda e, a=sq_whole, b=tile_whole: e.activation(a, b, AF.Square),
               inc="ssq_" + tag)
    if n > 0:
        S.wait("tensor", "svrow", n)
    for c in range(KC):
        S.op("tensor", lambda e, r=tile_chunks[c], c0=(c == 0), c1=(c == KC - 1):
             e.matmul(p_sum, ones_col, r, start=c0, stop=c1))
    S.wait("tensor", "ssq_" + tag, ssq)
    for c in range(KC):
        S.op("tensor", lambda e, r=sq_chunks[c], c0=(c == 0), c1=(c == KC - 1):
             e.matmul(p_sumsq, ones_col, r, start=c0, stop=c1),
             inc=("sm_" + tag) if c == KC - 1 else None)
    smv = S.cnt["sm_" + tag]
    ctx[("lsmv", tag)] = smv

    S.wait("vector", "sm_" + tag, smv)
    if ctx.get("last_smbc_cnt", 0) > 0:
        S.wait("vector", "sm_bc", ctx["last_smbc_cnt"])
    S.op("vector", lambda e: e.tensor_scalar_mul(R["mean"], p_sum, 1.0 / 1024.0))
    S.op("vector", lambda e: e.tensor_scalar_mul(R["msq"], p_sumsq, 1.0 / 1024.0),
         inc="svrow")
    S.edge("vector")
    S.op("vector", lambda e: e.tensor_mul(R["m2"], R["mean"], R["mean"]))
    S.edge("vector")
    S.op("vector", lambda e: e.scalar_tensor_tensor(R["var"], R["m2"], -1.0, R["msq"],
                                                    ALU.mult, ALU.add))
    S.edge("vector")
    sva = S.op("vector", lambda e: e.tensor_scalar_add(R["vare"], R["var"], 1e-5),
               inc="sv_" + tag)
    S.wait("scalar", "sv_" + tag, sva)
    ssd = S.op("scalar", lambda e: e.activation(R["std"], R["vare"], AF.Sqrt),
               inc="ss_" + tag)
    S.wait("vector", "ss_" + tag, ssd)
    S.op("vector", lambda e: e.reciprocal(R["rstd"], R["std"]))
    S.edge("vector")
    svb = S.op("vector", lambda e: e.scalar_tensor_tensor(R["brow"], R["mean"], -1.0,
                                                          R["rstd"], ALU.mult, ALU.mult),
               inc="sv2_" + tag)
    S.wait("tensor", "sv2_" + tag, svb)
    if bcfree is not None:
        S.wait("tensor", bcfree[0], bcfree[1])
    S.op("tensor", lambda e: e.matmul(p_bcA, ones_row, R["rstd"], start=True, stop=True))
    smbc = S.op("tensor", lambda e: e.matmul(p_bcB, ones_row, R["brow"], start=True, stop=True),
                inc="sm_bc")
    ctx["ln_n"] = n + 1
    ctx["last_smbc_cnt"] = smbc
    return smbc


def _mkctx(S, e, nc, T=512):
    ctx = {}
    ctx["ones_col"] = e(nc.sbuf_tensor("ones_col", [128, 1], F32))[:]
    ctx["ones_row"] = e(nc.sbuf_tensor("ones_row", [1, 128], F32))[:]
    rowbuf = e(nc.sbuf_tensor("rowbuf", [1, 8 * T], F32))
    names = ["mean", "msq", "m2", "var", "vare", "std", "rstd", "brow"]
    ctx["rows"] = {nm: rowbuf[:, i * T:(i + 1) * T] for i, nm in enumerate(names)}
    ctx["p_sum"] = e(nc.psum_tensor("p_sum", [1, T], F32))[:]
    ctx["p_sumsq"] = e(nc.psum_tensor("p_sumsq", [1, T], F32))[:]
    ctx["p_bcA"] = e(nc.psum_tensor("p_bcA", [128, T], F32))[:]
    ctx["p_bcB"] = e(nc.psum_tensor("p_bcB", [128, T], F32))[:]
    S.op("gpsimd", lambda en, a=ctx["ones_col"]: en.memset(a, 1.0), inc="sg")
    S.op("gpsimd", lambda en, a=ctx["ones_row"]: en.memset(a, 1.0), inc="sg")
    return ctx


def chunks8(tile, T=512):
    return [tile[:, c * T:(c + 1) * T] for c in range(8)]


def build_A1():
    nc = bass.Bass("TRN2", target_bir_lowering=False)
    xT = nc.dram_tensor("xT", [1024, 8192], F32, kind="ExternalInput").ap()
    bmat = nc.dram_tensor("bmat", [1024, 16], F32, kind="ExternalInput").ap()
    alhsT = nc.dram_tensor("alhsT", [16, 16], F32, kind="ExternalInput").ap()
    hs_out = nc.dram_tensor("hs_out", [16, 8192], F32, kind="ExternalOutput").ap()
    T, NT, SEQ = 512, 16, 8192

    with ExitStack() as st_:
        e = st_.enter_context
        S = Sched()
        xc = e(nc.sbuf_tensor("xc", [128, 4096], F32))
        sq = e(nc.sbuf_tensor("sq", [128, 4096], F32))
        xn = e(nc.sbuf_tensor("xn", [128, 4096], F32))
        xb_buf = e(nc.sbuf_tensor("xb_buf", [16, SEQ], F32))
        hs_buf = e(nc.sbuf_tensor("hs_buf", [16, SEQ], F32))
        bm_sb = e(nc.sbuf_tensor("bm_sb", [128, 128], F32))
        al_sb = e(nc.sbuf_tensor("al_sb", [16, 16], F32))
        ctx = _mkctx(S, e, nc, T)
        ctx["sq_chunks"] = None
        p_xb = e(nc.psum_tensor("p_xb", [16, T], F32))[:]
        p_h = e(nc.psum_tensor("p_h", [16, 1], F32))[:]

        S.op("sync", lambda en: en.dma_start(
            out=bm_sb[:].rearrange("p (c n) -> p c n", n=16),
            in_=bmat.rearrange("(c p) n -> p c n", p=128)), inc="dw", dma=True)
        S.op("sync", lambda en: en.dma_start(out=al_sb[:], in_=alhsT), inc="dw", dma=True)

        S.wait("tensor", "sg", 2)
        S.wait("tensor", "dw", 32)
        for t in range(NT):
            tsl = slice(t * T, (t + 1) * T)
            if t > 0:
                S.wait("sync", "sv_xc", t)
            dl = S.op("sync", lambda en, s=tsl: en.dma_start(
                out=xc[:].rearrange("p (c s) -> p c s", s=T),
                in_=xT[:, s].rearrange("(c p) s -> p c s", p=128)), inc="dl", dma=True)
            S.wait("scalar", "dl", dl)
            S.wait("tensor", "dl", dl)
            smbc = emit_ln_stats(S, ctx, xc[:], chunks8(xc[:]), sq[:], chunks8(sq[:]), "ln1",
                                 bcfree=("sv_xn", t) if t > 0 else None)
            S.wait("vector", "sm_bc", smbc)
            xnc, xcc = chunks8(xn[:]), chunks8(xc[:])
            for c in range(8):
                S.op("vector", lambda en, a=xnc[c], b=xcc[c]:
                     en.tensor_tensor(a, b, ctx["p_bcA"], ALU.mult),
                     inc="sv_xc" if c == 7 else None)
            S.edge("vector")
            for c in range(8):
                S.op("vector", lambda en, a=xnc[c]:
                     en.tensor_tensor(a, a, ctx["p_bcB"], ALU.add),
                     inc="sv_xn" if c == 7 else None)
            S.wait("tensor", "sv_xn", t + 1)
            if t > 0:
                S.wait("tensor", "ss_xb", t)
            for c in range(8):
                S.op("tensor", lambda en, r=xnc[c], w=bm_sb[:, c * 16:(c + 1) * 16],
                     c0=(c == 0), c1=(c == 7):
                     en.matmul(p_xb, w, r, start=c0, stop=c1),
                     inc="sm3" if c == 7 else None)
            S.wait("scalar", "sm3", t + 1)
            S.op("scalar", lambda en, o=xb_buf[:, tsl]: en.copy(o, p_xb), inc="ss_xb")

        S.op("scalar", lambda en: en.activation(hs_buf[:, 0:1], xb_buf[:, 0:1], AF.Tanh),
             inc="sh")
        for s in range(1, SEQ):
            S.wait("tensor", "sh", s)
            S.op("tensor", lambda en, h=hs_buf[:, s - 1:s]:
                 en.matmul(p_h, al_sb[:], h, start=True, stop=True), inc="smh")
            S.wait("scalar", "smh", s)
            S.op("scalar", lambda en, o=hs_buf[:, s:s + 1], b=xb_buf[:, s:s + 1]:
                 en.activation(o, p_h, AF.Tanh, bias=b, scale=1.0), inc="sh")

        S.wait("gpsimd", "sh", SEQ)
        S.op("gpsimd", lambda en: en.dma_start(out=hs_out, in_=hs_buf[:]), inc="st", dma=True)
        S.wait("gpsimd", "st", 16)

        hmap = {}
        for nm in S.sem_names:
            hmap[nm] = e(nc.semaphore(nm))
        block = e(nc.Block())
        S.emit(nc, block, hmap)
    return nc


def build_A2():
    nc = bass.Bass("TRN2", target_bir_lowering=False)
    hs_in = nc.dram_tensor("hs_in", [16, 4096], F32, kind="ExternalInput").ap()
    xTown = nc.dram_tensor("xTown", [1024, 4096], F32, kind="ExternalInput").ap()
    cmat = nc.dram_tensor("cmat", [16, 1024], F32, kind="ExternalInput").ap()
    wot = nc.dram_tensor("wot", [1024, 1024], F32, kind="ExternalInput").ap()
    bo = nc.dram_tensor("bo", [128, 8], F32, kind="ExternalInput").ap()
    x1 = nc.dram_tensor("x1", [1024, 4096], F32, kind="ExternalOutput").ap()
    T, NU = 512, 8

    with ExitStack() as st_:
        e = st_.enter_context
        S = Sched()
        xc = e(nc.sbuf_tensor("xc", [128, 4096], F32))
        sq = e(nc.sbuf_tensor("sq", [128, 4096], F32))
        ao = e(nc.sbuf_tensor("ao", [128, 4096], F32))
        y_bf = e(nc.sbuf_tensor("y_bf", [128, 4096], BF16))
        hs_sb = e(nc.sbuf_tensor("hs_sb", [16, 4096], F32))
        cm_sb = e(nc.sbuf_tensor("cm_sb", [16, 1024], F32))
        wo_st = e(nc.sbuf_tensor("wo_st", [128, 1024], F32))
        wo_bf = e(nc.sbuf_tensor("wo_bf", [128, 8192], BF16))
        bo_sb = e(nc.sbuf_tensor("bo_sb", [128, 8], F32))
        ctx = _mkctx(S, e, nc, T)
        p_y = e(nc.psum_tensor("p_y", [128, T], F32))[:]
        p_o = e(nc.psum_tensor("p_o", [128, T], F32))[:]

        S.op("sync", lambda en: en.dma_start(out=hs_sb[:], in_=hs_in), inc="dw", dma=True)
        S.op("sync", lambda en: en.dma_start(out=cm_sb[:], in_=cmat), inc="dw", dma=True)
        S.op("sync", lambda en: en.dma_start(out=bo_sb[:], in_=bo), inc="dw", dma=True)
        for k in range(8):
            if k > 0:
                S.wait("sync", "sv_wo", k)
            dk = S.op("sync", lambda en, w=wot[k * 128:(k + 1) * 128, :]:
                      en.dma_start(out=wo_st[:], in_=w), inc="dwo", dma=True)
            S.wait("vector", "dwo", dk)
            S.op("vector", lambda en, o=wo_bf[:, k * 1024:(k + 1) * 1024]:
                 en.tensor_copy(o, wo_st[:]), inc="sv_wo")

        S.wait("tensor", "sg", 2)
        S.wait("tensor", "dw", 48)
        for u in range(NU):
            usl = slice(u * T, (u + 1) * T)
            if u > 0:
                S.wait("sync", "sv_out", u)
            dl = S.op("sync", lambda en, s=usl: en.dma_start(
                out=xc[:].rearrange("p (c s) -> p c s", s=T),
                in_=xTown[:, s].rearrange("(c p) s -> p c s", p=128)), inc="dl", dma=True)
            for m in range(8):
                idx = u * 8 + m
                if idx > 0:
                    S.wait("tensor", "ss_y", idx)
                S.op("tensor", lambda en, w=cm_sb[:, m * 128:(m + 1) * 128], r=hs_sb[:, usl]:
                     en.matmul(p_y, w, r, start=True, stop=True), inc="sm_y")
                S.wait("scalar", "sm_y", idx + 1)
                if u > 0 and m == 0:
                    S.wait("scalar", "sm_o", 8 * u)
                S.op("scalar", lambda en, o=y_bf[:, m * T:(m + 1) * T]:
                     en.copy(o, p_y), inc="ss_y")
            S.wait("tensor", "ss_y", u * 8 + 8)
            if u == 0:
                S.wait("tensor", "sv_wo", 8)
            for m2 in range(8):
                idx2 = u * 8 + m2
                if idx2 > 0:
                    S.wait("tensor", "ss_o", idx2)
                for kc in range(8):
                    S.op("tensor", lambda en,
                         w=wo_bf[:, kc * 1024 + m2 * 128: kc * 1024 + m2 * 128 + 128],
                         r=y_bf[:, kc * T:(kc + 1) * T], c0=(kc == 0), c1=(kc == 7):
                         en.matmul(p_o, w, r, start=c0, stop=c1),
                         inc="sm_o" if kc == 7 else None)
                S.wait("scalar", "sm_o", idx2 + 1)
                if u > 0 and m2 == 0:
                    S.wait("scalar", "sv_ao", u)
                S.op("scalar", lambda en, o=ao[:, m2 * T:(m2 + 1) * T], b=bo_sb[:, m2:m2 + 1]:
                     en.activation(o, p_o, AF.Identity, bias=b, scale=1.0), inc="ss_o")
            S.edge("scalar")
            if u > 0:
                S.wait("scalar", "st", 16 * u)
            smbc = emit_ln_stats(S, ctx, ao[:], chunks8(ao[:]), sq[:], chunks8(sq[:]), "ln2",
                                 bcfree=("svbc2", u) if u > 0 else None,
                                 ready=[("ss_o", u * 8 + 8)])
            S.wait("vector", "sm_bc", smbc)
            S.wait("vector", "dl", dl)
            sqc, aoc, xcc = chunks8(sq[:]), chunks8(ao[:]), chunks8(xc[:])
            for c in range(8):
                S.op("vector", lambda en, a=sqc[c], b=aoc[c]:
                     en.tensor_tensor(a, b, ctx["p_bcA"], ALU.mult),
                     inc="sv_ao" if c == 7 else None)
            S.edge("vector")
            for c in range(8):
                S.op("vector", lambda en, a=sqc[c]:
                     en.tensor_tensor(a, a, ctx["p_bcB"], ALU.add),
                     inc="svbc2" if c == 7 else None)
            S.edge("vector")
            for c in range(8):
                S.op("vector", lambda en, a=sqc[c], b=xcc[c]:
                     en.tensor_tensor(a, a, b, ALU.add),
                     inc="sv_out" if c == 7 else None)
            S.wait("gpsimd", "sv_out", u + 1)
            S.op("gpsimd", lambda en, s=usl: en.dma_start(
                out=x1[:, s].rearrange("(c p) s -> p c s", p=128),
                in_=sq[:].rearrange("p (c s) -> p c s", s=T)), inc="st", dma=True)
        S.wait("gpsimd", "st", 16 * NU)

        hmap = {}
        for nm in S.sem_names:
            hmap[nm] = e(nc.semaphore(nm))
        block = e(nc.Block())
        S.emit(nc, block, hmap)
    return nc


def build_B():
    nc = bass.Bass("TRN2", target_bir_lowering=False)
    x1T = nc.dram_tensor("x1T", [1024, 4096], F32, kind="ExternalInput").ap()
    pcol = nc.dram_tensor("pcol", [128, 8], F32, kind="ExternalInput").ap()
    wrT = nc.dram_tensor("wrT", [1024, 1024], F32, kind="ExternalInput").ap()
    wkT = nc.dram_tensor("wkT", [1024, 1024], F32, kind="ExternalInput").ap()
    wvT = nc.dram_tensor("wvT", [1024, 1024], F32, kind="ExternalInput").ap()
    mur = nc.dram_tensor("mur", [128, 8], F32, kind="ExternalInput").ap()
    muk = nc.dram_tensor("muk", [128, 8], F32, kind="ExternalInput").ap()
    x2T = nc.dram_tensor("x2T", [1024, 4096], F32, kind="ExternalOutput").ap()
    T, NU = 512, 8

    with ExitStack() as st_:
        e = st_.enter_context
        S = Sched()
        xc = e(nc.sbuf_tensor("xc", [128, 4096], F32))
        sq = e(nc.sbuf_tensor("sq", [128, 4096], F32))
        dbuf = e(nc.sbuf_tensor("dbuf", [128, 4096], F32))
        vout = e(nc.sbuf_tensor("vout", [128, 4096], F32))
        r_sb = e(nc.sbuf_tensor("r_sb", [128, 4096], F32))
        xn_pad = e(nc.sbuf_tensor("xn_pad", [128, 8 * 513], F32))
        xr_bf = e(nc.sbuf_tensor("xr_bf", [128, 4096], BF16))
        xk_bf = e(nc.sbuf_tensor("xk_bf", [128, 4096], BF16))
        ka_bf = e(nc.sbuf_tensor("ka_bf", [128, 4096], BF16))
        w_st = e(nc.sbuf_tensor("w_st", [128, 1024], F32))
        wr_bf = e(nc.sbuf_tensor("wr_bf", [128, 8192], BF16))
        wk_bf = e(nc.sbuf_tensor("wk_bf", [128, 8192], BF16))
        wv_bf = e(nc.sbuf_tensor("wv_bf", [128, 8192], BF16))
        mur_sb = e(nc.sbuf_tensor("mur_sb", [128, 8], F32))
        muk_sb = e(nc.sbuf_tensor("muk_sb", [128, 8], F32))
        pc_sb = e(nc.sbuf_tensor("pc_sb", [128, 8], F32))
        pcsq = e(nc.sbuf_tensor("pcsq", [128, 8], F32))
        xnprev0 = e(nc.sbuf_tensor("xnprev0", [128, 8], F32))
        prow = e(nc.sbuf_tensor("prow", [1, 32], F32))
        bc2 = e(nc.sbuf_tensor("bc2", [128, 2], F32))
        ctx = _mkctx(S, e, nc, T)
        p_m = e(nc.psum_tensor("p_m", [128, T], F32))[:]
        p_r1 = e(nc.psum_tensor("p_r1", [1, 8], F32))[:]
        p_b1 = e(nc.psum_tensor("p_b1", [128, 1], F32))[:]
        p_b2 = e(nc.psum_tensor("p_b2", [128, 1], F32))[:]

        for nm, dst in (("mur", mur_sb), ("muk", muk_sb), ("pcol", pc_sb)):
            src = {"mur": mur, "muk": muk, "pcol": pcol}[nm]
            S.op("sync", lambda en, o=dst[:], i=src: en.dma_start(out=o, in_=i),
                 inc="dw", dma=True)
        for wi, (wdram, wbf) in enumerate(((wrT, wr_bf), (wkT, wk_bf), (wvT, wv_bf))):
            for k in range(8):
                idx = wi * 8 + k
                if idx > 0:
                    S.wait("sync", "sv_wo", idx)
                dk = S.op("sync", lambda en, w=wdram[k * 128:(k + 1) * 128, :]:
                          en.dma_start(out=w_st[:], in_=w), inc="dwo", dma=True)
                S.wait("vector", "dwo", dk)
                S.op("vector", lambda en, o=wbf[:, k * 1024:(k + 1) * 1024]:
                     en.tensor_copy(o, w_st[:]), inc="sv_wo")

        R = ctx["rows"]
        S.wait("tensor", "sg", 2)
        S.wait("tensor", "dw", 48)
        S.op("tensor", lambda en: en.matmul(p_r1, ctx["ones_col"], pc_sb[:], start=True, stop=True),
             inc="smp")
        S.wait("scalar", "dw", 48)
        sqp = S.op("scalar", lambda en: en.activation(pcsq[:], pc_sb[:], AF.Square), inc="ssp")
        S.wait("vector", "smp", 1)
        S.op("vector", lambda en: en.reduce_sum(prow[0:1, 0:1], p_r1, axis=AX.X), inc="svp")
        S.wait("tensor", "ssp", sqp)
        S.wait("tensor", "svp", 1)
        S.op("tensor", lambda en: en.matmul(p_r1, ctx["ones_col"], pcsq[:], start=True, stop=True),
             inc="smp")
        S.wait("vector", "smp", 2)
        S.op("vector", lambda en: en.reduce_sum(prow[0:1, 1:2], p_r1, axis=AX.X))
        S.edge("vector")
        S.op("vector", lambda en: en.tensor_scalar_mul(prow[0:1, 2:3], prow[0:1, 0:1], 1.0 / 1024.0))
        S.edge("vector")
        S.op("vector", lambda en: en.tensor_mul(prow[0:1, 3:4], prow[0:1, 2:3], prow[0:1, 2:3]))
        S.op("vector", lambda en: en.tensor_scalar_mul(prow[0:1, 4:5], prow[0:1, 1:2], 1.0 / 1024.0))
        S.edge("vector")
        S.op("vector", lambda en: en.tensor_sub(prow[0:1, 5:6], prow[0:1, 4:5], prow[0:1, 3:4]))
        S.edge("vector")
        svp2 = S.op("vector", lambda en: en.tensor_scalar_add(prow[0:1, 6:7], prow[0:1, 5:6], 1e-5),
                    inc="svp")
        S.wait("scalar", "svp", svp2)
        ssp2 = S.op("scalar", lambda en: en.activation(prow[0:1, 7:8], prow[0:1, 6:7], AF.Sqrt),
                    inc="ssp")
        S.wait("vector", "ssp", ssp2)
        S.op("vector", lambda en: en.reciprocal(prow[0:1, 8:9], prow[0:1, 7:8]))
        S.edge("vector")
        svp3 = S.op("vector", lambda en: en.scalar_tensor_tensor(
            prow[0:1, 9:10], prow[0:1, 2:3], -1.0, prow[0:1, 8:9], ALU.mult, ALU.mult),
            inc="svp")
        S.wait("tensor", "svp", svp3)
        S.op("tensor", lambda en: en.matmul(p_b1, ctx["ones_row"], prow[0:1, 8:9], start=True, stop=True))
        S.op("tensor", lambda en: en.matmul(p_b2, ctx["ones_row"], prow[0:1, 9:10], start=True, stop=True),
             inc="smp")
        S.wait("scalar", "smp", 3)
        S.op("scalar", lambda en: en.copy(bc2[:, 0:1], p_b1))
        S.op("scalar", lambda en: en.copy(bc2[:, 1:2], p_b2), inc="ssp")
        S.edge("scalar")
        S.wait("vector", "ssp", S.cnt["ssp"])
        svp4 = S.op("vector", lambda en: en.tensor_scalar(
            xnprev0[:], pc_sb[:], bc2[:, 0:1], bc2[:, 1:2], ALU.mult, ALU.add), inc="svp")

        xpc3 = xn_pad[:].rearrange("p (c s) -> p c s", s=513)
        xn_ch = [xn_pad[:, c * 513 + 1: c * 513 + 513] for c in range(8)]
        xsh_ch = [xn_pad[:, c * 513 + 0: c * 513 + 512] for c in range(8)]
        for u in range(NU):
            usl = slice(u * T, (u + 1) * T)
            if u > 0:
                S.wait("sync", "sv_out", u)
            dl = S.op("sync", lambda en, s=usl: en.dma_start(
                out=xc[:].rearrange("p (c s) -> p c s", s=T),
                in_=x1T[:, s].rearrange("(c p) s -> p c s", p=128)), inc="dl", dma=True)
            S.wait("scalar", "dl", dl)
            S.wait("tensor", "dl", dl)
            if u > 0:
                S.wait("scalar", "st", 16 * u)
            smbc = emit_ln_stats(S, ctx, xc[:], chunks8(xc[:]), sq[:], chunks8(sq[:]), "lnb",
                                 bcfree=("sv_xn", 2 * u) if u > 0 else None)
            if u == 0:
                S.wait("vector", "svp", svp4)
                S.op("vector", lambda en: en.tensor_copy(
                    xpc3[:, :, 0:1], xnprev0[:].rearrange("p (c o) -> p c o", o=1)))
            else:
                S.op("vector", lambda en: en.tensor_copy(xpc3[:, :, 0:1], xpc3[:, :, 512:513]))
            S.edge("vector")
            S.wait("vector", "sm_bc", smbc)
            xcc = chunks8(xc[:])
            for c in range(8):
                S.op("vector", lambda en, a=xn_ch[c], b=xcc[c]:
                     en.tensor_tensor(a, b, ctx["p_bcA"], ALU.mult))
            S.edge("vector")
            for c in range(8):
                S.op("vector", lambda en, a=xn_ch[c]:
                     en.tensor_tensor(a, a, ctx["p_bcB"], ALU.add),
                     inc="sv_xn" if c == 7 else None)
            S.edge("vector")
            dch = chunks8(dbuf[:])
            xrc, xkc, kac = chunks8(xr_bf[:]), chunks8(xk_bf[:]), chunks8(ka_bf[:])
            if u > 0:
                S.wait("vector", "st", 16 * u)
            for c in range(8):
                S.op("vector", lambda en, a=dch[c], n=xn_ch[c], s=xsh_ch[c]:
                     en.tensor_sub(a, n, s))
            S.edge("vector")
            for c in range(8):
                S.op("vector", lambda en, a=xrc[c], dd=dch[c], s=xsh_ch[c], m=mur_sb[:, c:c + 1]:
                     en.scalar_tensor_tensor(a, dd, m, s, ALU.mult, ALU.add))
            for c in range(8):
                S.op("vector", lambda en, a=xkc[c], dd=dch[c], s=xsh_ch[c], m=muk_sb[:, c:c + 1]:
                     en.scalar_tensor_tensor(a, dd, m, s, ALU.mult, ALU.add),
                     inc="sv_xk" if c == 7 else None)
            S.wait("tensor", "sv_xk", u + 1)
            if u == 0:
                S.wait("tensor", "sv_wo", 24)
            rch = chunks8(r_sb[:])
            if u > 0:
                S.wait("tensor", "sv_vo", 8 * u)
            for m in range(8):
                idx = u * 16 + m
                if idx > 0:
                    S.wait("tensor", "ss_e", idx)
                for kc in range(8):
                    S.op("tensor", lambda en,
                         w=wr_bf[:, kc * 1024 + m * 128: kc * 1024 + m * 128 + 128],
                         r=xrc[kc], c0=(kc == 0), c1=(kc == 7):
                         en.matmul(p_m, w, r, start=c0, stop=c1),
                         inc="sm_e" if kc == 7 else None)
                S.wait("scalar", "sm_e", idx + 1)
                if u > 0 and m == 0:
                    S.wait("scalar", "sv_vo", 8 * u)
                S.op("scalar", lambda en, o=rch[m]:
                     en.activation(o, p_m, AF.Sigmoid), inc="ss_e")
            for m in range(8):
                idx = u * 16 + 8 + m
                S.wait("tensor", "ss_e", idx)
                for kc in range(8):
                    S.op("tensor", lambda en,
                         w=wk_bf[:, kc * 1024 + m * 128: kc * 1024 + m * 128 + 128],
                         r=xkc[kc], c0=(kc == 0), c1=(kc == 7):
                         en.matmul(p_m, w, r, start=c0, stop=c1),
                         inc="sm_e" if kc == 7 else None)
                S.wait("scalar", "sm_e", idx + 1)
                S.op("scalar", lambda en, o=dch[m]: en.activation(o, p_m, AF.Relu))
                S.edge("scalar")
                S.op("scalar", lambda en, o=kac[m], i=dch[m]:
                     en.activation(o, i, AF.Square), inc="ss_e")
            vch = chunks8(vout[:])
            S.wait("tensor", "ss_e", u * 16 + 16)
            for m in range(8):
                idx = u * 8 + m
                if idx > 0:
                    S.wait("tensor", "sv_vo", idx)
                for kc in range(8):
                    S.op("tensor", lambda en,
                         w=wv_bf[:, kc * 1024 + m * 128: kc * 1024 + m * 128 + 128],
                         r=kac[kc], c0=(kc == 0), c1=(kc == 7):
                         en.matmul(p_m, w, r, start=c0, stop=c1),
                         inc="sm_v" if kc == 7 else None)
                S.wait("vector", "sm_v", idx + 1)
                S.op("vector", lambda en, o=vch[m], r=rch[m]:
                     en.tensor_tensor(o, p_m, r, ALU.mult), inc="sv_vo")
            S.wait("scalar", "sv_vo", u * 8 + 8)
            smbc2 = emit_ln_stats(S, ctx, vout[:], chunks8(vout[:]), sq[:], chunks8(sq[:]), "lnb",
                                  bcfree=("sv_xn", 2 * u + 1),
                                  ready=[("sv_vo", u * 8 + 8)])
            S.wait("vector", "sm_bc", smbc2)
            for c in range(8):
                S.op("vector", lambda en, a=dch[c], b=vch[c]:
                     en.tensor_tensor(a, b, ctx["p_bcA"], ALU.mult))
            S.edge("vector")
            for c in range(8):
                S.op("vector", lambda en, a=dch[c]:
                     en.tensor_tensor(a, a, ctx["p_bcB"], ALU.add),
                     inc="sv_xn" if c == 7 else None)
            S.edge("vector")
            for c in range(8):
                S.op("vector", lambda en, a=dch[c], b=xcc[c]:
                     en.tensor_tensor(a, a, b, ALU.add),
                     inc="sv_out" if c == 7 else None)
            S.wait("gpsimd", "sv_out", u + 1)
            S.op("gpsimd", lambda en, s=usl: en.dma_start(
                out=x2T[:, s].rearrange("(c p) s -> p c s", p=128),
                in_=dbuf[:].rearrange("p (c s) -> p c s", s=T)), inc="st", dma=True)
        S.wait("gpsimd", "st", 16 * NU)

        hmap = {}
        for nm in S.sem_names:
            hmap[nm] = e(nc.semaphore(nm))
        block = e(nc.Block())
        S.emit(nc, block, hmap)
    return nc


def build_C1():
    nc = bass.Bass("TRN2", target_bir_lowering=False)
    x2T = nc.dram_tensor("x2T", [1024, 4096], F32, kind="ExternalInput").ap()
    wqkv = nc.dram_tensor("wqkv", [1024, 3072], F32, kind="ExternalInput").ap()
    gam = nc.dram_tensor("gam", [128, 8], F32, kind="ExternalInput").ap()
    bet = nc.dram_tensor("bet", [128, 8], F32, kind="ExternalInput").ap()
    xnO = nc.dram_tensor("xnO", [1024, 4096], BF16, kind="ExternalOutput").ap()
    qO = nc.dram_tensor("qO", [1024, 4096], BF16, kind="ExternalOutput").ap()
    kvO = nc.dram_tensor("kvO", [128, 520], F32, kind="ExternalOutput").ap()
    T, NU = 512, 8

    with ExitStack() as st_:
        e = st_.enter_context
        S = Sched()
        xc = e(nc.sbuf_tensor("xc", [128, 4096], F32))
        sq = e(nc.sbuf_tensor("sq", [128, 4096], F32))
        xin = e(nc.sbuf_tensor("xin", [128, 4096], F32))
        xn_bf = e(nc.sbuf_tensor("xn_bf", [128, 4096], BF16))
        xnn_bf = e(nc.sbuf_tensor("xnn_bf", [128, 4096], BF16))
        q_bf = e(nc.sbuf_tensor("q_bf", [128, 4096], BF16))
        kvtm = e(nc.sbuf_tensor("kvtm", [128, 2048], BF16))
        tmpe = e(nc.sbuf_tensor("tmpe", [128, 1536], F32))
        w_st = e(nc.sbuf_tensor("w_st", [128, 3072], F32))
        wq_bf = e(nc.sbuf_tensor("wq_bf", [128, 24576], BF16))
        gam_sb = e(nc.sbuf_tensor("gam_sb", [128, 8], F32))
        bet_sb = e(nc.sbuf_tensor("bet_sb", [128, 8], F32))
        kvp_sb = e(nc.sbuf_tensor("kvp_sb", [128, 520], F32))
        onesbf = e(nc.sbuf_tensor("onesbf", [128, 1], BF16))
        ctx = _mkctx(S, e, nc, T)
        p_t = e(nc.psum_tensor("p_t", [128, T], F32))[:]
        p_kvA = e(nc.psum_tensor("p_kvA", [128, 260], F32))[:]
        p_kvB = e(nc.psum_tensor("p_kvB", [128, 260], F32))[:]

        S.op("gpsimd", lambda en: en.memset(onesbf[:], 1.0), inc="sg")
        S.op("sync", lambda en: en.dma_start(out=gam_sb[:], in_=gam), inc="dw", dma=True)
        S.op("sync", lambda en: en.dma_start(out=bet_sb[:], in_=bet), inc="dw", dma=True)
        for k in range(8):
            if k > 0:
                S.wait("sync", "sv_wo", k)
            dk = S.op("sync", lambda en, w=wqkv[k * 128:(k + 1) * 128, :]:
                      en.dma_start(out=w_st[:], in_=w), inc="dwo", dma=True)
            S.wait("vector", "dwo", dk)
            S.op("vector", lambda en, o=wq_bf[:, k * 3072:(k + 1) * 3072]:
                 en.tensor_copy(o, w_st[:]), inc="sv_wo")

        S.wait("tensor", "sg", 3)
        S.wait("tensor", "dw", 32)
        xinc = chunks8(xin[:])
        xcc = chunks8(xc[:])
        sqc = chunks8(sq[:])
        for u in range(NU):
            usl = slice(u * T, (u + 1) * T)
            if u > 0:
                S.wait("sync", "sv_xcf", u)
            dl = S.op("sync", lambda en, s=usl: en.dma_start(
                out=xc[:].rearrange("p (c s) -> p c s", s=T),
                in_=x2T[:, s].rearrange("(c p) s -> p c s", p=128)), inc="dl", dma=True)
            S.wait("scalar", "dl", dl)
            S.wait("tensor", "dl", dl)
            if u > 0:
                S.wait("vector", "dgq", 32 * u)
            smbc = emit_ln_stats(S, ctx, xc[:], chunks8(xc[:]), sq[:], sqc, "lnc",
                                 bcfree=("svb2", 2 * u) if u > 0 else None)
            S.wait("vector", "sm_bc", smbc)
            for c in range(8):
                S.op("vector", lambda en, a=xinc[c], b=xcc[c]:
                     en.tensor_tensor(a, b, ctx["p_bcA"], ALU.mult),
                     inc="sv_xcf" if c == 7 else None)
            S.edge("vector")
            for c in range(8):
                S.op("vector", lambda en, a=xinc[c]:
                     en.tensor_tensor(a, a, ctx["p_bcB"], ALU.add),
                     inc="svb2" if c == 7 else None)
            S.edge("vector")
            svxn = S.op("vector", lambda en: en.tensor_copy(xn_bf[:], xin[:]), inc="sv_xnb")
            S.wait("gpsimd", "sv_xnb", svxn)
            S.op("gpsimd", lambda en, s=usl: en.dma_start(
                out=xnO[:, s].rearrange("(c p) s -> p c s", p=128),
                in_=xn_bf[:].rearrange("p (c s) -> p c s", s=T)), inc="dgq", dma=True)
            S.edge("scalar")
            smbc2 = emit_ln_stats(S, ctx, xin[:], xinc, sq[:], sqc, "lnc",
                                  bcfree=("svb2", 2 * u + 1),
                                  ready=[("svb2", 2 * u + 1)])
            S.wait("vector", "sm_bc", smbc2)
            for c in range(8):
                S.op("vector", lambda en, a=sqc[c], b=xinc[c]:
                     en.tensor_tensor(a, b, ctx["p_bcA"], ALU.mult))
            S.edge("vector")
            for c in range(8):
                S.op("vector", lambda en, a=sqc[c]:
                     en.tensor_tensor(a, a, ctx["p_bcB"], ALU.add),
                     inc="svb2" if c == 7 else None)
            S.edge("vector")
            xnnc = chunks8(xnn_bf[:])
            for c in range(8):
                S.op("vector", lambda en, a=xnnc[c], b=sqc[c], g=gam_sb[:, c:c + 1],
                     bb=bet_sb[:, c:c + 1]:
                     en.tensor_scalar(a, b, g, bb, ALU.mult, ALU.add),
                     inc="sv_xnn" if c == 7 else None)
            S.wait("tensor", "sv_xnn", u + 1)
            if u == 0:
                S.wait("tensor", "sv_wo", 8)
            qc = chunks8(q_bf[:])
            t1 = tmpe[:, 0:512]
            t2 = tmpe[:, 512:1024]
            t3 = tmpe[:, 1024:1536]
            for m in range(8):
                idx = u * 20 + m
                if u + m > 0:
                    S.wait("tensor", "sv_pt", S.cnt["sv_pt"])
                for kc in range(8):
                    S.op("tensor", lambda en,
                         w=wq_bf[:, kc * 3072 + m * 128: kc * 3072 + m * 128 + 128],
                         r=xnnc[kc], c0=(kc == 0), c1=(kc == 7):
                         en.matmul(p_t, w, r, start=c0, stop=c1),
                         inc="sm_t" if kc == 7 else None)
                smt = S.cnt["sm_t"]
                S.wait("vector", "sm_t", smt)
                S.op("vector", lambda en: en.tensor_scalar_min(t1, p_t, 0.0))
                S.op("vector", lambda en: en.tensor_scalar_max(t3, p_t, 0.0), inc="sv_e1")
                S.wait("scalar", "sv_e1", S.cnt["sv_e1"])
                S.op("scalar", lambda en: en.activation(t2, t1, AF.Exp), inc="ss_e1")
                S.wait("vector", "ss_e1", S.cnt["ss_e1"])
                S.op("vector", lambda en, o=qc[m]: en.tensor_add(o, t2, t3), inc="sv_pt")
            svq = S.cnt["sv_pt"]
            S.wait("gpsimd", "sv_pt", svq)
            S.op("gpsimd", lambda en, s=usl: en.dma_start(
                out=qO[:, s].rearrange("(c p) s -> p c s", p=128),
                in_=q_bf[:].rearrange("p (c s) -> p c s", s=T)), inc="dgq", dma=True)
            for tb in range(4):
                tok0 = tb * 128
                gb = u * 4 + tb
                for nb in range(4):
                    S.wait("tensor", "sv_pt", S.cnt["sv_pt"])
                    for kc in range(8):
                        S.op("tensor", lambda en,
                             w=xnnc[kc][:, tok0:tok0 + 128],
                             r=wq_bf[:, kc * 3072 + 1024 + nb * 512: kc * 3072 + 1024 + (nb + 1) * 512],
                             c0=(kc == 0), c1=(kc == 7):
                             en.matmul(p_t, w, r, start=c0, stop=c1),
                             inc="sm_t" if kc == 7 else None)
                    smt = S.cnt["sm_t"]
                    if nb < 2:
                        S.wait("vector", "sm_t", smt)
                        if gb > 0 and nb == 0:
                            S.wait("vector", "sm_kv", gb)
                        S.op("vector", lambda en: en.tensor_scalar_min(t1, p_t, 0.0))
                        S.op("vector", lambda en: en.tensor_scalar_max(t3, p_t, 0.0), inc="sv_e1")
                        S.wait("scalar", "sv_e1", S.cnt["sv_e1"])
                        S.op("scalar", lambda en: en.activation(t2, t1, AF.Exp), inc="ss_e1")
                        S.wait("vector", "ss_e1", S.cnt["ss_e1"])
                        S.op("vector", lambda en, o=kvtm[:, nb * 512:(nb + 1) * 512]:
                             en.tensor_add(o, t2, t3), inc="sv_pt")
                    else:
                        S.wait("scalar", "sm_t", smt)
                        if gb > 0 and nb == 2:
                            S.wait("scalar", "sm_kv", gb)
                        S.op("scalar", lambda en, o=kvtm[:, nb * 512:(nb + 1) * 512]:
                             en.copy(o, p_t), inc="sv_pt")
                S.wait("tensor", "sv_pt", S.cnt["sv_pt"])
                for hd in range(16):
                    pk = p_kvA if hd < 8 else p_kvB
                    base = (hd % 2) * 64
                    j2 = (hd // 2) % 4
                    S.op("tensor", lambda en, l=kvtm[:, hd * 64:(hd + 1) * 64],
                         r=kvtm[:, 1024 + hd * 64: 1024 + (hd + 1) * 64],
                         o=pk, b=base, j=j2, c0=(gb == 0), c1=(gb == 31):
                         en.matmul(o[b:b + 64, j * 65: j * 65 + 64], l, r, start=c0, stop=c1))
                    S.op("tensor", lambda en, l=kvtm[:, hd * 64:(hd + 1) * 64],
                         o=pk, b=base, j=j2, c0=(gb == 0), c1=(gb == 31):
                         en.matmul(o[b:b + 64, j * 65 + 64: j * 65 + 65], l, onesbf[:],
                                   start=c0, stop=c1),
                         inc="sm_kv" if hd == 15 else None)
        S.wait("scalar", "sm_kv", S.cnt["sm_kv"])
        S.op("scalar", lambda en: en.copy(kvp_sb[:, 0:260], p_kvA))
        S.op("scalar", lambda en: en.copy(kvp_sb[:, 260:520], p_kvB), inc="ss_f")
        S.wait("gpsimd", "ss_f", 1)
        S.op("gpsimd", lambda en: en.dma_start(out=kvO, in_=kvp_sb[:]), inc="dgq", dma=True)
        S.wait("gpsimd", "dgq", S.cnt["dgq"])

        hmap = {}
        for nm in S.sem_names:
            hmap[nm] = e(nc.semaphore(nm))
        block = e(nc.Block())
        S.emit(nc, block, hmap)
    return nc


def build_C2():
    nc = bass.Bass("TRN2", target_bir_lowering=False)
    x2T = nc.dram_tensor("x2T", [1024, 4096], F32, kind="ExternalInput").ap()
    xnI = nc.dram_tensor("xnI", [1024, 4096], BF16, kind="ExternalInput").ap()
    qI = nc.dram_tensor("qI", [1024, 4096], BF16, kind="ExternalInput").ap()
    kva = nc.dram_tensor("kva", [128, 520], F32, kind="ExternalInput").ap()
    kvb = nc.dram_tensor("kvb", [128, 520], F32, kind="ExternalInput").ap()
    wout = nc.dram_tensor("wout", [1024, 1024], F32, kind="ExternalInput").ap()
    x3T = nc.dram_tensor("x3T", [1024, 4096], F32, kind="ExternalOutput").ap()
    attO = nc.dram_tensor("attO", [1024, 4096], BF16, kind="ExternalOutput").ap()
    innO = nc.dram_tensor("innO", [1024, 4096], F32, kind="ExternalOutput").ap()
    T, NU = 512, 8

    with ExitStack() as st_:
        e = st_.enter_context
        S = Sched()
        xc = e(nc.sbuf_tensor("xc", [128, 4096], F32))
        sq = e(nc.sbuf_tensor("sq", [128, 4096], F32))
        ao = e(nc.sbuf_tensor("ao", [128, 4096], F32))
        qt = e(nc.sbuf_tensor("qt", [128, 4096], BF16))
        xnt = e(nc.sbuf_tensor("xnt", [128, 4096], BF16))
        att_bf = e(nc.sbuf_tensor("att_bf", [128, 4096], BF16))
        kv2 = e(nc.sbuf_tensor("kv2", [128, 1040], F32))
        kvf = e(nc.sbuf_tensor("kvf", [128, 520], F32))
        kv_bf = e(nc.sbuf_tensor("kv_bf", [128, 520], BF16))
        w_st = e(nc.sbuf_tensor("w_st", [128, 1024], F32))
        wo_bf = e(nc.sbuf_tensor("wo_bf", [128, 8192], BF16))
        dnbuf = e(nc.sbuf_tensor("dnbuf", [1, 2048], F32))
        ctx = _mkctx(S, e, nc, T)
        p_at = e(nc.psum_tensor("p_at", [128, T], F32))[:]
        p_dn = e(nc.psum_tensor("p_dn", [1, T], F32))[:]
        p_bc2 = e(nc.psum_tensor("p_bc2", [128, T], F32))[:]
        p_o = e(nc.psum_tensor("p_o", [128, T], F32))[:]

        S.op("sync", lambda en: en.dma_start(out=kv2[:, 0:520], in_=kva), inc="dw", dma=True)
        S.op("sync", lambda en: en.dma_start(out=kv2[:, 520:1040], in_=kvb), inc="dw", dma=True)
        S.wait("vector", "dw", 32)
        S.op("vector", lambda en: en.tensor_add(kvf[:], kv2[:, 0:520], kv2[:, 520:1040]))
        S.edge("vector")
        S.op("vector", lambda en: en.tensor_copy(kv_bf[:], kvf[:]), inc="sv_kv")
        for k in range(8):
            if k > 0:
                S.wait("sync", "sv_wo", k)
            dk = S.op("sync", lambda en, w=wout[k * 128:(k + 1) * 128, :]:
                      en.dma_start(out=w_st[:], in_=w), inc="dwo", dma=True)
            S.wait("vector", "dwo", dk)
            S.op("vector", lambda en, o=wo_bf[:, k * 1024:(k + 1) * 1024]:
                 en.tensor_copy(o, w_st[:]), inc="sv_wo")

        S.wait("tensor", "sg", 2)
        S.wait("tensor", "sv_kv", 1)
        aoc = chunks8(ao[:])
        xcc = chunks8(xc[:])
        sqc = chunks8(sq[:])
        attc = chunks8(att_bf[:])
        xntc = chunks8(xnt[:])
        for u in range(NU):
            usl = slice(u * T, (u + 1) * T)
            if u > 0:
                S.wait("sync", "sv_out", u)
                S.wait("sync", "ss_at", 8 * u)
                S.wait("sync", "sv_wo2", 8 * u)
            dl = S.op("sync", lambda en, s=usl: en.dma_start(
                out=xc[:].rearrange("p (c s) -> p c s", s=T),
                in_=x2T[:, s].rearrange("(c p) s -> p c s", p=128)), inc="dl", dma=True)
            S.op("sync", lambda en, s=usl: en.dma_start(
                out=qt[:].rearrange("p (c s) -> p c s", s=T),
                in_=qI[:, s].rearrange("(c p) s -> p c s", p=128)), inc="dl", dma=True)
            dl = S.op("sync", lambda en, s=usl: en.dma_start(
                out=xnt[:].rearrange("p (c s) -> p c s", s=T),
                in_=xnI[:, s].rearrange("(c p) s -> p c s", p=128)), inc="dl", dma=True)
            S.wait("tensor", "dl", dl)
            for j in range(8):
                jdx = u * 8 + j
                if jdx > 0:
                    S.wait("tensor", "ss_at", jdx)
                if u > 0 and j == 0:
                    S.wait("scalar", "dgx", 32 * u)
                for par in range(2):
                    hd = 2 * j + par
                    base = (hd % 2) * 64
                    cb = (hd // 2) * 65 if hd < 8 else 260 + (hd // 2 - 4) * 65
                    S.op("tensor", lambda en, l=kv_bf[base:base + 64, cb:cb + 64],
                         r=qt[base:base + 64, j * T:(j + 1) * T], b=base:
                         en.matmul(p_at[b:b + 64, :], l, r, start=True, stop=True),
                         inc="sm_at" if par == 1 else None)
                    if S.cnt.get("ss_dn", 0) > 0:
                        S.wait("tensor", "ss_dn", S.cnt["ss_dn"])
                    S.op("tensor", lambda en, l=kv_bf[base:base + 64, cb + 64:cb + 65],
                         r=qt[base:base + 64, j * T:(j + 1) * T]:
                         en.matmul(p_dn, l, r, start=True, stop=True), inc="sm_dn")
                    S.wait("scalar", "sm_dn", S.cnt["sm_dn"])
                    if par == 0 and jdx > 0:
                        S.wait("scalar", "sm_bc2", jdx)
                    S.op("scalar", lambda en, o=dnbuf[0:1, par * T:(par + 1) * T]:
                         en.copy(o, p_dn), inc="ss_dn")
                S.wait("vector", "ss_dn", S.cnt["ss_dn"])
                S.op("vector", lambda en: en.tensor_scalar_add(
                    dnbuf[0:1, 1024:2048], dnbuf[0:1, 0:1024], 1e-6))
                S.edge("vector")
                svr = S.op("vector", lambda en: en.reciprocal(
                    dnbuf[0:1, 0:1024], dnbuf[0:1, 1024:2048]), inc="sv_dn")
                S.wait("tensor", "sv_dn", svr)
                if jdx > 0:
                    S.wait("tensor", "sv_ab", jdx)
                for par in range(2):
                    S.op("tensor", lambda en, r=dnbuf[0:1, par * T:(par + 1) * T], b=par * 64:
                         en.matmul(p_bc2[b:b + 64, :], ctx["ones_row"][0:1, 0:64], r,
                                   start=True, stop=True),
                         inc="sm_bc2" if par == 1 else None)
                S.wait("scalar", "sm_at", S.cnt["sm_at"])
                S.op("scalar", lambda en, o=aoc[j]: en.copy(o, p_at), inc="ss_at")
                S.wait("vector", "ss_at", S.cnt["ss_at"])
                S.wait("vector", "sm_bc2", S.cnt["sm_bc2"])
                if u > 0 and j == 0:
                    S.wait("vector", "dgx", 32 * u)
                S.op("vector", lambda en, o=attc[j], a=aoc[j]:
                     en.tensor_tensor(o, a, p_bc2, ALU.mult), inc="sv_ab")
            S.wait("gpsimd", "sv_ab", 8 * (u + 1))
            S.op("gpsimd", lambda en, s=usl: en.dma_start(
                out=attO[:, s].rearrange("(c p) s -> p c s", p=128),
                in_=att_bf[:].rearrange("p (c s) -> p c s", s=T)), inc="dgx", dma=True)
            S.wait("tensor", "sv_ab", 8 * (u + 1))
            if u == 0:
                S.wait("tensor", "sv_wo", 8)
            for m in range(8):
                idx = u * 8 + m
                if idx > 0:
                    S.wait("tensor", "sv_wo2", idx)
                for kc in range(8):
                    S.op("tensor", lambda en,
                         w=wo_bf[:, kc * 1024 + m * 128: kc * 1024 + m * 128 + 128],
                         r=attc[kc], c0=(kc == 0), c1=(kc == 7):
                         en.matmul(p_o, w, r, start=c0, stop=c1),
                         inc="sm_o" if kc == 7 else None)
                S.wait("vector", "sm_o", S.cnt["sm_o"])
                S.op("vector", lambda en, o=aoc[m], xn_=xntc[m]:
                     en.tensor_tensor(o, p_o, xn_, ALU.add), inc="sv_wo2")
            S.wait("gpsimd", "sv_wo2", 8 * (u + 1))
            S.op("gpsimd", lambda en, s=usl: en.dma_start(
                out=innO[:, s].rearrange("(c p) s -> p c s", p=128),
                in_=ao[:].rearrange("p (c s) -> p c s", s=T)), inc="dgx", dma=True)
            S.edge("vector")
            if u > 0:
                S.wait("scalar", "st", 16 * u)
            smbc = emit_ln_stats(S, ctx, ao[:], aoc, sq[:], sqc, "lnc2",
                                 bcfree=("svb2", u) if u > 0 else None,
                                 ready=[("sv_wo2", 8 * (u + 1))])
            S.wait("vector", "sm_bc", smbc)
            S.wait("vector", "dl", dl)
            for c in range(8):
                S.op("vector", lambda en, a=sqc[c], b=aoc[c]:
                     en.tensor_tensor(a, b, ctx["p_bcA"], ALU.mult))
            S.edge("vector")
            for c in range(8):
                S.op("vector", lambda en, a=sqc[c]:
                     en.tensor_tensor(a, a, ctx["p_bcB"], ALU.add),
                     inc="svb2" if c == 7 else None)
            S.edge("vector")
            for c in range(8):
                S.op("vector", lambda en, a=sqc[c], b=xcc[c]:
                     en.tensor_tensor(a, a, b, ALU.add),
                     inc="sv_out" if c == 7 else None)
            S.wait("gpsimd", "sv_out", u + 1)
            S.op("gpsimd", lambda en, s=usl: en.dma_start(
                out=x3T[:, s].rearrange("(c p) s -> p c s", p=128),
                in_=sq[:].rearrange("p (c s) -> p c s", s=T)), inc="st", dma=True)
        S.wait("gpsimd", "st", 16 * NU)

        hmap = {}
        for nm in S.sem_names:
            hmap[nm] = e(nc.semaphore(nm))
        block = e(nc.Block())
        S.emit(nc, block, hmap)
    return nc


def build_D():
    nc = bass.Bass("TRN2", target_bir_lowering=False)
    x3T = nc.dram_tensor("x3T", [1024, 4096], F32, kind="ExternalInput").ap()
    gwT = nc.dram_tensor("gwT", [1024, 8], F32, kind="ExternalInput").ap()
    ebias = nc.dram_tensor("ebias", [8, 1], F32, kind="ExternalInput").ap()
    wupT = nc.dram_tensor("wupT", [8, 1024, 1408], F32, kind="ExternalInput").ap()
    wdnT = nc.dram_tensor("wdnT", [8, 1408, 1024], F32, kind="ExternalInput").ap()
    swuT = nc.dram_tensor("swuT", [1024, 1408], F32, kind="ExternalInput").ap()
    swdT = nc.dram_tensor("swdT", [1408, 1024], F32, kind="ExternalInput").ap()
    i8d = nc.dram_tensor("i8d", [8, 8], F32, kind="ExternalInput").ap()
    i128d = nc.dram_tensor("i128d", [128, 128], F32, kind="ExternalInput").ap()
    x4T = nc.dram_tensor("x4T", [1024, 4096], F32, kind="ExternalOutput").ap()
    T, NV, TV = 512, 4, 1024

    with ExitStack() as st_:
        e = st_.enter_context
        S = Sched()
        xc = e(nc.sbuf_tensor("xc", [128, 4096], F32))
        sq = e(nc.sbuf_tensor("sq", [128, 4096], F32))
        xnf = e(nc.sbuf_tensor("xnf", [128, 4096], F32))
        xn_bf = e(nc.sbuf_tensor("xn_bf", [128, 8192], BF16))
        z_bf = e(nc.sbuf_tensor("z_bf", [128, 4096], BF16))
        h_bf = e(nc.sbuf_tensor("h_bf", [128, 5632], BF16))
        macc = e(nc.sbuf_tensor("macc", [128, 8192], F32))
        w_st = e(nc.sbuf_tensor("w_st", [128, 1408], F32))
        wup_bf = e(nc.sbuf_tensor("wup_bf", [128, 11264], BF16))
        wdn_bf = e(nc.sbuf_tensor("wdn_bf", [128, 11264], BF16))
        gw_sb = e(nc.sbuf_tensor("gw_sb", [128, 64], F32))
        eb_sb = e(nc.sbuf_tensor("eb_sb", [8, 1], F32))
        i8_sb = e(nc.sbuf_tensor("i8_sb", [8, 8], F32))
        i128_sb = e(nc.sbuf_tensor("i128_sb", [128, 128], F32))
        scr_fm = e(nc.sbuf_tensor("scr_fm", [8, 512], F32))
        bia_fm = e(nc.sbuf_tensor("bia_fm", [8, 512], F32))
        tm_sb = e(nc.sbuf_tensor("tm_sb", [128, 64], F32))
        msk_sb = e(nc.sbuf_tensor("msk_sb", [128, 80], F32))
        gsc_row = e(nc.sbuf_tensor("gsc_row", [1, 8192], BF16))
        onesbf = e(nc.sbuf_tensor("onesbf", [1, 128], BF16))
        w_st2 = e(nc.sbuf_tensor("w_st2", [128, 1024], F32))
        ctx = _mkctx(S, e, nc, T)
        p_g = e(nc.psum_tensor("p_g", [128, T], F32))[:]
        p_r = e(nc.psum_tensor("p_r", [128, T], F32))[:]
        p_u = e(nc.psum_tensor("p_u", [128, T], F32))[:]
        p_d = e(nc.psum_tensor("p_d", [128, T], F32))[:]
        p_u2, p_d2 = p_r, p_g

        S.op("gpsimd", lambda en: en.memset(onesbf[:], 1.0), inc="sg")
        S.op("sync", lambda en: en.dma_start(
            out=gw_sb[:].rearrange("p (c n) -> p c n", n=8),
            in_=gwT.rearrange("(c p) n -> p c n", p=128)), inc="dw", dma=True)
        S.op("sync", lambda en: en.dma_start(out=eb_sb[:], in_=ebias), inc="dw", dma=True)
        S.op("sync", lambda en: en.dma_start(out=i8_sb[:], in_=i8d), inc="dw", dma=True)
        S.op("sync", lambda en: en.dma_start(out=i128_sb[:], in_=i128d), inc="dw", dma=True)

        stslots = [w_st, w_st2]
        slot_free = {}
        slot_i = [0]
        bankfree = {}

        def stage(src, dst_cast, width):
            s = slot_i[0] % 2 if width <= 1024 else 0
            slot_i[0] += 1
            st = stslots[s]
            if slot_free.get(s, 0) > 0:
                S.wait("sync", "sv_wc", slot_free[s])
            dk = S.op("sync", lambda en, w=src, o=st[:, 0:width]:
                      en.dma_start(out=o, in_=w), inc="dwo", dma=True)
            S.wait("vector", "dwo", dk)
            S.op("vector", lambda en, o=dst_cast, i_=st[:, 0:width]:
                 en.tensor_copy(o, i_), inc="sv_wc")
            slot_free[s] = S.cnt["sv_wc"]

        def load_expert(g):
            if S.cnt.get("sm_u", 0) > 0:
                S.wait("vector", "sm_u", S.cnt["sm_u"])
            for k in range(8):
                src = swuT[k * 128:(k + 1) * 128, :] if g < 0 else wupT[g, k * 128:(k + 1) * 128, :]
                stage(src, wup_bf[:, k * 1408:(k + 1) * 1408], 1408)
            upready = S.cnt["sv_wc"]
            if S.cnt.get("sm_d", 0) > 0:
                S.wait("vector", "sm_d", S.cnt["sm_d"])
            for k in range(11):
                src = swdT[k * 128:(k + 1) * 128, :] if g < 0 else wdnT[g, k * 128:(k + 1) * 128, :]
                stage(src, wdn_bf[:, k * 1024:(k + 1) * 1024], 1024)
            return upready, S.cnt["sv_wc"]

        S.wait("tensor", "sg", 3)
        S.wait("tensor", "dw", 64)
        xnfc = chunks8(xnf[:])
        xcc = chunks8(xc[:])
        sqc = chunks8(sq[:])
        for v in range(NV):
            for w in range(2):
                g0 = v * TV + w * T
                usl = slice(g0, g0 + T)
                if v + w > 0:
                    S.wait("sync", "sv_xnb", 2 * v + w)
                if S.cnt.get("sv_out", 0) > 0 and w == 0:
                    S.wait("sync", "sv_out", S.cnt["sv_out"])
                dl = S.op("sync", lambda en, s=usl: en.dma_start(
                    out=xc[:].rearrange("p (c s) -> p c s", s=T),
                    in_=x3T[:, s].rearrange("(c p) s -> p c s", p=128)), inc="dl", dma=True)
                S.wait("scalar", "dl", dl)
                S.wait("tensor", "dl", dl)
                if v + w == 0:
                    bcf = None
                elif w == 0:
                    bcf = ("sv_z", S.cnt["sv_z"])
                else:
                    bcf = ("sv_xn", S.cnt["sv_xn"])
                smbc = emit_ln_stats(S, ctx, xc[:], xcc, sq[:], sqc, "lnd", bcfree=bcf)
                S.wait("vector", "sm_bc", smbc)
                if S.cnt.get("st", 0) > 0 and w == 0:
                    S.wait("vector", "st", S.cnt["st"])
                for c in range(8):
                    S.op("vector", lambda en, a=xnfc[c], b=xcc[c]:
                         en.tensor_tensor(a, b, ctx["p_bcA"], ALU.mult))
                S.edge("vector")
                for c in range(8):
                    S.op("vector", lambda en, a=xnfc[c]:
                         en.tensor_tensor(a, a, ctx["p_bcB"], ALU.add),
                         inc="sv_xn" if c == 7 else None)
                S.edge("vector")
                xnb3 = xn_bf[:].rearrange("p (c s) -> p c s", s=TV)
                svb = S.op("vector", lambda en, o=xnb3[:, :, w * T:(w + 1) * T],
                           i=xnf[:].rearrange("p (c s) -> p c s", s=T):
                           en.tensor_copy(o, i), inc="sv_xnb")
                S.wait("tensor", "sv_xn", S.cnt["sv_xn"])
                if S.cnt.get("ss_g", 0) > 0:
                    S.wait("tensor", "ss_g", S.cnt["ss_g"])
                if S.cnt.get("sv_ac", 0) > 0:
                    S.wait("tensor", "sv_ac", S.cnt["sv_ac"])
                for c in range(8):
                    S.op("tensor", lambda en, wg=gw_sb[:, c * 8:(c + 1) * 8], r=xnfc[c],
                         c0=(c == 0), c1=(c == 7):
                         en.matmul(p_g[0:8, :], wg, r, start=c0, stop=c1),
                         inc="sm_g" if c == 7 else None)
                S.wait("scalar", "sm_g", S.cnt["sm_g"])
                S.op("scalar", lambda en: en.activation(scr_fm[:], p_g[0:8, :], AF.Sigmoid), inc="ss_g")
                S.wait("vector", "ss_g", S.cnt["ss_g"])
                S.op("vector", lambda en: en.tensor_scalar(
                    bia_fm[:], scr_fm[:], eb_sb[:], None, ALU.add), inc="sv_g")
                S.wait("tensor", "sv_g", S.cnt["sv_g"])
                if S.cnt.get("sv_tm", 0) > 0:
                    S.wait("tensor", "sv_tm", S.cnt["sv_tm"])
                if S.cnt.get("ss_r", 0) > 0:
                    S.wait("tensor", "ss_r", S.cnt["ss_r"])
                for bk in range(4):
                    S.op("tensor", lambda en, l=bia_fm[:, bk * 128:(bk + 1) * 128], b=bk:
                         en.matmul(p_r[:, b * 8:(b + 1) * 8], l, i8_sb[:], start=True, stop=True))
                for bk in range(4):
                    S.op("tensor", lambda en, l=scr_fm[:, bk * 128:(bk + 1) * 128], b=bk:
                         en.matmul(p_r[:, 32 + b * 8: 32 + (b + 1) * 8], l, i8_sb[:],
                                   start=True, stop=True),
                         inc="sm_tm" if bk == 3 else None)
                S.wait("vector", "sm_tm", S.cnt["sm_tm"])
                S.op("vector", lambda en: en.tensor_copy(tm_sb[:], p_r[:, 0:64]), inc="sv_tm")
                S.edge("vector")
                for bk in range(4):
                    S.op("vector", lambda en, b=bk: en.reduce_max(
                        msk_sb[:, b:b + 1], tm_sb[:, b * 8:(b + 1) * 8], axis=AX.X))
                S.edge("vector")
                for bk in range(4):
                    S.op("vector", lambda en, b=bk: en.tensor_scalar(
                        msk_sb[:, 8 + b * 8: 8 + (b + 1) * 8], tm_sb[:, b * 8:(b + 1) * 8],
                        msk_sb[:, b:b + 1], None, ALU.is_equal))
                S.edge("vector")
                for bk in range(4):
                    S.op("vector", lambda en, b=bk: en.scalar_tensor_tensor(
                        msk_sb[:, 40 + b * 8: 40 + (b + 1) * 8],
                        msk_sb[:, 8 + b * 8: 8 + (b + 1) * 8], -1e30,
                        tm_sb[:, b * 8:(b + 1) * 8], ALU.mult, ALU.add))
                S.edge("vector")
                for bk in range(4):
                    S.op("vector", lambda en, b=bk: en.reduce_max(
                        msk_sb[:, 4 + b:5 + b], msk_sb[:, 40 + b * 8: 40 + (b + 1) * 8], axis=AX.X))
                S.edge("vector")
                for bk in range(4):
                    S.op("vector", lambda en, b=bk: en.tensor_scalar(
                        msk_sb[:, 8 + b * 8: 8 + (b + 1) * 8], tm_sb[:, b * 8:(b + 1) * 8],
                        msk_sb[:, 4 + b:5 + b], None, ALU.is_ge))
                S.edge("vector")
                for bk in range(4):
                    S.op("vector", lambda en, b=bk: en.tensor_mul(
                        msk_sb[:, 8 + b * 8: 8 + (b + 1) * 8],
                        msk_sb[:, 8 + b * 8: 8 + (b + 1) * 8],
                        tm_sb[:, 32 + b * 8: 32 + (b + 1) * 8]),
                        inc="sv_mk" if bk == 3 else None)
                S.wait("tensor", "sv_mk", S.cnt["sv_mk"])
                for g in range(8):
                    S.wait("tensor", "ss_g", S.cnt["ss_g"])
                    for bk in range(4):
                        S.op("tensor", lambda en,
                             l=msk_sb[:, 8 + bk * 8 + g: 8 + bk * 8 + g + 1], b=bk:
                             en.matmul(p_g[0:1, b * 128:(b + 1) * 128], l, i128_sb[:],
                                       start=True, stop=True),
                             inc="sm_g2" if bk == 3 else None)
                    S.wait("scalar", "sm_g2", S.cnt["sm_g2"])
                    S.op("scalar", lambda en, o=gsc_row[0:1, g * 1024 + w * T: g * 1024 + (w + 1) * T]:
                         en.copy(o, p_g[0:1, 0:T]), inc="ss_g")
            gsc_ready = S.cnt["ss_g"]
            S.wait("tensor", "sv_tm", S.cnt["sv_tm"])
            S.wait("tensor", "ss_g", gsc_ready)
            for g in range(-1, 8):
                upready, dnready = load_expert(g)
                S.wait("tensor", "sv_wc", upready)
                for half in range(2):
                    if g >= 0:
                        if S.cnt.get("sv_z", 0) > 0:
                            S.wait("tensor", "sv_z", S.cnt["sv_z"])
                        S.wait("tensor", "sv_xnb", 2 * v + 2)
                        S.wait("tensor", "ss_g", gsc_ready)
                        S.op("tensor", lambda en,
                             r=gsc_row[0:1, g * 1024 + half * T: g * 1024 + (half + 1) * T]:
                             en.matmul(ctx["p_bcA"], onesbf[:], r, start=True, stop=True),
                             inc="sm_z")
                        S.wait("vector", "sm_z", S.cnt["sm_z"])
                        for c in range(8):
                            S.op("vector", lambda en, o=z_bf[:, c * T:(c + 1) * T],
                                 i=xn_bf[:, c * TV + half * T: c * TV + (half + 1) * T]:
                                 en.tensor_tensor(o, i, ctx["p_bcA"], ALU.mult),
                                 inc="sv_z" if c == 7 else None)
                        S.wait("tensor", "sv_z", S.cnt["sv_z"])
                    for j in range(11):
                        pu, ubk = (p_u, "u0") if j % 2 == 0 else (p_u2, "u1")
                        if bankfree.get(ubk, 0) > 0:
                            S.wait("tensor", "ss_r", bankfree[ubk])
                        if S.cnt.get("sm_d", 0) > 0 and j == 0:
                            S.wait("scalar", "sm_d", S.cnt["sm_d"])
                        for kc in range(8):
                            if g >= 0:
                                rz = z_bf[:, kc * T:(kc + 1) * T]
                            else:
                                rz = xn_bf[:, kc * TV + half * T: kc * TV + (half + 1) * T]
                            S.op("tensor", lambda en, p=pu,
                                 wslice=wup_bf[:, kc * 1408 + j * 128: kc * 1408 + j * 128 + 128],
                                 r=rz, c0=(kc == 0), c1=(kc == 7):
                                 en.matmul(p, wslice, r, start=c0, stop=c1),
                                 inc="sm_u" if kc == 7 else None)
                        S.wait("scalar", "sm_u", S.cnt["sm_u"])
                        S.op("scalar", lambda en, p=pu, o=h_bf[:, j * T:(j + 1) * T]:
                             en.activation(o, p, AF.Relu), inc="ss_r")
                        bankfree[ubk] = S.cnt["ss_r"]
                        S.edge("scalar")
                        S.op("scalar", lambda en, o=h_bf[:, j * T:(j + 1) * T]:
                             en.activation(o, o, AF.Square), inc="ss_h")
                    S.wait("tensor", "ss_h", S.cnt["ss_h"])
                    if half == 0:
                        S.wait("tensor", "sv_wc", dnready)
                    for m in range(8):
                        pd, dbk = (p_d, "d0") if m % 2 == 0 else (p_d2, "d1")
                        if bankfree.get(dbk, 0) > 0:
                            S.wait("tensor", "sv_ac", bankfree[dbk])
                        for jc in range(11):
                            S.op("tensor", lambda en, p=pd,
                                 wslice=wdn_bf[:, jc * 1024 + m * 128: jc * 1024 + m * 128 + 128],
                                 r=h_bf[:, jc * T:(jc + 1) * T],
                                 c0=(jc == 0), c1=(jc == 10):
                                 en.matmul(p, wslice, r, start=c0, stop=c1),
                                 inc="sm_d" if jc == 10 else None)
                        S.wait("vector", "sm_d", S.cnt["sm_d"])
                        mslice = macc[:, m * TV + half * T: m * TV + (half + 1) * T]
                        if g < 0:
                            S.op("vector", lambda en, p=pd, o=mslice: en.tensor_copy(o, p),
                                 inc="sv_ac")
                        else:
                            S.op("vector", lambda en, p=pd, o=mslice:
                                 en.tensor_tensor(o, o, p, ALU.add), inc="sv_ac")
                        bankfree[dbk] = S.cnt["sv_ac"]
            for w in range(2):
                g0 = v * TV + w * T
                usl = slice(g0, g0 + T)
                if S.cnt.get("st", 0) > 0:
                    S.wait("sync", "st", S.cnt["st"])
                S.wait("sync", "sv_xnb", 2 * v + 2)
                if S.cnt.get("sv_out", 0) > 0:
                    S.wait("sync", "sv_out", S.cnt["sv_out"])
                dl = S.op("sync", lambda en, s=usl: en.dma_start(
                    out=xc[:].rearrange("p (c s) -> p c s", s=T),
                    in_=x3T[:, s].rearrange("(c p) s -> p c s", p=128)), inc="dl", dma=True)
                mwhole = macc[:].rearrange("p (c s) -> p c s", s=TV)[:, :, w * T:(w + 1) * T]
                mch = [macc[:, c * TV + w * T: c * TV + w * T + T] for c in range(8)]
                if w == 0:
                    bcf2 = ("sv_z", S.cnt["sv_z"])
                else:
                    bcf2 = ("svo2", S.cnt["svo2"])
                smbc = emit_ln_stats(S, ctx, mwhole, mch, sq[:], sqc, "lnd2",
                                     bcfree=bcf2, ready=[("sv_ac", S.cnt["sv_ac"])])
                S.wait("vector", "sm_bc", smbc)
                S.wait("vector", "dl", dl)
                if w == 1:
                    S.wait("vector", "st", S.cnt["st"])
                for c in range(8):
                    S.op("vector", lambda en, a=xnfc[c], b=mch[c]:
                         en.tensor_tensor(a, b, ctx["p_bcA"], ALU.mult))
                S.edge("vector")
                for c in range(8):
                    S.op("vector", lambda en, a=xnfc[c]:
                         en.tensor_tensor(a, a, ctx["p_bcB"], ALU.add),
                         inc="svo2" if c == 7 else None)
                S.edge("vector")
                for c in range(8):
                    S.op("vector", lambda en, a=xnfc[c], b=xcc[c]:
                         en.tensor_tensor(a, a, b, ALU.add),
                         inc="sv_out" if c == 7 else None)
                S.wait("gpsimd", "sv_out", S.cnt["sv_out"])
                S.op("gpsimd", lambda en, s=usl: en.dma_start(
                    out=x4T[:, s].rearrange("(c p) s -> p c s", p=128),
                    in_=xnf[:].rearrange("p (c s) -> p c s", s=T)), inc="st", dma=True)
        S.wait("gpsimd", "st", S.cnt["st"])

        hmap = {}
        for nm in S.sem_names:
            hmap[nm] = e(nc.semaphore(nm))
        block = e(nc.Block())
        S.emit(nc, block, hmap)
    return nc


def kernel(**inp):
    x = np.ascontiguousarray(inp["L0_attn_x"], dtype=np.float32)
    cores = list(range(8))

    def cat(v):
        return np.ascontiguousarray(v)

    xT = [cat(x[b].T) for b in range(4)]

    nc = build_A1()
    maps = [{"xT": xT[c // 2],
             "bmat": cat(inp["L0_attn_B"]),
             "alhsT": cat(inp["L0_attn_A"].T)} for c in cores]
    r1 = run_bass_kernel_spmd(nc, maps, core_ids=cores).results

    nc = build_A2()
    maps = []
    for c in cores:
        b, h = c // 2, c % 2
        sl = slice(h * 4096, (h + 1) * 4096)
        maps.append({"hs_in": cat(r1[c]["hs_out"][:, sl]),
                     "xTown": cat(xT[b][:, sl]),
                     "cmat": cat(inp["L0_attn_C"]),
                     "wot": cat(inp["L0_attn_W_out"].T),
                     "bo": cat(inp["L0_attn_b_out"].reshape(8, 128).T)})
    r2 = run_bass_kernel_spmd(nc, maps, core_ids=cores).results

    nc = build_B()
    maps = []
    for c in cores:
        b, h = c // 2, c % 2
        if h == 0:
            pcol = np.zeros((128, 8), np.float32)
        else:
            pcol = cat(r2[2 * b]["x1"][:, 4095].reshape(8, 128).T)
        maps.append({"x1T": cat(r2[c]["x1"]),
                     "pcol": pcol,
                     "wrT": cat(inp["L0_ffn_W_r"].T),
                     "wkT": cat(inp["L0_ffn_W_k"].T),
                     "wvT": cat(inp["L0_ffn_W_v"].T),
                     "mur": cat(inp["L0_ffn_mu_r"].reshape(8, 128).T),
                     "muk": cat(inp["L0_ffn_mu_k"].reshape(8, 128).T)})
    r3 = run_bass_kernel_spmd(nc, maps, core_ids=cores).results

    nc = build_C1()
    maps = [{"x2T": cat(r3[c]["x2T"]),
             "wqkv": cat(inp["L1_attn_Wqkv"]),
             "gam": cat(inp["L1_attn_gamma"].reshape(8, 128).T),
             "bet": cat(inp["L1_attn_beta"].reshape(8, 128).T)} for c in cores]
    r4 = run_bass_kernel_spmd(nc, maps, core_ids=cores).results

    nc = build_C2()
    maps = []
    for c in cores:
        b, h = c // 2, c % 2
        maps.append({"x2T": cat(r3[c]["x2T"]),
                     "xnI": cat(r4[c]["xnO"]),
                     "qI": cat(r4[c]["qO"]),
                     "kva": cat(r4[c]["kvO"]),
                     "kvb": cat(r4[2 * b + (1 - h)]["kvO"]),
                     "wout": cat(inp["L1_attn_Wout"])})
    r5 = run_bass_kernel_spmd(nc, maps, core_ids=cores).results

    nc = build_D()
    common = {"gwT": cat(inp["L1_ffn_gate_w"].T),
              "ebias": cat(inp["L1_ffn_expert_bias"].reshape(8, 1)),
              "wupT": cat(inp["L1_ffn_w_up"].transpose(0, 2, 1)),
              "wdnT": cat(inp["L1_ffn_w_down"].transpose(0, 2, 1)),
              "swuT": cat(inp["L1_ffn_shared_w_up"].T),
              "swdT": cat(inp["L1_ffn_shared_w_down"].T),
              "i8d": np.eye(8, dtype=np.float32),
              "i128d": np.eye(128, dtype=np.float32)}
    maps = []
    for c in cores:
        m = dict(common)
        m["x3T"] = cat(r5[c]["x3T"])
        maps.append(m)
    r6 = run_bass_kernel_spmd(nc, maps, core_ids=cores).results

    out = np.empty((4, 8192, 1024), np.float32)
    for c in cores:
        b, h = c // 2, c % 2
        out[b, h * 4096:(h + 1) * 4096, :] = np.asarray(r6[c]["x4T"]).T
    return out
